# version 15
# baseline (speedup 1.0000x reference)
import numpy as np
import ml_dtypes

import concourse.bass as bass
import concourse.mybir as mybir
from concourse.bass_utils import run_bass_kernel_spmd

F32 = mybir.dt.float32
BF16 = mybir.dt.bfloat16
F16 = mybir.dt.float16
U8 = mybir.dt.uint8
AF = mybir.ActivationFunctionType
ALU = mybir.AluOpType
BF16NP = ml_dtypes.bfloat16

D = 2048
H = 16
E = 128
CH = 128
NCORES = 8
I_FFN = 5632
NGU = 2 * I_FFN // 128
NIT = I_FFN // 128
GU_SH = NGU // NCORES
DN_IT = 11
DN_FT = 8
DN_FW = DN_FT * 128
SH_GU = 16 * 128 * 1408
SH_DN = DN_IT * 128 * DN_FW
SH_N = SH_GU + SH_DN
ZROWS = H * 128 + 128

perm128 = np.concatenate([np.arange(0, 128, 2), np.arange(1, 128, 2)])
PERM = (np.arange(0, D, 128)[:, None] + perm128[None, :]).reshape(-1)
PERM_GU = (np.arange(0, 2 * I_FFN, 128)[:, None] + perm128[None, :]).reshape(-1)

_CACHE = {}


class Buf:
    __slots__ = ("w", "r")

    def __init__(self):
        self.w = None
        self.r = []

    def for_write(self):
        t = list(self.r)
        if self.w is not None:
            t.append(self.w)
        return t

    def for_read(self):
        return [self.w] if self.w is not None else []

    def wrote(self, tok):
        self.w = tok
        self.r = []

    def read(self, tok):
        if tok is not None:
            self.r.append(tok)


NLANE = 8


class Prog:
    STREAMS = ("te", "dve", "act", "pool", "sp")
    SEMS = {"te": 1, "dve": 1, "act": 1, "cc": 1}
    for _q in ("dmap", "dmas"):
        for _l in range(NLANE):
            SEMS[f"{_q}{_l}"] = 16
    OWN = {"te": "te", "dve": "dve", "act": "act"}

    def __init__(self):
        self.ops = {s: [] for s in self.STREAMS}
        self.cnt = {s: 0 for s in self.SEMS}
        self.nq = {"dmap": 0, "dmas": 0}
        self.waited = {s: {} for s in self.STREAMS}

    def op(self, stream, make, waits=(), sig=None):
        for t in waits:
            self.wait(stream, t)
        tok = None
        if sig is not None:
            self.cnt[sig] += self.SEMS[sig]
            tok = (sig, self.cnt[sig])

            def cl(eng, sems, _m=make, _s=sig, _a=self.SEMS[sig]):
                _m(eng).then_inc(sems[_s], _a)
        else:
            def cl(eng, sems, _m=make):
                _m(eng)
        self.ops[stream].append(cl)
        return tok

    def wait(self, stream, tok):
        if tok is None:
            return
        sem, v = tok
        if self.OWN.get(stream) == sem:
            return
        assert v <= self.cnt[sem], f"wait on future token {sem}:{v}"
        if self.waited[stream].get(sem, 0) >= v:
            return
        self.waited[stream][sem] = v

        def cl(eng, sems, _sem=sem, _v=v):
            eng.wait_ge(sems[_sem], _v)
        self.ops[stream].append(cl)

    def barrier(self):
        names = ["te", "dve", "act"] + [f"dmap{l}" for l in range(NLANE)] \
            + [f"dmas{l}" for l in range(NLANE)]
        toks = [(s, self.cnt[s]) for s in names if self.cnt[s] > 0]
        for s in self.STREAMS:
            for t in toks:
                self.wait(s, t)


DTSZ = {F32: 4, BF16: 2, U8: 1, F16: 2}


def _tile_oxf(w):
    a = w.reshape(16, 128, 16, 128)
    return np.ascontiguousarray(a.transpose(2, 1, 0, 3)).reshape(16, 128, 2048)


def _u8t(packed):
    o, p8 = packed.shape
    a = packed.view(np.uint8).reshape(o, p8, 4)
    return np.ascontiguousarray(a.transpose(1, 2, 0).reshape(p8 * 4, o))


def _dup_rows(u8t, tiles):
    n = len(tiles)
    out = np.empty((n, 128, u8t.shape[1]), np.uint8)
    for k, t in enumerate(tiles):
        blk = u8t[64 * t:64 * t + 64]
        out[k, 0:64] = blk
        out[k, 64:128] = blk
    return out


def host_prepare(inputs, tcore):
    x = np.asarray(inputs["L0_attn_x"], np.float32).reshape(-1, D)
    Wq = np.asarray(inputs["L0_attn_Wq"], np.float32)[PERM, :]
    Wk = np.asarray(inputs["L0_attn_Wk"], np.float32)[PERM, :]
    Wv = np.asarray(inputs["L0_attn_Wv"], np.float32)[PERM, :]
    Wf = np.asarray(inputs["L0_attn_Wf"], np.float32)[PERM, :]
    Wo = np.asarray(inputs["L0_attn_Wo"], np.float32)[:, PERM]
    nw = np.asarray(inputs["L0_attn_norm_w"], np.float32)
    gup = np.asarray(inputs["L0_ffn_gate_up_packed"], np.int32)[PERM_GU, :]
    gus = np.asarray(inputs["L0_ffn_gate_up_scales"], np.float16)[:, PERM_GU]
    dnp = np.asarray(inputs["L0_ffn_down_packed"], np.int32)[PERM, :]
    dns = np.asarray(inputs["L0_ffn_down_scales"], np.float16)[:, PERM]

    wq_t = _tile_oxf(Wq)
    wk_t = _tile_oxf(Wk)
    wo_t = _tile_oxf(Wo)
    wv_t = np.ascontiguousarray(
        Wv.reshape(16, 128, 4, 512).transpose(2, 0, 1, 3)).reshape(64, 128, 512)
    wf_t = np.ascontiguousarray(Wf.reshape(16, 128, 16))

    u8_gu = _u8t(gup)
    u8_dn = _u8t(dnp)

    ones_f = np.ones((128, 128), np.float32)
    ones_b = np.ones((128, 128), BF16NP)
    id_f = np.eye(128, dtype=np.float32)
    id_b = np.eye(128, dtype=np.float32).astype(BF16NP)
    mt = np.triu(np.full((128, 128), -1e9, np.float32), 1)
    sel = np.zeros((16, 2048), np.float32)
    for h in range(16):
        sel[h, 128 * h:128 * h + 128] = 1.0
    nwcol = np.ascontiguousarray(nw.reshape(16, 128).T)
    misc = np.zeros((128, 8), np.float32)
    misc[:, 0] = -4.0
    misc[:, 1] = -6.0
    misc[:, 2] = 1e-6
    misc[:, 3] = 1e-5

    in_maps = []
    for r in range(NCORES):
        toks = slice(r * tcore, (r + 1) * tcore)
        xc = np.ascontiguousarray(x[toks][:, PERM].T)
        gu_cols = slice(1408 * r, 1408 * r + 1408)
        gu_u8 = _dup_rows(u8_gu[:, gu_cols], range(16))
        gu_sc = np.ascontiguousarray(
            np.broadcast_to(gus[:, gu_cols][:, None, :], (16, 128, 1408)))
        it0 = DN_IT * (r // 2)
        ft0 = DN_FT * (r % 2)
        dn_cols = slice(128 * ft0, 128 * ft0 + DN_FW)
        dn_u8 = _dup_rows(u8_dn[:, dn_cols], range(it0, it0 + DN_IT))
        dn_sc = np.ascontiguousarray(
            np.broadcast_to(dns[it0:it0 + DN_IT, dn_cols][:, None, :],
                            (DN_IT, 128, DN_FW)))
        g0 = (r // 4) * 4
        msel = np.zeros((8, 8), np.float32)
        maskb = np.full((8,), -1e9, np.float32)
        for p in range(g0, r):
            maskb[p] = 0.0
            for q in range(p + 1, r):
                msel[p, q] = 1.0
        in_maps.append({
            "xcore": xc,
            "wq_t": wq_t, "wk_t": wk_t, "wv_t": wv_t, "wf_t": wf_t, "wo_t": wo_t,
            "gu_u8": gu_u8, "gu_sc": gu_sc, "dn_u8": dn_u8, "dn_sc": dn_sc,
            "ones_f": ones_f, "ones_b": ones_b, "id_f": id_f, "id_b": id_b,
            "mt_f": mt, "sel_f": sel, "nwcol": nwcol, "misc": misc,
            "mselr": np.ascontiguousarray(
                np.broadcast_to(msel.reshape(1, 64), (128, 64))).astype(np.float32),
            "maskb": np.ascontiguousarray(
                np.broadcast_to(maskb.reshape(1, 8), (128, 8))).astype(np.float32),
        })
    return in_maps


def build_program(tcore, dbg=False, variant=0):
    TBv = 256
    nblk = tcore // TBv
    cpb = TBv // CH
    nchunk = tcore // CH
    TH = min(1024, tcore)
    nffn = tcore // TH
    nts_all = tcore // 512
    nts_h = TH // 512

    nc = bass.Bass("TRN2", target_bir_lowering=False, detect_race_conditions=False)

    d_x = nc.dram_tensor("xcore", [D, tcore], F32, kind="ExternalInput")
    d_wq = nc.dram_tensor("wq_t", [16, 128, 2048], F32, kind="ExternalInput")
    d_wk = nc.dram_tensor("wk_t", [16, 128, 2048], F32, kind="ExternalInput")
    d_wv = nc.dram_tensor("wv_t", [64, 128, 512], F32, kind="ExternalInput")
    d_wf = nc.dram_tensor("wf_t", [16, 128, 16], F32, kind="ExternalInput")
    d_wo = nc.dram_tensor("wo_t", [16, 128, 2048], F32, kind="ExternalInput")
    d_guu = nc.dram_tensor("gu_u8", [16, 128, 1408], U8, kind="ExternalInput")
    d_gus = nc.dram_tensor("gu_sc", [16, 128, 1408], F16, kind="ExternalInput")
    d_dnu = nc.dram_tensor("dn_u8", [DN_IT, 128, DN_FW], U8, kind="ExternalInput")
    d_dns = nc.dram_tensor("dn_sc", [DN_IT, 128, DN_FW], F16, kind="ExternalInput")
    d_onef = nc.dram_tensor("ones_f", [128, 128], F32, kind="ExternalInput")
    d_oneb = nc.dram_tensor("ones_b", [128, 128], BF16, kind="ExternalInput")
    d_idf = nc.dram_tensor("id_f", [128, 128], F32, kind="ExternalInput")
    d_idb = nc.dram_tensor("id_b", [128, 128], BF16, kind="ExternalInput")
    d_mt = nc.dram_tensor("mt_f", [128, 128], F32, kind="ExternalInput")
    d_sel = nc.dram_tensor("sel_f", [16, 2048], F32, kind="ExternalInput")
    d_nw = nc.dram_tensor("nwcol", [128, 16], F32, kind="ExternalInput")
    d_misc = nc.dram_tensor("misc", [128, 8], F32, kind="ExternalInput")
    d_msel = nc.dram_tensor("mselr", [128, 64], F32, kind="ExternalInput")
    d_mask = nc.dram_tensor("maskb", [128, 8], F32, kind="ExternalInput")
    d_out = nc.dram_tensor("xout", [D, tcore], F32, kind="ExternalOutput")
    if dbg:
        d_dbg_wdq = nc.dram_tensor("dbg_wdq", [SH_N // 128, 128], BF16,
                                   kind="ExternalOutput")
        d_dbg_cum = nc.dram_tensor("dbg_cum", [16, 256], F32, kind="ExternalOutput")
        d_dbg_qd = nc.dram_tensor("dbg_qd", [2 * 128, tcore], BF16, kind="ExternalOutput")
        d_dbg_ofm = nc.dram_tensor("dbg_ofm", [2 * 128, tcore], BF16, kind="ExternalOutput")
        d_dbg_x1 = nc.dram_tensor("dbg_x1", [D, tcore], BF16, kind="ExternalOutput")
        d_dbg_sin = nc.dram_tensor("dbg_sin", [2 * 128, 128], BF16, kind="ExternalOutput")
        d_dbg_of1 = nc.dram_tensor("dbg_of1", [2 * 128, tcore], BF16, kind="ExternalOutput")
        d_dbg_of2 = nc.dram_tensor("dbg_of2", [2 * 128, tcore], BF16, kind="ExternalOutput")
        d_dbg_qp = nc.dram_tensor("dbg_qp", [128, 256], BF16, kind="ExternalOutput")
        d_dbg_kp = nc.dram_tensor("dbg_kp", [128, 256], BF16, kind="ExternalOutput")
        d_dbg_s = nc.dram_tensor("dbg_s", [2 * 128, 128], F32, kind="ExternalOutput")
        d_dbg_pt = nc.dram_tensor("dbg_pt", [2 * 128, 128], BF16, kind="ExternalOutput")
    d_wdq_in = nc.dram_tensor("wdq_in", [SH_N // 128, 128], BF16)
    d_wdq_all = nc.dram_tensor("wdq_all", [NCORES * (SH_N // 128), 128], BF16, addr_space="Shared")
    d_z_in = nc.dram_tensor("z_in", [ZROWS, 128], F32)
    d_z_all = nc.dram_tensor("z_all", [NCORES * ZROWS, 128], F32, addr_space="Shared")

    P = Prog()
    view_ctx = {}
    sb_handles = {}
    _off = [0]
    SBUF_BYTES = 212600

    def sb(name, parts, cols, dt):
        a = max(4, DTSZ[dt])
        o = (_off[0] + a - 1) // a * a
        assert o + cols * DTSZ[dt] <= SBUF_BYTES, f"arena overflow at {name}"
        _off[0] = o + cols * DTSZ[dt]
        sb_handles[name] = (o, parts, cols, dt)

    def alias(name, byte_off, parts, cols, dt):
        assert byte_off % DTSZ[dt] == 0
        assert byte_off + cols * DTSZ[dt] <= SBUF_BYTES
        sb_handles[name] = (byte_off, parts, cols, dt)

    def V(name):
        off, parts, cols, dt = sb_handles[name]
        t = view_ctx["arena"]
        o = off // DTSZ[dt]
        return t.ap().bitcast(dt)[0:parts, o:o + cols]

    PW = 384
    for name, parts, cols, dt in [
        ("ones_f", 128, 128, F32), ("ones_b", 128, 128, BF16),
        ("id_f", 16, 16, F32),
        ("nwcol", 128, 16, F32), ("misc", 128, 8, F32),
        ("row0", 1, 512, F32), ("row1", 1, 512, F32), ("row2", 1, 512, F32),
    ]:
        sb(name, parts, cols, dt)
    for h in range(H):
        sb(f"qd{h}", 128, tcore, BF16)
    ofm_first = _off[0]
    for h in range(H):
        sb(f"ofm{h}", 128, tcore, BF16)
    for h in range(H):
        sb(f"kT{h}", 128, TBv, BF16)
    for tt in range(cpb):
        sb(f"vtm{tt}", 128, D, BF16)
    for f in range(16):
        sb(f"xbf{f}", 128, TBv, BF16)
    for par in range(3):
        sb(f"wqk{par}", 128, 1024, BF16)
    for par in range(2):
        sb(f"wv{par}", 128, 512, BF16)
    for name, parts, cols, dt in [
        ("id_b", 128, 128, BF16), ("mt_b", 128, 128, BF16),
        ("sel_b", 16, 2048, BF16), ("mselr", 128, 64, F32),
        ("maskb", 128, 8, F32),
        ("cumhi", 16, TBv, BF16), ("culo", 16, TBv, BF16),
        ("expcum_b", 16, TBv, BF16),
    ]:
        sb(name, parts, cols, dt)
    for f in range(16):
        sb(f"wf{f}", 128, 16, BF16)
    for h in range(H):
        sb(f"S{h}", 128, 128, F32)
        sb(f"Sbf{h}", 128, 128, BF16)
    for name, parts, cols, dt in [
        ("pcol", 128, (nchunk + 1) * 16, F32), ("ltot", 128, 16, F32),
        ("cum16", 16, TBv, F32), ("e1", 16, TBv, F32),
    ]:
        sb(name, parts, cols, dt)
    for par in range(2):
        sb(f"negcumT{par}", 128, 16, F32)
        sb(f"emcT{par}", 128, 16, F32)
        sb(f"exptot{par}", 128, 16, F32)
        sb(f"totrow{par}", 1, 16, F32)
        for ln in range(2):
            sb(f"PT{par}{ln}", 128, 128, BF16)
            sb(f"expD{par}{ln}", 128, 128, BF16)
    attn_end = _off[0]
    prep0 = _off[0]
    for par in range(2):
        sb(f"pu8_{par}", 128, PW, U8)
        sb(f"psc_{par}", 128, PW, F32)
        sb(f"pwo_{par}", 128, PW, BF16)
    sb("pnib", 128, PW, U8)
    sb("pnf", 128, PW, F32)
    sb("pge", 128, PW, F32)
    sb("pm", 128, PW, F32)
    sb("pr4", 128, PW, F32)
    prep_end = _off[0]

    qd0 = sb_handles["qd0"][0]
    xbf0 = sb_handles["xbf0"][0]
    for par in range(2):
        for ln in range(2):
            alias(f"qd3{par}{ln}", sb_handles[f"PT{par}{ln}"][0], 128, 128, BF16)
        alias(f"qd3{par}", sb_handles[f"PT{par}0"][0], 128, 128, BF16)
    po = prep0

    def pov(name, parts, cols, dt):
        nonlocal po
        a = max(4, DTSZ[dt])
        po = (po + a - 1) // a * a
        alias(name, po, parts, cols, dt)
        po += cols * DTSZ[dt]
    for h in range(H):
        pov(f"Sin{h}", 128, 128, BF16)
    pov("acc16", 128, 16, F32)
    pov("wcol", 128, 16, F32)
    pov("zst0", 128, 128, BF16)
    pov("tstall", 128, 128, F32)
    pov("tst0", 128, 16, F32)
    pov("tst1", 128, 16, F32)
    _off[0] = max(_off[0], po)
    for f in range(16):
        alias(f"x1_{f}", qd0 + f * tcore * 2, 128, tcore, BF16)
    fo = ofm_first

    def fal(name, parts, cols, dt):
        nonlocal fo
        a = max(4, DTSZ[dt])
        fo = (fo + a - 1) // a * a
        alias(name, fo, parts, cols, dt)
        fo += cols * DTSZ[dt]
    for c in range(NIT):
        fal(f"hh{c}", 128, TH, BF16)
    for f in range(16):
        fal(f"ysb{f}", 128, 512, BF16)
    for par in range(2):
        fal(f"ostg{par}", 128, 512, F32)
    for k in range(2):
        fal(f"gwb{k}", 128, 2048, BF16)
    for k in range(2):
        fal(f"dnb{k}", 128, 11 * 128, BF16)
    fal("gtmp", 128, 512, BF16)
    fal("utmp", 128, 512, BF16)
    assert fo <= prep_end, (fo, prep_end)
    alias("zrank", xbf0, 128, 2048, F32)
    alias("xall", xbf0, 128, 16 * TBv, BF16)
    assert 2 * tcore * 2 <= 16 * TBv * 2
    for par in range(2):
        alias(f"x2_{par}", xbf0 + par * tcore * 2, 128, tcore, BF16)
    assert _off[0] <= SBUF_BYTES, f"total {_off[0]} > {SBUF_BYTES}"
    build_program.last_total = _off[0]

    bufs = {}

    def BUF(name):
        if name not in bufs:
            bufs[name] = Buf()
        return bufs[name]

    pbb = [Buf() for _ in range(8)]

    def pb(i, c0, c1, p0=0, p1=128):
        return view_ctx["pb"][i].ap()[p0:p1, c0:c1]

    def pb16(i, c0, ncols, p0=0, p1=128):
        return view_ctx["pb"][i].ap().bitcast(BF16)[p0:p1, 2 * c0:2 * c0 + ncols]

    def dma(queue, dst_fn, src_fn, wbufs=(), rbufs=(), extra=()):
        qn = "dmap" if queue == "pool" else "dmas"
        lane = P.nq[qn] % NLANE
        P.nq[qn] += 1
        semn = f"{qn}{lane}"
        waits = list(extra)
        for b in wbufs:
            waits += b.for_write()
        for b in rbufs:
            waits += b.for_read()
        if variant == 10 and P.cnt[semn] > 0:
            waits.append((semn, P.cnt[semn]))
        tok = P.op(queue, lambda e: e.dma_start(out=dst_fn(), in_=src_fn()),
                   waits=waits, sig=semn)
        for b in wbufs:
            b.wrote(tok)
        for b in rbufs:
            b.read(tok)
        return tok

    def eng(stream, make, wbufs=(), rbufs=(), extra=()):
        waits = list(extra)
        for b in wbufs:
            waits += b.for_write()
        for b in rbufs:
            waits += b.for_read()
        if variant == 9:
            for s2 in ("te", "dve", "act"):
                if P.cnt[s2] > 0:
                    waits.append((s2, P.cnt[s2]))
        tok = P.op(stream, make, waits=waits,
                   sig={"te": "te", "dve": "dve", "act": "act"}[stream])
        for b in wbufs:
            b.wrote(tok)
        for b in rbufs:
            b.read(tok)
        return tok

    def emit_setup():
        for name, dt_ in [("ones_f", d_onef), ("ones_b", d_oneb),
                          ("id_b", d_idb), ("mt_b", d_mt), ("sel_b", d_sel),
                          ("nwcol", d_nw), ("misc", d_misc), ("mselr", d_msel),
                          ("maskb", d_mask)]:
            dma("pool", lambda n=name: V(n), lambda t=dt_: t.ap(), wbufs=[BUF(name)])
        dma("pool", lambda: V("id_f"), lambda: d_idf.ap()[0:16, 0:16],
            wbufs=[BUF("id_f")])
        for f in range(16):
            dma("pool", lambda f=f: V(f"wf{f}"), lambda f=f: d_wf.ap()[f],
                wbufs=[BUF(f"wf{f}")])
        for h in range(H):
            eng("dve", lambda e, h=h: e.memset(V(f"S{h}"), 0.0), wbufs=[BUF(f"S{h}")])
            eng("dve", lambda e, h=h: e.memset(V(f"Sbf{h}"), 0.0), wbufs=[BUF(f"Sbf{h}")])
        eng("dve", lambda e: e.memset(V("pcol")[:, 0:16], 1.0), wbufs=[BUF("pcol")])
        eng("dve", lambda e: e.memset(V("ltot"), 0.0), wbufs=[BUF("ltot")])

    def prep_steps():
        jobs = []
        for q in range(16):
            for (c0, W) in ((0, 384), (384, 384), (768, 384), (1152, 256)):
                jobs.append(("gu", q, c0, W))
        for it in range(DN_IT):
            for (c0, W) in ((0, 384), (384, 384), (768, 256)):
                jobs.append(("dn", it, c0, W))
        par = 0
        for (kind, idx, c0, W) in jobs:
            u8b, scb, wob = BUF(f"pu8_{par}"), BUF(f"psc_{par}"), BUF(f"pwo_{par}")
            src_u = d_guu if kind == "gu" else d_dnu
            src_s = d_gus if kind == "gu" else d_dns
            dma("sp", lambda p=par, W=W: V(f"pu8_{p}")[:, 0:W],
                lambda s=src_u, i=idx, c0=c0, W=W: s.ap()[i, :, c0:c0 + W],
                wbufs=[u8b])
            dma("pool", lambda p=par, W=W: V(f"psc_{p}")[:, 0:W],
                lambda s=src_s, i=idx, c0=c0, W=W: s.ap()[i, :, c0:c0 + W],
                wbufs=[scb])
            yield
            nib, nf, ge, m, r4 = (BUF("pnib"), BUF("pnf"), BUF("pge"),
                                  BUF("pm"), BUF("pr4"))
            eng("dve", lambda e, W=W, p=par: e.tensor_single_scalar(
                V("pnib")[0:64, 0:W], V(f"pu8_{p}")[0:64, 0:W], 15,
                ALU.bitwise_and), wbufs=[nib], rbufs=[u8b])
            eng("dve", lambda e, W=W, p=par: e.tensor_single_scalar(
                V("pnib")[64:128, 0:W], V(f"pu8_{p}")[64:128, 0:W], 4,
                ALU.logical_shift_right), wbufs=[nib], rbufs=[u8b])
            eng("dve", lambda e, W=W: e.tensor_copy(
                V("pnf")[:, 0:W], V("pnib")[:, 0:W]),
                wbufs=[nf], rbufs=[nib])
            eng("dve", lambda e, W=W: e.tensor_single_scalar(
                V("pge")[:, 0:W], V("pnf")[:, 0:W], 8.0, ALU.is_ge),
                wbufs=[ge], rbufs=[nf])
            eng("dve", lambda e, W=W: e.scalar_tensor_tensor(
                V("pm")[:, 0:W], V("pge")[:, 0:W], -8.0, V("pnf")[:, 0:W],
                ALU.mult, ALU.add), wbufs=[m], rbufs=[ge, nf])
            eng("dve", lambda e, W=W: e.tensor_scalar(
                V("pge")[:, 0:W], V("pge")[:, 0:W], -2.0, 1.0, ALU.mult, ALU.add),
                wbufs=[ge])
            yield
            eng("act", lambda e, W=W: e.activation(
                V("pr4")[:, 0:W], V("pm")[:, 0:W], AF.Relu, bias=V("misc")[:, 0:1]),
                wbufs=[r4], rbufs=[m, BUF("misc")])
            eng("act", lambda e, W=W: e.activation(
                V("pnf")[:, 0:W], V("pm")[:, 0:W], AF.Relu, bias=V("misc")[:, 1:2]),
                wbufs=[nf], rbufs=[m, BUF("misc")])
            eng("dve", lambda e, W=W: e.tensor_add(
                V("pr4")[:, 0:W], V("pm")[:, 0:W], V("pr4")[:, 0:W]),
                wbufs=[r4], rbufs=[m])
            eng("dve", lambda e, W=W: e.scalar_tensor_tensor(
                V("pr4")[:, 0:W], V("pr4")[:, 0:W], 0.5, V("pnf")[:, 0:W],
                ALU.mult, ALU.add), wbufs=[r4], rbufs=[nf])
            eng("dve", lambda e, W=W, p=par: e.tensor_mul(
                V("pge")[:, 0:W], V("pge")[:, 0:W], V(f"psc_{p}")[:, 0:W]),
                wbufs=[ge], rbufs=[scb])
            eng("dve", lambda e, W=W, p=par: e.tensor_mul(
                V(f"pwo_{p}")[:, 0:W], V("pr4")[:, 0:W], V("pge")[:, 0:W]),
                wbufs=[wob], rbufs=[r4, ge])
            PLANE = 264 * 128
            nt = W // 128
            if kind == "gu":
                eoff = ((c0 // 128) * 16 + idx) * 128
                tstride = 16 * 128
            else:
                eoff = (176 + (c0 // 128) * 11 + idx) * 128
                tstride = 11 * 128
            dma("sp", lambda eo=eoff, tstr=tstride, nt=nt: bass.AP(
                    d_wdq_in, eo, [[PLANE, 128], [tstr, nt], [1, 128]]),
                lambda p=par, W=W: V(f"pwo_{p}")[:, 0:W],
                rbufs=[wob])
            par ^= 1
            yield

    def chunk_prep(cg):
        par = cg % 2
        lc = cg % cpb
        cb = lc * CH
        last = cb + CH - 1
        cumb = BUF("cum16")
        bank = 2 + 4 * par
        eng("te", lambda e: e.transpose(
            pb(bank, 256, 272), V("cum16")[:, cb:cb + CH], V("id_f")),
            wbufs=[pbb[bank]], rbufs=[cumb, BUF("id_f")])
        eng("act", lambda e: e.activation(
            V(f"negcumT{par}"), pb(bank, 256, 272), AF.Copy, scale=-1.0),
            wbufs=[BUF(f"negcumT{par}")], rbufs=[pbb[bank]])
        eng("act", lambda e: e.activation(
            V("e1")[:, 0:CH], V("cum16")[:, cb:cb + CH], AF.Exp, scale=-1.0,
            bias=V("cum16")[:, last:last + 1]),
            wbufs=[BUF("e1")], rbufs=[cumb])
        eng("te", lambda e: e.transpose(
            pb(bank, 272, 288), V("e1")[:, 0:CH], V("id_f")),
            wbufs=[pbb[bank]], rbufs=[BUF("e1"), BUF("id_f")])
        eng("act", lambda e: e.activation(
            V(f"emcT{par}"), pb(bank, 272, 288), AF.Copy),
            wbufs=[BUF(f"emcT{par}")], rbufs=[pbb[bank]])
        eng("te", lambda e: e.transpose(
            pb(bank, 304, 320, 0, 1), V("cum16")[:, last:last + 1], V("id_f")),
            wbufs=[pbb[bank]], rbufs=[cumb, BUF("id_f")])
        eng("act", lambda e: e.activation(
            V(f"totrow{par}"), pb(bank, 304, 320, 0, 1), AF.Copy),
            wbufs=[BUF(f"totrow{par}")], rbufs=[pbb[bank]])
        eng("te", lambda e: e.matmul(
            pb(bank, 288, 304), V("ones_f")[0:1, 0:128], V(f"totrow{par}"),
            start=True, stop=True),
            wbufs=[pbb[bank]], rbufs=[BUF(f"totrow{par}"), BUF("ones_f")])
        eng("dve", lambda e: e.tensor_add(V("ltot"), V("ltot"), pb(bank, 288, 304)),
            wbufs=[BUF("ltot")], rbufs=[pbb[bank]])
        eng("act", lambda e: e.activation(
            V(f"exptot{par}"), pb(bank, 288, 304), AF.Exp),
            wbufs=[BUF(f"exptot{par}")], rbufs=[pbb[bank]])
        eng("dve", lambda e, c=cg: e.tensor_mul(
            V("pcol")[:, (c + 1) * 16:(c + 2) * 16],
            V("pcol")[:, c * 16:(c + 1) * 16], V(f"exptot{par}")),
            wbufs=[BUF("pcol")], rbufs=[BUF(f"exptot{par}")])

    def inner2(cg, g):
        lc = cg % cpb
        cb = lc * CH
        par = g % 2
        cpar = cg % 2
        X = 4 * par
        hs = (2 * g, 2 * g + 1)
        g0 = cg * CH
        vb = BUF(f"vtm{lc}")
        def sc_q(i):
            return pb(X, 128 * i, 128 * i + 128)

        def kt_q(i):
            return pb16(X, 256 + 64 * i, 128)

        def d_q(i):
            return pb(X + 1, 128 * i, 128 * i + 128)

        def s_q(i):
            return pb(X + 1, 256 + 128 * i, 384 + 128 * i)

        def b_q(i):
            return pb(X + 2, 128 * i, 128 * i + 128)

        def o_q(i):
            return pb(X + 3, 128 * i, 128 * i + 128)
        for i, h in enumerate(hs):
            eng("te", lambda e, i=i, h=h: e.matmul(
                sc_q(i), V(f"kT{h}")[:, cb:cb + CH], V(f"qd{h}")[:, g0:g0 + CH],
                start=True, stop=True),
                wbufs=[pbb[X]], rbufs=[BUF(f"kT{h}"), BUF(f"qd{h}")])
        for i, h in enumerate(hs):
            eng("te", lambda e, i=i, h=h: e.matmul(
                d_q(i), V("sel_b")[:, 128 * h:128 * h + 128],
                V("cumhi")[:, cb:cb + CH], start=True, stop=False),
                wbufs=[pbb[X + 1]], rbufs=[BUF("sel_b"), BUF("cumhi")])
            eng("te", lambda e, i=i, h=h: e.matmul(
                d_q(i), V("sel_b")[:, 128 * h:128 * h + 128],
                V("culo")[:, cb:cb + CH], start=False, stop=False),
                wbufs=[pbb[X + 1]], rbufs=[BUF("sel_b"), BUF("culo")])
            eng("te", lambda e, i=i: e.matmul(
                d_q(i), V("mt_b"), V("id_b"), start=False, stop=True),
                wbufs=[pbb[X + 1]], rbufs=[BUF("mt_b"), BUF("id_b")])
        for i, h in enumerate(hs):
            eng("te", lambda e, i=i, h=h: e.matmul(
                b_q(i), V("sel_b")[:, 128 * h:128 * h + 128],
                V("expcum_b")[:, cb:cb + CH], start=True, stop=True),
                wbufs=[pbb[X + 2]], rbufs=[BUF("sel_b"), BUF("expcum_b")])
        for i, h in enumerate(hs):
            eng("act", lambda e, i=i, h=h: e.activation(
                V(f"expD{par}{i}"), d_q(i), AF.Exp,
                bias=V(f"negcumT{cpar}")[:, h:h + 1]),
                wbufs=[BUF(f"expD{par}{i}")],
                rbufs=[pbb[X + 1], BUF(f"negcumT{cpar}")])
        for i, h in enumerate(hs):
            eng("dve", lambda e, i=i: e.tensor_mul(
                V(f"PT{par}{i}"), V(f"expD{par}{i}"), sc_q(i)),
                wbufs=[BUF(f"PT{par}{i}")],
                rbufs=[BUF(f"expD{par}{i}"), pbb[X]])
            eng("dve", lambda e, i=i, h=h: e.tensor_mul(
                V(f"qd{h}")[:, g0:g0 + CH], V(f"qd{h}")[:, g0:g0 + CH], b_q(i)),
                wbufs=[BUF(f"qd{h}")], rbufs=[pbb[X + 2]])
        for i, h in enumerate(hs):
            eng("te", lambda e, i=i, h=h: e.matmul(
                o_q(i), V(f"vtm{lc}")[:, 128 * h:128 * h + 128], V(f"PT{par}{i}"),
                start=True, stop=False),
                wbufs=[pbb[X + 3]], rbufs=[vb, BUF(f"PT{par}{i}")])
            eng("te", lambda e, i=i, h=h: e.matmul(
                o_q(i), V(f"Sbf{h}"), V(f"qd{h}")[:, g0:g0 + CH],
                start=False, stop=True),
                wbufs=[pbb[X + 3]], rbufs=[BUF(f"Sbf{h}"), BUF(f"qd{h}")])
        for i, h in enumerate(hs):
            eng("te", lambda e, i=i, h=h: e.transpose(
                kt_q(i), V(f"kT{h}")[:, cb:cb + CH], V("id_b")),
                wbufs=[pbb[X]], rbufs=[BUF(f"kT{h}"), BUF("id_b")])
        for i, h in enumerate(hs):
            eng("act", lambda e, i=i, h=h: e.activation(
                V(f"qd3{par}{i}"), kt_q(i), AF.Copy,
                scale=V(f"emcT{cpar}")[:, h:h + 1]),
                wbufs=[BUF(f"qd3{par}{i}")],
                rbufs=[pbb[X], BUF(f"emcT{cpar}")])
        for i, h in enumerate(hs):
            eng("te", lambda e, i=i, h=h: e.matmul(
                s_q(i), V(f"qd3{par}{i}"), V(f"vtm{lc}")[:, 128 * h:128 * h + 128],
                start=True, stop=True),
                wbufs=[pbb[X + 1]], rbufs=[BUF(f"qd3{par}{i}"), vb])
        for i, h in enumerate(hs):
            eng("act", lambda e, i=i, h=h: e.activation(
                V(f"ofm{h}")[:, g0:g0 + CH], o_q(i), AF.Copy),
                wbufs=[BUF(f"ofm{h}_s{cg // 4}")], rbufs=[pbb[X + 3]])
        for i, h in enumerate(hs):
            eng("dve", lambda e, i=i, h=h: e.scalar_tensor_tensor(
                V(f"S{h}"), V(f"S{h}"), V(f"exptot{cpar}")[:, h:h + 1], s_q(i),
                ALU.mult, ALU.add),
                wbufs=[BUF(f"S{h}")], rbufs=[pbb[X + 1], BUF(f"exptot{cpar}")])
            eng("dve", lambda e, h=h: e.tensor_copy(V(f"Sbf{h}"), V(f"S{h}")),
                wbufs=[BUF(f"Sbf{h}")], rbufs=[BUF(f"S{h}")])

    def pass1_block(b, prep_pump):
        t0 = b * TBv
        xb = [BUF(f"xbf{f}") for f in range(16)]
        dma("pool", lambda: V("xall"),
            lambda t0=t0: bass.AP(d_x, t0, [[tcore, 128], [128 * tcore, 16],
                                            [1, TBv]]),
            wbufs=xb)
        for wi, (wd, kind) in enumerate(((d_wq, "q"), (d_wk, "k"))):
            for o in range(16):
                gidx = wi * 16 + o
                b0 = (2 * gidx) % 3
                b1 = (2 * gidx + 1) % 3
                wb0, wb1 = BUF(f"wqk{b0}"), BUF(f"wqk{b1}")
                dma("pool", lambda p=b0: V(f"wqk{p}"),
                    lambda w=wd, o=o: w.ap()[o, :, 0:1024], wbufs=[wb0])
                dma("pool", lambda p=b1: V(f"wqk{p}"),
                    lambda w=wd, o=o: w.ap()[o, :, 1024:2048], wbufs=[wb1])
                bank = 0 if o % 2 == 0 else 4
                for f in range(16):
                    p_ = b0 if f < 8 else b1
                    wb_ = wb0 if f < 8 else wb1
                    eng("te", lambda e, bk=bank, p=p_, f=f: e.matmul(
                        pb(bk, 0, TBv), V(f"wqk{p}")[:, 128 * (f % 8):128 * (f % 8) + 128],
                        V(f"xbf{f}"), start=(f == 0), stop=(f == 15)),
                        wbufs=[pbb[bank]], rbufs=[wb_, xb[f]])
                af_pr = AF.Copy if variant == 6 else AF.Silu
                if kind == "q":
                    eng("act", lambda e, bk=bank, o=o, t0=t0, af=af_pr: e.activation(
                        V(f"qd{o}")[:, t0:t0 + TBv], pb(bk, 0, TBv), af),
                        wbufs=[BUF(f"qd{o}")], rbufs=[pbb[bank]])
                else:
                    eng("act", lambda e, bk=bank, o=o, af=af_pr: e.activation(
                        V(f"kT{o}"), pb(bk, 0, TBv), af),
                        wbufs=[BUF(f"kT{o}")], rbufs=[pbb[bank]])
                prep_pump()
        for oq in range(4):
            base = 0 if oq % 2 == 0 else 4
            for fh in range(8):
                vpar = (oq * 8 + fh) % 3
                wvb = BUF(f"wqk{vpar}")
                eidx = oq * 16 + 2 * fh
                dma("pool", lambda p=vpar: V(f"wqk{p}"),
                    lambda k=eidx: bass.AP(d_wv, k * 128 * 512,
                                           [[512, 128], [128 * 512, 2], [1, 512]]),
                    wbufs=[wvb])
                for fi in range(2):
                    f = 2 * fh + fi
                    for tt in range(cpb):
                        bank = base + tt
                        eng("te", lambda e, bk=bank, p=vpar, f=f, fi=fi, tt=tt: e.matmul(
                            pb(bk, 0, 512), V(f"xbf{f}")[:, 128 * tt:128 * tt + 128],
                            V(f"wqk{p}")[:, 512 * fi:512 * fi + 512],
                            start=(f == 0), stop=(f == 15)),
                            wbufs=[pbb[bank]], rbufs=[wvb, xb[f]])
            for tt in range(cpb):
                bank = base + tt
                eng("dve", lambda e, bk=bank, tt=tt, oq=oq: e.tensor_copy(
                    V(f"vtm{tt}")[:, 512 * oq:512 * oq + 512], pb(bk, 0, 512)),
                    wbufs=[BUF(f"vtm{tt}")], rbufs=[pbb[bank]])
            prep_pump()
        for f in range(16):
            eng("te", lambda e, f=f: e.matmul(
                pb(1, 0, TBv, 0, 16), V(f"wf{f}"), V(f"xbf{f}"),
                start=(f == 0), stop=(f == 15)),
                wbufs=[pbb[1]], rbufs=[BUF(f"wf{f}"), xb[f]])
        eng("act", lambda e: e.activation(
            V("e1"), pb(1, 0, TBv, 0, 16), AF.Exp, scale=-1.0),
            wbufs=[BUF("e1")], rbufs=[pbb[1]])
        eng("dve", lambda e: e.tensor_single_scalar(V("e1"), V("e1"), 1.0, ALU.add),
            wbufs=[BUF("e1")])
        eng("act", lambda e: e.activation(V("e1"), V("e1"), AF.Ln),
            wbufs=[BUF("e1")])
        for lc in range(cpb):
            eng("dve", lambda e, lc=lc: e.tensor_tensor_scan(
                V("cum16")[:, lc * CH:lc * CH + CH], V("ones_f")[0:16, 0:CH],
                V("e1")[:, lc * CH:lc * CH + CH], 0.0, ALU.mult, ALU.subtract),
                wbufs=[BUF("cum16")], rbufs=[BUF("e1"), BUF("ones_f")])
        eng("dve", lambda e: e.tensor_copy(V("cumhi"), V("cum16")),
            wbufs=[BUF("cumhi")], rbufs=[BUF("cum16")])
        eng("dve", lambda e: e.tensor_sub(V("culo"), V("cum16"), V("cumhi")),
            wbufs=[BUF("culo")], rbufs=[BUF("cum16"), BUF("cumhi")])
        eng("act", lambda e: e.activation(V("expcum_b"), V("cum16"), AF.Exp),
            wbufs=[BUF("expcum_b")], rbufs=[BUF("cum16")])
        prep_pump()
        if dbg and b == nblk - 1:
            dma("sp", lambda: d_dbg_qp.ap(),
                lambda t0=t0: V("qd0")[:, t0:t0 + TBv], rbufs=[BUF("qd0")])
            dma("sp", lambda: d_dbg_kp.ap(), lambda: V("kT0"), rbufs=[BUF("kT0")])
        for lc in range(cpb):
            cg = b * cpb + lc
            chunk_prep(cg)
            for g in range(H // 2):
                inner2(cg, g)
            prep_pump()

    def emit_wdq_ag(prep_toks):
        return P.op("pool", lambda e: e.collective_compute(
            "AllGather", ALU.bypass, replica_groups=[list(range(NCORES))],
            ins=[d_wdq_in.ap().opt()], outs=[d_wdq_all.ap().opt()]),
            waits=prep_toks, sig="cc")

    def emit_z_ag():
        toks = []
        for h in range(H):
            toks.append(dma("sp", lambda h=h: d_z_in.ap()[128 * h:128 * h + 128, :],
                            lambda h=h: V(f"S{h}"), rbufs=[BUF(f"S{h}")]))
        toks.append(dma("sp", lambda: d_z_in.ap()[H * 128:H * 128 + 128, 0:16],
                        lambda: V("ltot"), rbufs=[BUF("ltot")]))
        toks.append(dma("sp", lambda: d_z_in.ap()[H * 128:H * 128 + 128, 16:128],
                        lambda: V("ones_f")[:, 0:112], rbufs=[BUF("ones_f")]))
        return P.op("pool", lambda e: e.collective_compute(
            "AllGather", ALU.bypass, replica_groups=[list(range(NCORES))],
            ins=[d_z_in.ap().opt()], outs=[d_z_all.ap().opt()]),
            waits=toks, sig="cc")

    def emit_combine(zag_tok):
        P.wait("sp", zag_tok)
        P.wait("pool", zag_tok)
        zrb = BUF("zrank")
        for f in range(16):
            zrb.r += BUF(f"xbf{f}").for_write()
        tb = BUF("tstall")
        for r in range(NCORES):
            dma("sp", lambda r=r: V("tstall")[:, 16 * r:16 * r + 16],
                lambda r=r: d_z_all.ap()[
                    r * ZROWS + H * 128:r * ZROWS + H * 128 + 128, 0:16],
                wbufs=[tb])
        for p in range(NCORES):
            accb = BUF("acc16")
            eng("dve", lambda e, p=p: e.tensor_scalar(
                V("acc16"), V("ones_f")[:, 0:16], V("maskb")[:, p:p + 1], None,
                ALU.mult), wbufs=[accb], rbufs=[BUF("maskb"), BUF("ones_f")])
            for r in range(NCORES):
                eng("dve", lambda e, p=p, r=r: e.scalar_tensor_tensor(
                    V("acc16"), V("tstall")[:, 16 * r:16 * r + 16],
                    V("mselr")[:, 8 * p + r:8 * p + r + 1],
                    V("acc16"), ALU.mult, ALU.add),
                    wbufs=[accb], rbufs=[tb, BUF("mselr")])
            eng("act", lambda e, p=p: e.activation(
                V("wcol"), V("acc16"), AF.Exp),
                wbufs=[BUF("wcol")], rbufs=[accb])
            dma("sp", lambda: V("zrank"),
                lambda p=p: bass.AP(d_z_all, (p * ZROWS) * 128,
                                    [[128, 128], [128 * 128, 16], [1, 128]]),
                wbufs=[zrb])
            for h in range(H):
                if p == 0:
                    eng("dve", lambda e, h=h: e.tensor_scalar(
                        V(f"S{h}"), V("zrank")[:, 128 * h:128 * h + 128],
                        V("wcol")[:, h:h + 1], None, ALU.mult),
                        wbufs=[BUF(f"S{h}")], rbufs=[zrb, BUF("wcol")])
                else:
                    eng("dve", lambda e, h=h: e.scalar_tensor_tensor(
                        V(f"S{h}"), V("zrank")[:, 128 * h:128 * h + 128],
                        V("wcol")[:, h:h + 1], V(f"S{h}"),
                        ALU.mult, ALU.add),
                        wbufs=[BUF(f"S{h}")], rbufs=[zrb, BUF("wcol")])
        for h in range(H):
            eng("dve", lambda e, h=h: e.tensor_copy(V(f"Sin{h}"), V(f"S{h}")),
                wbufs=[BUF(f"Sin{h}")], rbufs=[BUF(f"S{h}")])

    def emit_pass2(dbg_tap=None):
        for f in range(16):
            bufs[f"x1_{f}"] = BUF(f"qd{f}")
        for par in range(2):
            bb = BUF(f"x2_{par}")
            bb.r += BUF("zrank").for_write()
        for nm, src in (("gtmp", "wv0"), ("utmp", "wv1")):
            BUF(nm).r += BUF(src).for_write()
        def corr_chunk(cg):
            for h in range(H):
                par = (cg * H + h) % 2
                bank = 3 + 4 * par
                g0 = cg * CH
                eng("dve", lambda e, p=par, h=h, cg=cg, g0=g0: e.tensor_scalar(
                    V(f"qd3{p}"), V(f"qd{h}")[:, g0:g0 + CH],
                    V("pcol")[:, cg * 16 + h:cg * 16 + h + 1], None, ALU.mult),
                    wbufs=[BUF(f"qd3{par}")], rbufs=[BUF(f"qd{h}"), BUF("pcol")])
                eng("te", lambda e, p=par, h=h, bk=bank: e.matmul(
                    pb(bk, 0, 128), V(f"Sin{h}"), V(f"qd3{p}"), start=True, stop=True),
                    wbufs=[pbb[bank]], rbufs=[BUF(f"Sin{h}"), BUF(f"qd3{par}")])
                eng("dve", lambda e, h=h, g0=g0, bk=bank: e.tensor_add(
                    V(f"ofm{h}")[:, g0:g0 + CH], V(f"ofm{h}")[:, g0:g0 + CH],
                    pb(bk, 0, 128)),
                    wbufs=[BUF(f"ofm{h}_s{cg // 4}")], rbufs=[pbb[bank]])
        for ts in range(nts_all):
            for cg in range(4 * ts, 4 * ts + 4):
                corr_chunk(cg)
            if dbg_tap is not None and ts == nts_all - 1:
                for h in range(2):
                    dbg_tap(lambda h=h: d_dbg_of2.ap()[128 * h:128 * h + 128, :],
                            f"ofm{h}")
            c0 = 512 * ts
            for h in range(H):
                sqb = ("gtmp", "utmp")[h % 2]
                eng("dve", lambda e, h=h, c0=c0, sq=sqb: e.tensor_mul(
                    V(sq), V(f"ofm{h}")[:, c0:c0 + 512], V(f"ofm{h}")[:, c0:c0 + 512]),
                    wbufs=[BUF(sqb)], rbufs=[BUF(f"ofm{h}_s{ts}")])
                eng("te", lambda e, h=h, sq=sqb: e.matmul(
                    pb(1, 0, 512, 0, 1), V("ones_b")[:, 0:1], V(sq),
                    start=(h == 0), stop=(h == 15)),
                    wbufs=[pbb[1]], rbufs=[BUF(sqb), BUF("ones_b")])
            eng("act", lambda e: e.activation(
                V("row0"), pb(1, 0, 512, 0, 1), AF.Ln, scale=1.0 / D,
                bias=V("misc")[0:1, 2:3]),
                wbufs=[BUF("row0")], rbufs=[pbb[1], BUF("misc")])
            eng("act", lambda e: e.activation(V("row0"), V("row0"), AF.Exp, scale=-0.5),
                wbufs=[BUF("row0")])
            eng("te", lambda e: e.matmul(
                pb(2, 0, 512), V("ones_f")[0:1, 0:128], V("row0"),
                start=True, stop=True),
                wbufs=[pbb[2]], rbufs=[BUF("row0"), BUF("ones_f")])
            for h in range(H):
                eng("dve", lambda e, h=h, c0=c0: e.tensor_mul(
                    V(f"ofm{h}")[:, c0:c0 + 512], V(f"ofm{h}")[:, c0:c0 + 512],
                    pb(2, 0, 512)),
                    wbufs=[BUF(f"ofm{h}_s{ts}")], rbufs=[pbb[2]])
                eng("dve", lambda e, h=h, c0=c0: e.tensor_scalar(
                    V(f"ofm{h}")[:, c0:c0 + 512], V(f"ofm{h}")[:, c0:c0 + 512],
                    V("nwcol")[:, h:h + 1], None, ALU.mult),
                    wbufs=[BUF(f"ofm{h}_s{ts}")], rbufs=[BUF("nwcol")])
        for fp in range(16):
            b0 = (2 * fp) % 3
            b1 = (2 * fp + 1) % 3
            wb0, wb1 = BUF(f"wqk{b0}"), BUF(f"wqk{b1}")
            dma("pool", lambda p=b0: V(f"wqk{p}"),
                lambda fp=fp: d_wo.ap()[fp, :, 0:1024], wbufs=[wb0])
            dma("pool", lambda p=b1: V(f"wqk{p}"),
                lambda fp=fp: d_wo.ap()[fp, :, 1024:2048], wbufs=[wb1])
            base = 0 if fp % 2 == 0 else 4
            for fv in range(16):
                p_ = b0 if fv < 8 else b1
                wb_ = wb0 if fv < 8 else wb1
                for ts in range(nts_all):
                    bank = base + ts
                    eng("te", lambda e, bk=bank, p=p_, fv=fv, ts=ts: e.matmul(
                        pb(bk, 0, 512),
                        V(f"wqk{p}")[:, 128 * (fv % 8):128 * (fv % 8) + 128],
                        V(f"ofm{fv}")[:, 512 * ts:512 * ts + 512],
                        start=(fv == 0), stop=(fv == 15)),
                        wbufs=[pbb[bank]], rbufs=[wb_, BUF(f"ofm{fv}_s{ts}")])
            x2par = fp % 2
            x2b = BUF(f"x2_{x2par}")
            dma("pool", lambda p=x2par: V(f"x2_{p}"),
                lambda fp=fp: d_x.ap()[128 * fp:128 * fp + 128, :], wbufs=[x2b])
            for ts in range(nts_all):
                bank = base + ts
                eng("dve", lambda e, bk=bank, fp=fp, ts=ts, p=x2par: e.tensor_add(
                    V(f"x1_{fp}")[:, 512 * ts:512 * ts + 512],
                    V(f"x2_{p}")[:, 512 * ts:512 * ts + 512], pb(bk, 0, 512)),
                    wbufs=[BUF(f"x1_{fp}")], rbufs=[x2b, pbb[bank]])
        for ts in range(nts_all):
            c0 = 512 * ts
            for fp in range(16):
                sqb = ("gtmp", "utmp")[fp % 2]
                eng("te", lambda e, fp=fp, c0=c0: e.matmul(
                    pb(1, 0, 512, 0, 1), V("ones_b")[:, 0:1],
                    V(f"x1_{fp}")[:, c0:c0 + 512], start=(fp == 0), stop=(fp == 15)),
                    wbufs=[pbb[1]], rbufs=[BUF(f"x1_{fp}"), BUF("ones_b")])
                eng("dve", lambda e, fp=fp, c0=c0, sq=sqb: e.tensor_mul(
                    V(sq), V(f"x1_{fp}")[:, c0:c0 + 512],
                    V(f"x1_{fp}")[:, c0:c0 + 512]),
                    wbufs=[BUF(sqb)], rbufs=[BUF(f"x1_{fp}")])
                eng("te", lambda e, fp=fp, sq=sqb: e.matmul(
                    pb(5, 0, 512, 0, 1), V("ones_b")[:, 0:1], V(sq),
                    start=(fp == 0), stop=(fp == 15)),
                    wbufs=[pbb[5]], rbufs=[BUF(sqb), BUF("ones_b")])
            eng("act", lambda e: e.activation(
                V("row1"), pb(1, 0, 512, 0, 1), AF.Copy, scale=1.0 / D),
                wbufs=[BUF("row1")], rbufs=[pbb[1]])
            eng("act", lambda e: e.activation(V("row2"), V("row1"), AF.Square),
                wbufs=[BUF("row2")], rbufs=[BUF("row1")])
            eng("dve", lambda e: e.scalar_tensor_tensor(
                V("row2"), pb(5, 0, 512, 0, 1), 1.0 / D, V("row2"),
                ALU.mult, ALU.subtract),
                wbufs=[BUF("row2")], rbufs=[pbb[5]])
            eng("act", lambda e: e.activation(
                V("row2"), V("row2"), AF.Ln, bias=V("misc")[0:1, 3:4]),
                wbufs=[BUF("row2")], rbufs=[BUF("misc")])
            eng("act", lambda e: e.activation(V("row2"), V("row2"), AF.Exp, scale=-0.5),
                wbufs=[BUF("row2")])
            eng("te", lambda e: e.matmul(
                pb(2, 0, 512), V("ones_f")[0:1, 0:128], V("row1"),
                start=True, stop=True),
                wbufs=[pbb[2]], rbufs=[BUF("row1"), BUF("ones_f")])
            eng("te", lambda e: e.matmul(
                pb(6, 0, 512), V("ones_f")[0:1, 0:128], V("row2"),
                start=True, stop=True),
                wbufs=[pbb[6]], rbufs=[BUF("row2"), BUF("ones_f")])
            for fp in range(16):
                eng("dve", lambda e, fp=fp, c0=c0: e.tensor_sub(
                    V(f"x1_{fp}")[:, c0:c0 + 512], V(f"x1_{fp}")[:, c0:c0 + 512],
                    pb(2, 0, 512)),
                    wbufs=[BUF(f"x1_{fp}")], rbufs=[pbb[2]])
                eng("dve", lambda e, fp=fp, c0=c0: e.tensor_mul(
                    V(f"x1_{fp}")[:, c0:c0 + 512], V(f"x1_{fp}")[:, c0:c0 + 512],
                    pb(6, 0, 512)),
                    wbufs=[BUF(f"x1_{fp}")], rbufs=[pbb[6]])

    def emit_ffn(wag_tok):
        P.wait("sp", wag_tok)
        P.wait("te", wag_tok)
        PLANE = 264 * 128
        for half in range(nffn):
            t0 = half * TH
            ob_seq = []
            for c in range(NIT):
                ob_seq += [c, NIT + c]
            for pos, ot in enumerate(ob_seq):
                par = pos % 2
                bank0 = 0 if par == 0 else 4
                shard, olocal = ot // GU_SH, ot % GU_SH
                gb = BUF(f"gwb{par}")
                eoff = shard * SH_N + olocal * 16 * 128
                dma("sp", lambda p=par: V(f"gwb{p}"),
                    lambda eo=eoff: bass.AP(d_wdq_all, eo, [[PLANE, 128], [1, 2048]]),
                    wbufs=[gb])
                for q in range(16):
                    for ts in range(nts_h):
                        eng("te", lambda e, bk=bank0 + ts, p=par, q=q, ts=ts,
                            t0=t0: e.matmul(
                                pb(bk, 0, 512), V(f"gwb{p}")[:, 128 * q:128 * q + 128],
                                V(f"x1_{q}")[:, t0 + 512 * ts:t0 + 512 * ts + 512],
                                start=(q == 0), stop=(q == 15)),
                            wbufs=[pbb[bank0 + ts]], rbufs=[gb, BUF(f"x1_{q}")])
                if ot < NIT:
                    for ts in range(nts_h):
                        eng("act", lambda e, bk=bank0 + ts, c=ot, ts=ts: e.activation(
                            V(f"hh{c}")[:, 512 * ts:512 * ts + 512], pb(bk, 0, 512),
                            AF.Silu),
                            wbufs=[BUF(f"hh{ot}")], rbufs=[pbb[bank0 + ts]])
                else:
                    for ts in range(nts_h):
                        eng("dve", lambda e, bk=bank0 + ts, c=ot - NIT,
                            ts=ts: e.tensor_mul(
                                V(f"hh{c}")[:, 512 * ts:512 * ts + 512],
                                V(f"hh{c}")[:, 512 * ts:512 * ts + 512],
                                pb(bk, 0, 512)),
                            wbufs=[BUF(f"hh{ot - NIT}")], rbufs=[pbb[bank0 + ts]])
            for ts in range(nts_h):
                hoff = 512 * ts
                for F in range(16):
                    par = F % 2
                    bank = (2 if par == 0 else 6) + (F // 2) % 2
                    Floc = F % 8
                    for gdn in range(4):
                        sh = 2 * gdn + F // 8
                        db = BUF(f"dnb{(F * 4 + gdn) % 2}")
                        eoff = sh * SH_N + (176 + Floc * 11) * 128
                        dma("sp", lambda gg=(F * 4 + gdn) % 2: V(f"dnb{gg}"),
                            lambda eo=eoff: bass.AP(d_wdq_all, eo,
                                                    [[PLANE, 128], [1, 11 * 128]]),
                            wbufs=[db])
                        for itl in range(DN_IT):
                            it = gdn * DN_IT + itl
                            eng("te", lambda e, bk=bank, gg=(F * 4 + gdn) % 2,
                                itl=itl, it=it, ho=hoff: e.matmul(
                                    pb(bk, 0, 512),
                                    V(f"dnb{gg}")[:, 128 * itl:128 * itl + 128],
                                    V(f"hh{it}")[:, ho:ho + 512],
                                    start=(it == 0), stop=(it == NIT - 1)),
                                wbufs=[pbb[bank]], rbufs=[db, BUF(f"hh{it}")])
                    eng("dve", lambda e, bk=bank, fp=F: e.tensor_copy(
                        V(f"ysb{fp}")[:, 0:512], pb(bk, 0, 512)),
                        wbufs=[BUF(f"ysb{F}")], rbufs=[pbb[bank]])
                c0 = 0
                bS, bB, bV, bW = (0, 4, 1, 5) if ts == 0 else (2, 6, 3, 7)
                for fp in range(16):
                    sqb = ("gtmp", "utmp")[fp % 2]
                    eng("dve", lambda e, fp=fp, c0=c0, sq=sqb: e.tensor_mul(
                        V(sq), V(f"ysb{fp}")[:, c0:c0 + 512],
                        V(f"ysb{fp}")[:, c0:c0 + 512]),
                        wbufs=[BUF(sqb)], rbufs=[BUF(f"ysb{fp}")])
                    eng("te", lambda e, fp=fp, sq=sqb, b=bS: e.matmul(
                        pb(b, 0, 512, 0, 1), V("ones_b")[:, 0:1], V(sq),
                        start=(fp == 0), stop=(fp == 15)),
                        wbufs=[pbb[bS]], rbufs=[BUF(sqb), BUF("ones_b")])
                eng("act", lambda e, b=bS: e.activation(
                    V("row0"), pb(b, 0, 512, 0, 1), AF.Ln, scale=1.0 / D,
                    bias=V("misc")[0:1, 2:3]),
                    wbufs=[BUF("row0")], rbufs=[pbb[bS], BUF("misc")])
                eng("act", lambda e: e.activation(V("row0"), V("row0"), AF.Exp,
                                                  scale=-0.5),
                    wbufs=[BUF("row0")])
                eng("te", lambda e, b=bB: e.matmul(
                    pb(b, 0, 512), V("ones_f")[0:1, 0:128], V("row0"),
                    start=True, stop=True),
                    wbufs=[pbb[bB]], rbufs=[BUF("row0"), BUF("ones_f")])
                for fp in range(16):
                    eng("dve", lambda e, fp=fp, c0=c0, b=bB: e.tensor_mul(
                        V(f"ysb{fp}")[:, c0:c0 + 512], V(f"ysb{fp}")[:, c0:c0 + 512],
                        pb(b, 0, 512)),
                        wbufs=[BUF(f"ysb{fp}")], rbufs=[pbb[bB]])
                    eng("dve", lambda e, fp=fp, c0=c0, t0=t0, ho=hoff: e.tensor_add(
                        V(f"ysb{fp}")[:, c0:c0 + 512], V(f"ysb{fp}")[:, c0:c0 + 512],
                        V(f"x1_{fp}")[:, t0 + ho + c0:t0 + ho + c0 + 512]),
                        wbufs=[BUF(f"ysb{fp}")], rbufs=[BUF(f"x1_{fp}")])
                for fp in range(16):
                    eng("te", lambda e, fp=fp, c0=c0, b=bS: e.matmul(
                        pb(b, 0, 512, 0, 1), V("ones_b")[:, 0:1],
                        V(f"ysb{fp}")[:, c0:c0 + 512], start=(fp == 0), stop=(fp == 15)),
                        wbufs=[pbb[bS]], rbufs=[BUF(f"ysb{fp}"), BUF("ones_b")])
                    eng("dve", lambda e, fp=fp, c0=c0: e.tensor_mul(
                        V("gtmp"), V(f"ysb{fp}")[:, c0:c0 + 512],
                        V(f"ysb{fp}")[:, c0:c0 + 512]),
                        wbufs=[BUF("gtmp")], rbufs=[BUF(f"ysb{fp}")])
                    eng("te", lambda e, fp=fp, b=bV: e.matmul(
                        pb(b, 0, 512, 0, 1), V("ones_b")[:, 0:1], V("gtmp"),
                        start=(fp == 0), stop=(fp == 15)),
                        wbufs=[pbb[bV]], rbufs=[BUF("gtmp"), BUF("ones_b")])
                eng("act", lambda e, b=bS: e.activation(
                    V("row1"), pb(b, 0, 512, 0, 1), AF.Copy, scale=1.0 / D),
                    wbufs=[BUF("row1")], rbufs=[pbb[bS]])
                eng("act", lambda e: e.activation(V("row2"), V("row1"), AF.Square),
                    wbufs=[BUF("row2")], rbufs=[BUF("row1")])
                eng("dve", lambda e, b=bV: e.scalar_tensor_tensor(
                    V("row2"), pb(b, 0, 512, 0, 1), 1.0 / D, V("row2"),
                    ALU.mult, ALU.subtract),
                    wbufs=[BUF("row2")], rbufs=[pbb[bV]])
                eng("act", lambda e: e.activation(
                    V("row2"), V("row2"), AF.Ln, bias=V("misc")[0:1, 3:4]),
                    wbufs=[BUF("row2")], rbufs=[BUF("misc")])
                eng("act", lambda e: e.activation(V("row2"), V("row2"), AF.Exp,
                                                  scale=-0.5),
                    wbufs=[BUF("row2")])
                eng("te", lambda e, b=bB: e.matmul(
                    pb(b, 0, 512), V("ones_f")[0:1, 0:128], V("row1"),
                    start=True, stop=True),
                    wbufs=[pbb[bB]], rbufs=[BUF("row1"), BUF("ones_f")])
                eng("te", lambda e, b=bW: e.matmul(
                    pb(b, 0, 512), V("ones_f")[0:1, 0:128], V("row2"),
                    start=True, stop=True),
                    wbufs=[pbb[bW]], rbufs=[BUF("row2"), BUF("ones_f")])
                for fp in range(16):
                    spar = fp % 2
                    ob = BUF(f"ostg{spar}")
                    eng("dve", lambda e, fp=fp, c0=c0, p=spar, b=bB: e.tensor_sub(
                        V(f"ostg{p}"), V(f"ysb{fp}")[:, c0:c0 + 512], pb(b, 0, 512)),
                        wbufs=[ob], rbufs=[BUF(f"ysb{fp}"), pbb[bB]])
                    eng("dve", lambda e, p=spar, b=bW: e.tensor_mul(
                        V(f"ostg{p}"), V(f"ostg{p}"), pb(b, 0, 512)),
                        wbufs=[ob], rbufs=[pbb[bW]])
                    dma("pool", lambda fp=fp, t0=t0, c0=c0, ho=hoff: d_out.ap()[
                            128 * fp:128 * fp + 128,
                            t0 + ho + c0:t0 + ho + c0 + 512],
                        lambda p=spar: V(f"ostg{p}"), rbufs=[ob])

    emit_setup()
    prep_gen = prep_steps()
    state = {"done": False}

    def prep_pump():
        if state["done"]:
            return
        try:
            for _ in range(1):
                next(prep_gen)
        except StopIteration:
            state["done"] = True

    def tap(dst_fn, src_name, cols=None, rows=128):
        if not dbg:
            return
        b = BUF(src_name)
        dma("sp", dst_fn,
            lambda n=src_name, c=cols, r=rows: V(n)[0:r, :] if c is None else V(n)[0:r, 0:c],
            rbufs=[b])

    for b in range(nblk):
        pass1_block(b, prep_pump)
        if dbg and b == nblk - 1:
            tap(lambda: d_dbg_cum.ap()[:, 0:TBv], "cum16", rows=16)
    while not state["done"]:
        prep_pump()
    if dbg:
        for h in range(2):
            tap(lambda h=h: d_dbg_s.ap()[128 * h:128 * h + 128, :], f"S{h}")
        tap(lambda: d_dbg_pt.ap()[0:128, :], "PT1")
        tap(lambda: d_dbg_pt.ap()[128:256, :], "expD1")
        dma("sp", lambda: d_dbg_wdq.ap(),
            lambda: bass.AP(d_wdq_in, 0, [[128, SH_N // 128], [1, 128]]),
            extra=[(f"dmas{l}", P.cnt[f"dmas{l}"]) for l in range(NLANE)
                   if P.cnt[f"dmas{l}"] > 0])
    seed = [(s, P.cnt[s]) for s in (["te", "dve", "act"]
            + [f"dmas{l}" for l in range(NLANE)]
            + [f"dmap{l}" for l in range(NLANE)]) if P.cnt[s] > 0]
    for nm in ([f"Sin{h}" for h in range(H)] +
               ["acc16", "wcol", "zst0", "tst0", "tst1"]):
        BUF(nm).r = list(seed)
    zag_tok = emit_z_ag()
    wag_tok = emit_wdq_ag(
        [(f"dmas{l}", P.cnt[f"dmas{l}"]) for l in range(NLANE)
         if P.cnt[f"dmas{l}"] > 0]
        + [(f"dmap{l}", P.cnt[f"dmap{l}"]) for l in range(NLANE)
           if P.cnt[f"dmap{l}"] > 0])
    emit_combine(zag_tok)
    if dbg:
        for h in range(2):
            tap(lambda h=h: d_dbg_sin.ap()[128 * h:128 * h + 128, :], f"Sin{h}")
            tap(lambda h=h: d_dbg_qd.ap()[128 * h:128 * h + 128, :], f"qd{h}")
            for s_ in range(tcore // 512):
                tap(lambda h=h: d_dbg_of1.ap()[128 * h:128 * h + 128, :],
                    f"ofm{h}_s{s_}")
    emit_pass2(dbg_tap=tap if dbg else None)
    if dbg:
        for h in range(2):
            tap(lambda h=h: d_dbg_ofm.ap()[128 * h:128 * h + 128, :], f"ofm{h}")
        for f in range(16):
            tap(lambda f=f: d_dbg_x1.ap()[128 * f:128 * f + 128, :], f"x1_{f}")
    P.barrier()
    emit_ffn(wag_tok)
    for l in range(NLANE):
        if P.cnt[f"dmas{l}"] > 0:
            P.wait("sp", (f"dmas{l}", P.cnt[f"dmas{l}"]))
        if P.cnt[f"dmap{l}"] > 0:
            P.wait("sp", (f"dmap{l}", P.cnt[f"dmap{l}"]))

    with (
        nc.sbuf_tensor("arena", [128, _off[0]], U8) as arena_t,
        nc.psum_tensor("pb0", [128, 512], F32) as pb0,
        nc.psum_tensor("pb1", [128, 512], F32) as pb1,
        nc.psum_tensor("pb2", [128, 512], F32) as pb2,
        nc.psum_tensor("pb3", [128, 512], F32) as pb3,
        nc.psum_tensor("pb4", [128, 512], F32) as pb4,
        nc.psum_tensor("pb5", [128, 512], F32) as pb5,
        nc.psum_tensor("pb6", [128, 512], F32) as pb6,
        nc.psum_tensor("pb7", [128, 512], F32) as pb7,
        nc.semaphore("s_te") as s_te,
        nc.semaphore("s_dve") as s_dve,
        nc.semaphore("s_act") as s_act,
        nc.semaphore("s_cc") as s_cc,
        nc.Block() as block,
    ):
        view_ctx["arena"] = arena_t
        view_ctx["pb"] = [pb0, pb1, pb2, pb3, pb4, pb5, pb6, pb7]
        sems = {"te": s_te, "dve": s_dve, "act": s_act, "cc": s_cc}
        import contextlib
        _semstack = contextlib.ExitStack()
        for _q in ("dmap", "dmas"):
            for _l in range(NLANE):
                sems[f"{_q}{_l}"] = _semstack.enter_context(
                    nc.semaphore(f"s_{_q}{_l}"))

        @block.tensor
        def _(e):
            for cl in P.ops["te"]:
                cl(e, sems)

        @block.vector
        def _(e):
            for cl in P.ops["dve"]:
                cl(e, sems)

        @block.scalar
        def _(e):
            for cl in P.ops["act"]:
                cl(e, sems)

        @block.gpsimd
        def _(e):
            for cl in P.ops["pool"]:
                cl(e, sems)

        @block.sync
        def _(e):
            for cl in P.ops["sp"]:
                cl(e, sems)

    return nc


def kernel(L0_attn_x, L0_attn_Wq, L0_attn_Wk, L0_attn_Wv, L0_attn_Wf,
           L0_attn_Wo, L0_attn_norm_w, L0_ffn_gate_up_packed,
           L0_ffn_gate_up_scales, L0_ffn_down_packed, L0_ffn_down_scales):
    inputs = dict(L0_attn_x=L0_attn_x, L0_attn_Wq=L0_attn_Wq,
                  L0_attn_Wk=L0_attn_Wk, L0_attn_Wv=L0_attn_Wv,
                  L0_attn_Wf=L0_attn_Wf, L0_attn_Wo=L0_attn_Wo,
                  L0_attn_norm_w=L0_attn_norm_w,
                  L0_ffn_gate_up_packed=L0_ffn_gate_up_packed,
                  L0_ffn_gate_up_scales=L0_ffn_gate_up_scales,
                  L0_ffn_down_packed=L0_ffn_down_packed,
                  L0_ffn_down_scales=L0_ffn_down_scales)
    x = np.asarray(L0_attn_x)
    bsz, seq, d = x.shape
    tcore = bsz * seq // NCORES
    in_maps = host_prepare(inputs, tcore)
    if _CACHE.get("tcore") != tcore:
        _CACHE["nc"] = build_program(tcore)
        _CACHE["tcore"] = tcore
    res = run_bass_kernel_spmd(_CACHE["nc"], in_maps, core_ids=list(range(NCORES)))
    out = np.empty((bsz * seq, d), np.float32)
    for r in range(NCORES):
        out[r * tcore:(r + 1) * tcore][:, PERM] = res.results[r]["xout"].T
    return out.reshape(bsz, seq, d)


# version 16
# speedup vs baseline: 1.0248x; 1.0248x over previous
import numpy as np
import ml_dtypes

import concourse.bass as bass
import concourse.mybir as mybir
from concourse.bass_utils import run_bass_kernel_spmd

F32 = mybir.dt.float32
BF16 = mybir.dt.bfloat16
F16 = mybir.dt.float16
U8 = mybir.dt.uint8
AF = mybir.ActivationFunctionType
ALU = mybir.AluOpType
BF16NP = ml_dtypes.bfloat16

D = 2048
H = 16
E = 128
CH = 128
NCORES = 8
I_FFN = 5632
NGU = 2 * I_FFN // 128
NIT = I_FFN // 128
GU_SH = NGU // NCORES
DN_IT = 11
DN_FT = 8
DN_FW = DN_FT * 128
SH_GU = 16 * 128 * 1408
SH_DN = DN_IT * 128 * DN_FW
SH_N = SH_GU + SH_DN
ZROWS = H * 128 + 128

perm128 = np.concatenate([np.arange(0, 128, 2), np.arange(1, 128, 2)])
PERM = (np.arange(0, D, 128)[:, None] + perm128[None, :]).reshape(-1)
PERM_GU = (np.arange(0, 2 * I_FFN, 128)[:, None] + perm128[None, :]).reshape(-1)

_CACHE = {}


class Buf:
    __slots__ = ("w", "r")

    def __init__(self):
        self.w = None
        self.r = []

    def for_write(self):
        t = list(self.r)
        if self.w is not None:
            t.append(self.w)
        return t

    def for_read(self):
        return [self.w] if self.w is not None else []

    def wrote(self, tok):
        self.w = tok
        self.r = []

    def read(self, tok):
        if tok is not None:
            self.r.append(tok)


NLANE = 8


class Prog:
    STREAMS = ("te", "dve", "act", "pool", "sp")
    SEMS = {"te": 1, "dve": 1, "act": 1, "cc": 1}
    for _q in ("dmap", "dmas"):
        for _l in range(NLANE):
            SEMS[f"{_q}{_l}"] = 16
    OWN = {"te": "te", "dve": "dve", "act": "act"}

    def __init__(self):
        self.ops = {s: [] for s in self.STREAMS}
        self.cnt = {s: 0 for s in self.SEMS}
        self.nq = {"dmap": 0, "dmas": 0}
        self.waited = {s: {} for s in self.STREAMS}

    def op(self, stream, make, waits=(), sig=None):
        for t in waits:
            self.wait(stream, t)
        tok = None
        if sig is not None:
            self.cnt[sig] += self.SEMS[sig]
            tok = (sig, self.cnt[sig])

            def cl(eng, sems, _m=make, _s=sig, _a=self.SEMS[sig]):
                _m(eng).then_inc(sems[_s], _a)
        else:
            def cl(eng, sems, _m=make):
                _m(eng)
        self.ops[stream].append(cl)
        return tok

    def wait(self, stream, tok):
        if tok is None:
            return
        sem, v = tok
        if self.OWN.get(stream) == sem:
            return
        assert v <= self.cnt[sem], f"wait on future token {sem}:{v}"
        if self.waited[stream].get(sem, 0) >= v:
            return
        self.waited[stream][sem] = v

        def cl(eng, sems, _sem=sem, _v=v):
            eng.wait_ge(sems[_sem], _v)
        self.ops[stream].append(cl)

    def barrier(self):
        names = ["te", "dve", "act"] + [f"dmap{l}" for l in range(NLANE)] \
            + [f"dmas{l}" for l in range(NLANE)]
        toks = [(s, self.cnt[s]) for s in names if self.cnt[s] > 0]
        for s in self.STREAMS:
            for t in toks:
                self.wait(s, t)


DTSZ = {F32: 4, BF16: 2, U8: 1, F16: 2}


def _tile_oxf(w):
    a = w.reshape(16, 128, 16, 128)
    return np.ascontiguousarray(a.transpose(2, 1, 0, 3)).reshape(16, 128, 2048)


def _u8t(packed):
    o, p8 = packed.shape
    a = packed.view(np.uint8).reshape(o, p8, 4)
    return np.ascontiguousarray(a.transpose(1, 2, 0).reshape(p8 * 4, o))


def _dup_rows(u8t, tiles):
    n = len(tiles)
    out = np.empty((n, 128, u8t.shape[1]), np.uint8)
    for k, t in enumerate(tiles):
        blk = u8t[64 * t:64 * t + 64]
        out[k, 0:64] = blk
        out[k, 64:128] = blk
    return out


def host_prepare(inputs, tcore):
    x = np.asarray(inputs["L0_attn_x"], np.float32).reshape(-1, D)
    Wq = np.asarray(inputs["L0_attn_Wq"], np.float32)[PERM, :]
    Wk = np.asarray(inputs["L0_attn_Wk"], np.float32)[PERM, :]
    Wv = np.asarray(inputs["L0_attn_Wv"], np.float32)[PERM, :]
    Wf = np.asarray(inputs["L0_attn_Wf"], np.float32)[PERM, :]
    Wo = np.asarray(inputs["L0_attn_Wo"], np.float32)[:, PERM]
    nw = np.asarray(inputs["L0_attn_norm_w"], np.float32)
    gup = np.asarray(inputs["L0_ffn_gate_up_packed"], np.int32)[PERM_GU, :]
    gus = np.asarray(inputs["L0_ffn_gate_up_scales"], np.float16)[:, PERM_GU]
    dnp = np.asarray(inputs["L0_ffn_down_packed"], np.int32)[PERM, :]
    dns = np.asarray(inputs["L0_ffn_down_scales"], np.float16)[:, PERM]

    wq_t = _tile_oxf(Wq)
    wk_t = _tile_oxf(Wk)
    wo_t = _tile_oxf(Wo)
    wv_t = np.ascontiguousarray(
        Wv.reshape(16, 128, 4, 512).transpose(2, 0, 1, 3)).reshape(64, 128, 512)
    wf_t = np.ascontiguousarray(Wf.reshape(16, 128, 16))

    u8_gu = _u8t(gup)
    u8_dn = _u8t(dnp)

    ones_f = np.ones((128, 128), np.float32)
    ones_b = np.ones((128, 128), BF16NP)
    id_f = np.eye(128, dtype=np.float32)
    id_b = np.eye(128, dtype=np.float32).astype(BF16NP)
    mt = np.triu(np.full((128, 128), -1e9, np.float32), 1)
    sel = np.zeros((16, 2048), np.float32)
    for h in range(16):
        sel[h, 128 * h:128 * h + 128] = 1.0
    nwcol = np.ascontiguousarray(nw.reshape(16, 128).T)
    misc = np.zeros((128, 8), np.float32)
    misc[:, 0] = -4.0
    misc[:, 1] = -6.0
    misc[:, 2] = 1e-6
    misc[:, 3] = 1e-5

    in_maps = []
    for r in range(NCORES):
        toks = slice(r * tcore, (r + 1) * tcore)
        xc = np.ascontiguousarray(x[toks][:, PERM].T)
        gu_cols = slice(1408 * r, 1408 * r + 1408)
        gu_u8 = _dup_rows(u8_gu[:, gu_cols], range(16))
        gu_sc = np.ascontiguousarray(
            np.broadcast_to(gus[:, gu_cols][:, None, :], (16, 128, 1408)))
        it0 = DN_IT * (r // 2)
        ft0 = DN_FT * (r % 2)
        dn_cols = slice(128 * ft0, 128 * ft0 + DN_FW)
        dn_u8 = _dup_rows(u8_dn[:, dn_cols], range(it0, it0 + DN_IT))
        dn_sc = np.ascontiguousarray(
            np.broadcast_to(dns[it0:it0 + DN_IT, dn_cols][:, None, :],
                            (DN_IT, 128, DN_FW)))
        g0 = (r // 4) * 4
        msel = np.zeros((8, 8), np.float32)
        maskb = np.full((8,), -1e9, np.float32)
        for p in range(g0, r):
            maskb[p] = 0.0
            for q in range(p + 1, r):
                msel[p, q] = 1.0
        in_maps.append({
            "xcore": xc,
            "wq_t": wq_t, "wk_t": wk_t, "wv_t": wv_t, "wf_t": wf_t, "wo_t": wo_t,
            "gu_u8": gu_u8, "gu_sc": gu_sc, "dn_u8": dn_u8, "dn_sc": dn_sc,
            "ones_f": ones_f, "ones_b": ones_b, "id_f": id_f, "id_b": id_b,
            "mt_f": mt, "sel_f": sel, "nwcol": nwcol, "misc": misc,
            "mselr": np.ascontiguousarray(
                np.broadcast_to(msel.reshape(1, 64), (128, 64))).astype(np.float32),
            "maskb": np.ascontiguousarray(
                np.broadcast_to(maskb.reshape(1, 8), (128, 8))).astype(np.float32),
        })
    return in_maps


def build_program(tcore, dbg=False, variant=0):
    TBv = 256
    nblk = tcore // TBv
    cpb = TBv // CH
    nchunk = tcore // CH
    TH = min(1024, tcore)
    nffn = tcore // TH
    nts_all = tcore // 512
    nts_h = TH // 512

    nc = bass.Bass("TRN2", target_bir_lowering=False, detect_race_conditions=False)

    d_x = nc.dram_tensor("xcore", [D, tcore], F32, kind="ExternalInput")
    d_wq = nc.dram_tensor("wq_t", [16, 128, 2048], F32, kind="ExternalInput")
    d_wk = nc.dram_tensor("wk_t", [16, 128, 2048], F32, kind="ExternalInput")
    d_wv = nc.dram_tensor("wv_t", [64, 128, 512], F32, kind="ExternalInput")
    d_wf = nc.dram_tensor("wf_t", [16, 128, 16], F32, kind="ExternalInput")
    d_wo = nc.dram_tensor("wo_t", [16, 128, 2048], F32, kind="ExternalInput")
    d_guu = nc.dram_tensor("gu_u8", [16, 128, 1408], U8, kind="ExternalInput")
    d_gus = nc.dram_tensor("gu_sc", [16, 128, 1408], F16, kind="ExternalInput")
    d_dnu = nc.dram_tensor("dn_u8", [DN_IT, 128, DN_FW], U8, kind="ExternalInput")
    d_dns = nc.dram_tensor("dn_sc", [DN_IT, 128, DN_FW], F16, kind="ExternalInput")
    d_onef = nc.dram_tensor("ones_f", [128, 128], F32, kind="ExternalInput")
    d_oneb = nc.dram_tensor("ones_b", [128, 128], BF16, kind="ExternalInput")
    d_idf = nc.dram_tensor("id_f", [128, 128], F32, kind="ExternalInput")
    d_idb = nc.dram_tensor("id_b", [128, 128], BF16, kind="ExternalInput")
    d_mt = nc.dram_tensor("mt_f", [128, 128], F32, kind="ExternalInput")
    d_sel = nc.dram_tensor("sel_f", [16, 2048], F32, kind="ExternalInput")
    d_nw = nc.dram_tensor("nwcol", [128, 16], F32, kind="ExternalInput")
    d_misc = nc.dram_tensor("misc", [128, 8], F32, kind="ExternalInput")
    d_msel = nc.dram_tensor("mselr", [128, 64], F32, kind="ExternalInput")
    d_mask = nc.dram_tensor("maskb", [128, 8], F32, kind="ExternalInput")
    d_out = nc.dram_tensor("xout", [D, tcore], F32, kind="ExternalOutput")
    if dbg:
        d_dbg_wdq = nc.dram_tensor("dbg_wdq", [SH_N // 128, 128], BF16,
                                   kind="ExternalOutput")
        d_dbg_cum = nc.dram_tensor("dbg_cum", [16, 256], F32, kind="ExternalOutput")
        d_dbg_qd = nc.dram_tensor("dbg_qd", [2 * 128, tcore], BF16, kind="ExternalOutput")
        d_dbg_ofm = nc.dram_tensor("dbg_ofm", [2 * 128, tcore], BF16, kind="ExternalOutput")
        d_dbg_x1 = nc.dram_tensor("dbg_x1", [D, tcore], BF16, kind="ExternalOutput")
        d_dbg_sin = nc.dram_tensor("dbg_sin", [2 * 128, 128], BF16, kind="ExternalOutput")
        d_dbg_of1 = nc.dram_tensor("dbg_of1", [2 * 128, tcore], BF16, kind="ExternalOutput")
        d_dbg_of2 = nc.dram_tensor("dbg_of2", [2 * 128, tcore], BF16, kind="ExternalOutput")
        d_dbg_qp = nc.dram_tensor("dbg_qp", [128, 256], BF16, kind="ExternalOutput")
        d_dbg_kp = nc.dram_tensor("dbg_kp", [128, 256], BF16, kind="ExternalOutput")
        d_dbg_s = nc.dram_tensor("dbg_s", [2 * 128, 128], F32, kind="ExternalOutput")
        d_dbg_pt = nc.dram_tensor("dbg_pt", [2 * 128, 128], BF16, kind="ExternalOutput")
    d_wdq_in = nc.dram_tensor("wdq_in", [SH_N // 128, 128], BF16)
    d_wdq_all = nc.dram_tensor("wdq_all", [NCORES * (SH_N // 128), 128], BF16, addr_space="Shared")
    d_z_in = nc.dram_tensor("z_in", [ZROWS, 128], F32)
    d_z_all = nc.dram_tensor("z_all", [NCORES * ZROWS, 128], F32, addr_space="Shared")

    P = Prog()
    view_ctx = {}
    sb_handles = {}
    _off = [0]
    SBUF_BYTES = 212600

    def sb(name, parts, cols, dt):
        a = max(4, DTSZ[dt])
        o = (_off[0] + a - 1) // a * a
        assert o + cols * DTSZ[dt] <= SBUF_BYTES, f"arena overflow at {name}"
        _off[0] = o + cols * DTSZ[dt]
        sb_handles[name] = (o, parts, cols, dt)

    def alias(name, byte_off, parts, cols, dt):
        assert byte_off % DTSZ[dt] == 0
        assert byte_off + cols * DTSZ[dt] <= SBUF_BYTES
        sb_handles[name] = (byte_off, parts, cols, dt)

    def V(name):
        off, parts, cols, dt = sb_handles[name]
        t = view_ctx["arena"]
        o = off // DTSZ[dt]
        return t.ap().bitcast(dt)[0:parts, o:o + cols]

    PW = 384
    for name, parts, cols, dt in [
        ("ones_f", 128, 128, F32), ("ones_b", 128, 128, BF16),
        ("id_f", 16, 16, F32),
        ("nwcol", 128, 16, F32), ("misc", 128, 8, F32),
        ("row0", 1, 512, F32), ("row1", 1, 512, F32), ("row2", 1, 512, F32),
    ]:
        sb(name, parts, cols, dt)
    for h in range(H):
        sb(f"qd{h}", 128, tcore, BF16)
    ofm_first = _off[0]
    for h in range(H):
        sb(f"ofm{h}", 128, tcore, BF16)
    for h in range(H):
        sb(f"kT{h}", 128, TBv, BF16)
    for tt in range(cpb):
        sb(f"vtm{tt}", 128, D, BF16)
    for f in range(16):
        sb(f"xbf{f}", 128, TBv, BF16)
    for par in range(3):
        sb(f"wqk{par}", 128, 1024, BF16)
    for par in range(2):
        sb(f"wv{par}", 128, 512, BF16)
    alias("wqk3", sb_handles["wv0"][0], 128, 1024, BF16)
    alias("wqkp0", sb_handles["wqk0"][0], 128, 2048, BF16)
    alias("wqkp1", sb_handles["wqk2"][0], 128, 2048, BF16)
    for name, parts, cols, dt in [
        ("id_b", 128, 128, BF16), ("mt_b", 128, 128, BF16),
        ("sel_b", 16, 2048, BF16), ("mselr", 128, 64, F32),
        ("maskb", 128, 8, F32),
        ("cumhi", 16, TBv, BF16), ("culo", 16, TBv, BF16),
        ("expcum_b", 16, TBv, BF16),
    ]:
        sb(name, parts, cols, dt)
    for f in range(16):
        sb(f"wf{f}", 128, 16, BF16)
    for h in range(H):
        sb(f"S{h}", 128, 128, F32)
        sb(f"Sbf{h}", 128, 128, BF16)
    for name, parts, cols, dt in [
        ("pcol", 128, (nchunk + 1) * 16, F32), ("ltot", 128, 16, F32),
        ("cum16", 16, TBv, F32), ("e1", 16, TBv, F32),
    ]:
        sb(name, parts, cols, dt)
    for par in range(2):
        sb(f"negcumT{par}", 128, 16, F32)
        sb(f"emcT{par}", 128, 16, F32)
        sb(f"exptot{par}", 128, 16, F32)
        sb(f"totrow{par}", 1, 16, F32)
        for ln in range(2):
            sb(f"PT{par}{ln}", 128, 128, BF16)
            sb(f"expD{par}{ln}", 128, 128, BF16)
    attn_end = _off[0]
    prep0 = _off[0]
    for par in range(2):
        sb(f"pu8_{par}", 128, PW, U8)
        sb(f"psc_{par}", 128, PW, F32)
        sb(f"pwo_{par}", 128, PW, BF16)
    sb("pnib", 128, PW, U8)
    sb("pnf", 128, PW, F32)
    sb("pge", 128, PW, F32)
    sb("pm", 128, PW, F32)
    sb("pr4", 128, PW, F32)
    prep_end = _off[0]

    qd0 = sb_handles["qd0"][0]
    xbf0 = sb_handles["xbf0"][0]
    for par in range(2):
        for ln in range(2):
            alias(f"qd3{par}{ln}", sb_handles[f"PT{par}{ln}"][0], 128, 128, BF16)
        alias(f"qd3{par}", sb_handles[f"PT{par}0"][0], 128, 128, BF16)
    po = prep0

    def pov(name, parts, cols, dt):
        nonlocal po
        a = max(4, DTSZ[dt])
        po = (po + a - 1) // a * a
        alias(name, po, parts, cols, dt)
        po += cols * DTSZ[dt]
    for h in range(H):
        pov(f"Sin{h}", 128, 128, BF16)
    pov("acc16", 128, 16, F32)
    pov("wcol", 128, 16, F32)
    pov("zst0", 128, 128, BF16)
    pov("tstall", 128, 128, F32)
    pov("tst0", 128, 16, F32)
    pov("tst1", 128, 16, F32)
    _off[0] = max(_off[0], po)
    for f in range(16):
        alias(f"x1_{f}", qd0 + f * tcore * 2, 128, tcore, BF16)
    fo = ofm_first

    def fal(name, parts, cols, dt):
        nonlocal fo
        a = max(4, DTSZ[dt])
        fo = (fo + a - 1) // a * a
        alias(name, fo, parts, cols, dt)
        fo += cols * DTSZ[dt]
    for c in range(NIT):
        fal(f"hh{c}", 128, TH, BF16)
    for f in range(16):
        fal(f"ysb{f}", 128, 512, BF16)
    for par in range(2):
        fal(f"ostg{par}", 128, 512, F32)
    for k in range(2):
        fal(f"gwb{k}", 128, 2048, BF16)
    for k in range(2):
        fal(f"dnb{k}", 128, 11 * 128, BF16)
    fal("gtmp", 128, 512, BF16)
    fal("utmp", 128, 512, BF16)
    assert fo <= prep_end, (fo, prep_end)
    alias("zrank", xbf0, 128, 2048, F32)
    alias("xall", xbf0, 128, 16 * TBv, BF16)
    assert 2 * tcore * 2 <= 16 * TBv * 2
    for par in range(2):
        alias(f"x2_{par}", xbf0 + par * tcore * 2, 128, tcore, BF16)
    assert _off[0] <= SBUF_BYTES, f"total {_off[0]} > {SBUF_BYTES}"
    build_program.last_total = _off[0]

    bufs = {}

    def BUF(name):
        if name not in bufs:
            bufs[name] = Buf()
        return bufs[name]

    pbb = [Buf() for _ in range(8)]

    def pb(i, c0, c1, p0=0, p1=128):
        return view_ctx["pb"][i].ap()[p0:p1, c0:c1]

    def pb16(i, c0, ncols, p0=0, p1=128):
        return view_ctx["pb"][i].ap().bitcast(BF16)[p0:p1, 2 * c0:2 * c0 + ncols]

    def dma(queue, dst_fn, src_fn, wbufs=(), rbufs=(), extra=()):
        qn = "dmap" if queue == "pool" else "dmas"
        lane = P.nq[qn] % NLANE
        P.nq[qn] += 1
        semn = f"{qn}{lane}"
        waits = list(extra)
        for b in wbufs:
            waits += b.for_write()
        for b in rbufs:
            waits += b.for_read()
        if variant == 10 and P.cnt[semn] > 0:
            waits.append((semn, P.cnt[semn]))
        tok = P.op(queue, lambda e: e.dma_start(out=dst_fn(), in_=src_fn()),
                   waits=waits, sig=semn)
        for b in wbufs:
            b.wrote(tok)
        for b in rbufs:
            b.read(tok)
        return tok

    def eng(stream, make, wbufs=(), rbufs=(), extra=()):
        waits = list(extra)
        for b in wbufs:
            waits += b.for_write()
        for b in rbufs:
            waits += b.for_read()
        if variant == 9:
            for s2 in ("te", "dve", "act"):
                if P.cnt[s2] > 0:
                    waits.append((s2, P.cnt[s2]))
        tok = P.op(stream, make, waits=waits,
                   sig={"te": "te", "dve": "dve", "act": "act"}[stream])
        for b in wbufs:
            b.wrote(tok)
        for b in rbufs:
            b.read(tok)
        return tok

    def emit_setup():
        for name, dt_ in [("ones_f", d_onef), ("ones_b", d_oneb),
                          ("id_b", d_idb), ("mt_b", d_mt), ("sel_b", d_sel),
                          ("nwcol", d_nw), ("misc", d_misc), ("mselr", d_msel),
                          ("maskb", d_mask)]:
            dma("pool", lambda n=name: V(n), lambda t=dt_: t.ap(), wbufs=[BUF(name)])
        dma("pool", lambda: V("id_f"), lambda: d_idf.ap()[0:16, 0:16],
            wbufs=[BUF("id_f")])
        for f in range(16):
            dma("pool", lambda f=f: V(f"wf{f}"), lambda f=f: d_wf.ap()[f],
                wbufs=[BUF(f"wf{f}")])
        for h in range(H):
            eng("dve", lambda e, h=h: e.memset(V(f"S{h}"), 0.0), wbufs=[BUF(f"S{h}")])
            eng("dve", lambda e, h=h: e.memset(V(f"Sbf{h}"), 0.0), wbufs=[BUF(f"Sbf{h}")])
        eng("dve", lambda e: e.memset(V("pcol")[:, 0:16], 1.0), wbufs=[BUF("pcol")])
        eng("dve", lambda e: e.memset(V("ltot"), 0.0), wbufs=[BUF("ltot")])

    def prep_steps():
        jobs = []
        for q in range(16):
            for (c0, W) in ((0, 384), (384, 384), (768, 384), (1152, 256)):
                jobs.append(("gu", q, c0, W))
        for it in range(DN_IT):
            for (c0, W) in ((0, 384), (384, 384), (768, 256)):
                jobs.append(("dn", it, c0, W))
        par = 0
        for (kind, idx, c0, W) in jobs:
            u8b, scb, wob = BUF(f"pu8_{par}"), BUF(f"psc_{par}"), BUF(f"pwo_{par}")
            src_u = d_guu if kind == "gu" else d_dnu
            src_s = d_gus if kind == "gu" else d_dns
            dma("sp", lambda p=par, W=W: V(f"pu8_{p}")[:, 0:W],
                lambda s=src_u, i=idx, c0=c0, W=W: s.ap()[i, :, c0:c0 + W],
                wbufs=[u8b])
            dma("pool", lambda p=par, W=W: V(f"psc_{p}")[:, 0:W],
                lambda s=src_s, i=idx, c0=c0, W=W: s.ap()[i, :, c0:c0 + W],
                wbufs=[scb])
            yield
            nib, nf, ge, m, r4 = (BUF("pnib"), BUF("pnf"), BUF("pge"),
                                  BUF("pm"), BUF("pr4"))
            eng("dve", lambda e, W=W, p=par: e.tensor_single_scalar(
                V("pnib")[0:64, 0:W], V(f"pu8_{p}")[0:64, 0:W], 15,
                ALU.bitwise_and), wbufs=[nib], rbufs=[u8b])
            eng("dve", lambda e, W=W, p=par: e.tensor_single_scalar(
                V("pnib")[64:128, 0:W], V(f"pu8_{p}")[64:128, 0:W], 4,
                ALU.logical_shift_right), wbufs=[nib], rbufs=[u8b])
            eng("dve", lambda e, W=W: e.tensor_copy(
                V("pnf")[:, 0:W], V("pnib")[:, 0:W]),
                wbufs=[nf], rbufs=[nib])
            eng("dve", lambda e, W=W: e.tensor_single_scalar(
                V("pge")[:, 0:W], V("pnf")[:, 0:W], 8.0, ALU.is_ge),
                wbufs=[ge], rbufs=[nf])
            eng("dve", lambda e, W=W: e.scalar_tensor_tensor(
                V("pm")[:, 0:W], V("pge")[:, 0:W], -8.0, V("pnf")[:, 0:W],
                ALU.mult, ALU.add), wbufs=[m], rbufs=[ge, nf])
            eng("dve", lambda e, W=W: e.tensor_scalar(
                V("pge")[:, 0:W], V("pge")[:, 0:W], -2.0, 1.0, ALU.mult, ALU.add),
                wbufs=[ge])
            yield
            eng("act", lambda e, W=W: e.activation(
                V("pr4")[:, 0:W], V("pm")[:, 0:W], AF.Relu, bias=V("misc")[:, 0:1]),
                wbufs=[r4], rbufs=[m, BUF("misc")])
            eng("act", lambda e, W=W: e.activation(
                V("pnf")[:, 0:W], V("pm")[:, 0:W], AF.Relu, bias=V("misc")[:, 1:2]),
                wbufs=[nf], rbufs=[m, BUF("misc")])
            eng("dve", lambda e, W=W: e.tensor_add(
                V("pr4")[:, 0:W], V("pm")[:, 0:W], V("pr4")[:, 0:W]),
                wbufs=[r4], rbufs=[m])
            eng("dve", lambda e, W=W: e.scalar_tensor_tensor(
                V("pr4")[:, 0:W], V("pr4")[:, 0:W], 0.5, V("pnf")[:, 0:W],
                ALU.mult, ALU.add), wbufs=[r4], rbufs=[nf])
            eng("dve", lambda e, W=W, p=par: e.tensor_mul(
                V("pge")[:, 0:W], V("pge")[:, 0:W], V(f"psc_{p}")[:, 0:W]),
                wbufs=[ge], rbufs=[scb])
            eng("dve", lambda e, W=W, p=par: e.tensor_mul(
                V(f"pwo_{p}")[:, 0:W], V("pr4")[:, 0:W], V("pge")[:, 0:W]),
                wbufs=[wob], rbufs=[r4, ge])
            PLANE = 264 * 128
            nt = W // 128
            if kind == "gu":
                eoff = ((c0 // 128) * 16 + idx) * 128
                tstride = 16 * 128
            else:
                eoff = (176 + (c0 // 128) * 11 + idx) * 128
                tstride = 11 * 128
            dma("sp", lambda eo=eoff, tstr=tstride, nt=nt: bass.AP(
                    d_wdq_in, eo, [[PLANE, 128], [tstr, nt], [1, 128]]),
                lambda p=par, W=W: V(f"pwo_{p}")[:, 0:W],
                rbufs=[wob])
            par ^= 1
            yield

    def chunk_prep(cg):
        par = cg % 2
        lc = cg % cpb
        cb = lc * CH
        last = cb + CH - 1
        cumb = BUF("cum16")
        bank = 2 + 4 * par
        eng("te", lambda e: e.transpose(
            pb(bank, 256, 272), V("cum16")[:, cb:cb + CH], V("id_f")),
            wbufs=[pbb[bank]], rbufs=[cumb, BUF("id_f")])
        eng("act", lambda e: e.activation(
            V(f"negcumT{par}"), pb(bank, 256, 272), AF.Copy, scale=-1.0),
            wbufs=[BUF(f"negcumT{par}")], rbufs=[pbb[bank]])
        eng("act", lambda e: e.activation(
            V("e1")[:, 0:CH], V("cum16")[:, cb:cb + CH], AF.Exp, scale=-1.0,
            bias=V("cum16")[:, last:last + 1]),
            wbufs=[BUF("e1")], rbufs=[cumb])
        eng("te", lambda e: e.transpose(
            pb(bank, 272, 288), V("e1")[:, 0:CH], V("id_f")),
            wbufs=[pbb[bank]], rbufs=[BUF("e1"), BUF("id_f")])
        eng("act", lambda e: e.activation(
            V(f"emcT{par}"), pb(bank, 272, 288), AF.Copy),
            wbufs=[BUF(f"emcT{par}")], rbufs=[pbb[bank]])
        eng("te", lambda e: e.transpose(
            pb(bank, 304, 320, 0, 1), V("cum16")[:, last:last + 1], V("id_f")),
            wbufs=[pbb[bank]], rbufs=[cumb, BUF("id_f")])
        eng("act", lambda e: e.activation(
            V(f"totrow{par}"), pb(bank, 304, 320, 0, 1), AF.Copy),
            wbufs=[BUF(f"totrow{par}")], rbufs=[pbb[bank]])
        eng("te", lambda e: e.matmul(
            pb(bank, 288, 304), V("ones_f")[0:1, 0:128], V(f"totrow{par}"),
            start=True, stop=True),
            wbufs=[pbb[bank]], rbufs=[BUF(f"totrow{par}"), BUF("ones_f")])
        eng("dve", lambda e: e.tensor_add(V("ltot"), V("ltot"), pb(bank, 288, 304)),
            wbufs=[BUF("ltot")], rbufs=[pbb[bank]])
        eng("act", lambda e: e.activation(
            V(f"exptot{par}"), pb(bank, 288, 304), AF.Exp),
            wbufs=[BUF(f"exptot{par}")], rbufs=[pbb[bank]])
        eng("dve", lambda e, c=cg: e.tensor_mul(
            V("pcol")[:, (c + 1) * 16:(c + 2) * 16],
            V("pcol")[:, c * 16:(c + 1) * 16], V(f"exptot{par}")),
            wbufs=[BUF("pcol")], rbufs=[BUF(f"exptot{par}")])

    def inner2(cg, g):
        lc = cg % cpb
        cb = lc * CH
        par = g % 2
        cpar = cg % 2
        X = 4 * par
        hs = (2 * g, 2 * g + 1)
        g0 = cg * CH
        vb = BUF(f"vtm{lc}")
        def sc_q(i):
            return pb(X, 128 * i, 128 * i + 128)

        def kt_q(i):
            return pb16(X, 256 + 64 * i, 128)

        def d_q(i):
            return pb(X + 1, 128 * i, 128 * i + 128)

        def s_q(i):
            return pb(X + 1, 256 + 128 * i, 384 + 128 * i)

        def b_q(i):
            return pb(X + 2, 128 * i, 128 * i + 128)

        def o_q(i):
            return pb(X + 3, 128 * i, 128 * i + 128)
        for i, h in enumerate(hs):
            eng("te", lambda e, i=i, h=h: e.matmul(
                sc_q(i), V(f"kT{h}")[:, cb:cb + CH], V(f"qd{h}")[:, g0:g0 + CH],
                start=True, stop=True),
                wbufs=[pbb[X]], rbufs=[BUF(f"kT{h}"), BUF(f"qd{h}")])
        for i, h in enumerate(hs):
            eng("te", lambda e, i=i, h=h: e.matmul(
                d_q(i), V("sel_b")[:, 128 * h:128 * h + 128],
                V("cumhi")[:, cb:cb + CH], start=True, stop=False),
                wbufs=[pbb[X + 1]], rbufs=[BUF("sel_b"), BUF("cumhi")])
            eng("te", lambda e, i=i, h=h: e.matmul(
                d_q(i), V("sel_b")[:, 128 * h:128 * h + 128],
                V("culo")[:, cb:cb + CH], start=False, stop=False),
                wbufs=[pbb[X + 1]], rbufs=[BUF("sel_b"), BUF("culo")])
            eng("te", lambda e, i=i: e.matmul(
                d_q(i), V("mt_b"), V("id_b"), start=False, stop=True),
                wbufs=[pbb[X + 1]], rbufs=[BUF("mt_b"), BUF("id_b")])
        for i, h in enumerate(hs):
            eng("te", lambda e, i=i, h=h: e.matmul(
                b_q(i), V("sel_b")[:, 128 * h:128 * h + 128],
                V("expcum_b")[:, cb:cb + CH], start=True, stop=True),
                wbufs=[pbb[X + 2]], rbufs=[BUF("sel_b"), BUF("expcum_b")])
        for i, h in enumerate(hs):
            eng("act", lambda e, i=i, h=h: e.activation(
                V(f"expD{par}{i}"), d_q(i), AF.Exp,
                bias=V(f"negcumT{cpar}")[:, h:h + 1]),
                wbufs=[BUF(f"expD{par}{i}")],
                rbufs=[pbb[X + 1], BUF(f"negcumT{cpar}")])
        for i, h in enumerate(hs):
            eng("dve", lambda e, i=i: e.tensor_mul(
                V(f"PT{par}{i}"), V(f"expD{par}{i}"), sc_q(i)),
                wbufs=[BUF(f"PT{par}{i}")],
                rbufs=[BUF(f"expD{par}{i}"), pbb[X]])
            eng("dve", lambda e, i=i, h=h: e.tensor_mul(
                V(f"qd{h}")[:, g0:g0 + CH], V(f"qd{h}")[:, g0:g0 + CH], b_q(i)),
                wbufs=[BUF(f"qd{h}")], rbufs=[pbb[X + 2]])
        for i, h in enumerate(hs):
            eng("te", lambda e, i=i, h=h: e.matmul(
                o_q(i), V(f"vtm{lc}")[:, 128 * h:128 * h + 128], V(f"PT{par}{i}"),
                start=True, stop=False),
                wbufs=[pbb[X + 3]], rbufs=[vb, BUF(f"PT{par}{i}")])
            eng("te", lambda e, i=i, h=h: e.matmul(
                o_q(i), V(f"Sbf{h}"), V(f"qd{h}")[:, g0:g0 + CH],
                start=False, stop=True),
                wbufs=[pbb[X + 3]], rbufs=[BUF(f"Sbf{h}"), BUF(f"qd{h}")])
        for i, h in enumerate(hs):
            eng("te", lambda e, i=i, h=h: e.transpose(
                kt_q(i), V(f"kT{h}")[:, cb:cb + CH], V("id_b")),
                wbufs=[pbb[X]], rbufs=[BUF(f"kT{h}"), BUF("id_b")])
        for i, h in enumerate(hs):
            eng("act", lambda e, i=i, h=h: e.activation(
                V(f"qd3{par}{i}"), kt_q(i), AF.Copy,
                scale=V(f"emcT{cpar}")[:, h:h + 1]),
                wbufs=[BUF(f"qd3{par}{i}")],
                rbufs=[pbb[X], BUF(f"emcT{cpar}")])
        for i, h in enumerate(hs):
            eng("te", lambda e, i=i, h=h: e.matmul(
                s_q(i), V(f"qd3{par}{i}"), V(f"vtm{lc}")[:, 128 * h:128 * h + 128],
                start=True, stop=True),
                wbufs=[pbb[X + 1]], rbufs=[BUF(f"qd3{par}{i}"), vb])
        for i, h in enumerate(hs):
            eng("act", lambda e, i=i, h=h: e.activation(
                V(f"ofm{h}")[:, g0:g0 + CH], o_q(i), AF.Copy),
                wbufs=[BUF(f"ofm{h}_s{cg // 4}")], rbufs=[pbb[X + 3]])
        for i, h in enumerate(hs):
            eng("dve", lambda e, i=i, h=h: e.scalar_tensor_tensor(
                V(f"S{h}"), V(f"S{h}"), V(f"exptot{cpar}")[:, h:h + 1], s_q(i),
                ALU.mult, ALU.add),
                wbufs=[BUF(f"S{h}")], rbufs=[pbb[X + 1], BUF(f"exptot{cpar}")])
            eng("dve", lambda e, h=h: e.tensor_copy(V(f"Sbf{h}"), V(f"S{h}")),
                wbufs=[BUF(f"Sbf{h}")], rbufs=[BUF(f"S{h}")])

    def pass1_block(b, prep_pump):
        t0 = b * TBv
        xb = [BUF(f"xbf{f}") for f in range(16)]
        dma("pool", lambda: V("xall"),
            lambda t0=t0: bass.AP(d_x, t0, [[tcore, 128], [128 * tcore, 16],
                                            [1, TBv]]),
            wbufs=xb)
        for wi, (wd, kind) in enumerate(((d_wq, "q"), (d_wk, "k"))):
            for o in range(16):
                gidx = wi * 16 + o
                pr = gidx % 2
                b0, b1 = 2 * pr, 2 * pr + 1
                wb0, wb1 = BUF(f"wqk{b0}"), BUF(f"wqk{b1}")
                dma("pool", lambda p=pr: V(f"wqkp{p}"),
                    lambda w=wd, o=o: w.ap()[o, :, 0:2048],
                    wbufs=[wb0, wb1])
                bank = 0 if o % 2 == 0 else 4
                for f in range(16):
                    p_ = b0 if f < 8 else b1
                    wb_ = wb0 if f < 8 else wb1
                    eng("te", lambda e, bk=bank, p=p_, f=f: e.matmul(
                        pb(bk, 0, TBv), V(f"wqk{p}")[:, 128 * (f % 8):128 * (f % 8) + 128],
                        V(f"xbf{f}"), start=(f == 0), stop=(f == 15)),
                        wbufs=[pbb[bank]], rbufs=[wb_, xb[f]])
                af_pr = AF.Copy if variant == 6 else AF.Silu
                if kind == "q":
                    eng("act", lambda e, bk=bank, o=o, t0=t0, af=af_pr: e.activation(
                        V(f"qd{o}")[:, t0:t0 + TBv], pb(bk, 0, TBv), af),
                        wbufs=[BUF(f"qd{o}")], rbufs=[pbb[bank]])
                else:
                    eng("act", lambda e, bk=bank, o=o, af=af_pr: e.activation(
                        V(f"kT{o}"), pb(bk, 0, TBv), af),
                        wbufs=[BUF(f"kT{o}")], rbufs=[pbb[bank]])
                prep_pump()
        for oq in range(4):
            base = 0 if oq % 2 == 0 else 4
            for j in range(4):
                pr = (oq * 4 + j) % 2
                wvb0, wvb1 = BUF(f"wqk{2 * pr}"), BUF(f"wqk{2 * pr + 1}")
                eidx = oq * 16 + 4 * j
                dma("pool", lambda p=pr: V(f"wqkp{p}"),
                    lambda k=eidx: bass.AP(d_wv, k * 128 * 512,
                                           [[512, 128], [128 * 512, 4], [1, 512]]),
                    wbufs=[wvb0, wvb1])
                for fi in range(4):
                    f = 4 * j + fi
                    ph = 2 * pr + fi // 2
                    wbh = wvb0 if fi < 2 else wvb1
                    for tt in range(cpb):
                        bank = base + tt
                        eng("te", lambda e, bk=bank, p=ph, f=f, fi=fi, tt=tt: e.matmul(
                            pb(bk, 0, 512), V(f"xbf{f}")[:, 128 * tt:128 * tt + 128],
                            V(f"wqk{p}")[:, 512 * (fi % 2):512 * (fi % 2) + 512],
                            start=(f == 0), stop=(f == 15)),
                            wbufs=[pbb[bank]], rbufs=[wbh, xb[f]])
            for tt in range(cpb):
                bank = base + tt
                eng("dve", lambda e, bk=bank, tt=tt, oq=oq: e.tensor_copy(
                    V(f"vtm{tt}")[:, 512 * oq:512 * oq + 512], pb(bk, 0, 512)),
                    wbufs=[BUF(f"vtm{tt}")], rbufs=[pbb[bank]])
            prep_pump()
        for f in range(16):
            eng("te", lambda e, f=f: e.matmul(
                pb(1, 0, TBv, 0, 16), V(f"wf{f}"), V(f"xbf{f}"),
                start=(f == 0), stop=(f == 15)),
                wbufs=[pbb[1]], rbufs=[BUF(f"wf{f}"), xb[f]])
        eng("act", lambda e: e.activation(
            V("e1"), pb(1, 0, TBv, 0, 16), AF.Exp, scale=-1.0),
            wbufs=[BUF("e1")], rbufs=[pbb[1]])
        eng("dve", lambda e: e.tensor_single_scalar(V("e1"), V("e1"), 1.0, ALU.add),
            wbufs=[BUF("e1")])
        eng("act", lambda e: e.activation(V("e1"), V("e1"), AF.Ln),
            wbufs=[BUF("e1")])
        for lc in range(cpb):
            eng("dve", lambda e, lc=lc: e.tensor_tensor_scan(
                V("cum16")[:, lc * CH:lc * CH + CH], V("ones_f")[0:16, 0:CH],
                V("e1")[:, lc * CH:lc * CH + CH], 0.0, ALU.mult, ALU.subtract),
                wbufs=[BUF("cum16")], rbufs=[BUF("e1"), BUF("ones_f")])
        eng("dve", lambda e: e.tensor_copy(V("cumhi"), V("cum16")),
            wbufs=[BUF("cumhi")], rbufs=[BUF("cum16")])
        eng("dve", lambda e: e.tensor_sub(V("culo"), V("cum16"), V("cumhi")),
            wbufs=[BUF("culo")], rbufs=[BUF("cum16"), BUF("cumhi")])
        eng("act", lambda e: e.activation(V("expcum_b"), V("cum16"), AF.Exp),
            wbufs=[BUF("expcum_b")], rbufs=[BUF("cum16")])
        prep_pump()
        if dbg and b == nblk - 1:
            dma("sp", lambda: d_dbg_qp.ap(),
                lambda t0=t0: V("qd0")[:, t0:t0 + TBv], rbufs=[BUF("qd0")])
            dma("sp", lambda: d_dbg_kp.ap(), lambda: V("kT0"), rbufs=[BUF("kT0")])
        for lc in range(cpb):
            cg = b * cpb + lc
            chunk_prep(cg)
            for g in range(H // 2):
                inner2(cg, g)
            prep_pump()

    def emit_wdq_ag(prep_toks):
        return P.op("pool", lambda e: e.collective_compute(
            "AllGather", ALU.bypass, replica_groups=[list(range(NCORES))],
            ins=[d_wdq_in.ap().opt()], outs=[d_wdq_all.ap().opt()]),
            waits=prep_toks, sig="cc")

    def emit_z_ag():
        toks = []
        for h in range(H):
            toks.append(dma("sp", lambda h=h: d_z_in.ap()[128 * h:128 * h + 128, :],
                            lambda h=h: V(f"S{h}"), rbufs=[BUF(f"S{h}")]))
        toks.append(dma("sp", lambda: d_z_in.ap()[H * 128:H * 128 + 128, 0:16],
                        lambda: V("ltot"), rbufs=[BUF("ltot")]))
        toks.append(dma("sp", lambda: d_z_in.ap()[H * 128:H * 128 + 128, 16:128],
                        lambda: V("ones_f")[:, 0:112], rbufs=[BUF("ones_f")]))
        return P.op("pool", lambda e: e.collective_compute(
            "AllGather", ALU.bypass, replica_groups=[list(range(NCORES))],
            ins=[d_z_in.ap().opt()], outs=[d_z_all.ap().opt()]),
            waits=toks, sig="cc")

    def emit_combine(zag_tok):
        P.wait("sp", zag_tok)
        P.wait("pool", zag_tok)
        zrb = BUF("zrank")
        for f in range(16):
            zrb.r += BUF(f"xbf{f}").for_write()
        tb = BUF("tstall")
        for r in range(NCORES):
            dma("sp", lambda r=r: V("tstall")[:, 16 * r:16 * r + 16],
                lambda r=r: d_z_all.ap()[
                    r * ZROWS + H * 128:r * ZROWS + H * 128 + 128, 0:16],
                wbufs=[tb])
        for p in range(NCORES):
            accb = BUF("acc16")
            eng("dve", lambda e, p=p: e.tensor_scalar(
                V("acc16"), V("ones_f")[:, 0:16], V("maskb")[:, p:p + 1], None,
                ALU.mult), wbufs=[accb], rbufs=[BUF("maskb"), BUF("ones_f")])
            for r in range(NCORES):
                eng("dve", lambda e, p=p, r=r: e.scalar_tensor_tensor(
                    V("acc16"), V("tstall")[:, 16 * r:16 * r + 16],
                    V("mselr")[:, 8 * p + r:8 * p + r + 1],
                    V("acc16"), ALU.mult, ALU.add),
                    wbufs=[accb], rbufs=[tb, BUF("mselr")])
            eng("act", lambda e, p=p: e.activation(
                V("wcol"), V("acc16"), AF.Exp),
                wbufs=[BUF("wcol")], rbufs=[accb])
            dma("sp", lambda: V("zrank"),
                lambda p=p: bass.AP(d_z_all, (p * ZROWS) * 128,
                                    [[128, 128], [128 * 128, 16], [1, 128]]),
                wbufs=[zrb])
            for h in range(H):
                if p == 0:
                    eng("dve", lambda e, h=h: e.tensor_scalar(
                        V(f"S{h}"), V("zrank")[:, 128 * h:128 * h + 128],
                        V("wcol")[:, h:h + 1], None, ALU.mult),
                        wbufs=[BUF(f"S{h}")], rbufs=[zrb, BUF("wcol")])
                else:
                    eng("dve", lambda e, h=h: e.scalar_tensor_tensor(
                        V(f"S{h}"), V("zrank")[:, 128 * h:128 * h + 128],
                        V("wcol")[:, h:h + 1], V(f"S{h}"),
                        ALU.mult, ALU.add),
                        wbufs=[BUF(f"S{h}")], rbufs=[zrb, BUF("wcol")])
        for h in range(H):
            eng("dve", lambda e, h=h: e.tensor_copy(V(f"Sin{h}"), V(f"S{h}")),
                wbufs=[BUF(f"Sin{h}")], rbufs=[BUF(f"S{h}")])

    def emit_pass2(dbg_tap=None):
        for f in range(16):
            bufs[f"x1_{f}"] = BUF(f"qd{f}")
        for par in range(2):
            bb = BUF(f"x2_{par}")
            bb.r += BUF("zrank").for_write()
        for nm, src in (("gtmp", "wv0"), ("utmp", "wv1")):
            BUF(nm).r += BUF(src).for_write()
        def corr_chunk(cg):
            for h in range(H):
                par = (cg * H + h) % 2
                bank = 3 + 4 * par
                g0 = cg * CH
                eng("dve", lambda e, p=par, h=h, cg=cg, g0=g0: e.tensor_scalar(
                    V(f"qd3{p}"), V(f"qd{h}")[:, g0:g0 + CH],
                    V("pcol")[:, cg * 16 + h:cg * 16 + h + 1], None, ALU.mult),
                    wbufs=[BUF(f"qd3{par}")], rbufs=[BUF(f"qd{h}"), BUF("pcol")])
                eng("te", lambda e, p=par, h=h, bk=bank: e.matmul(
                    pb(bk, 0, 128), V(f"Sin{h}"), V(f"qd3{p}"), start=True, stop=True),
                    wbufs=[pbb[bank]], rbufs=[BUF(f"Sin{h}"), BUF(f"qd3{par}")])
                eng("dve", lambda e, h=h, g0=g0, bk=bank: e.tensor_add(
                    V(f"ofm{h}")[:, g0:g0 + CH], V(f"ofm{h}")[:, g0:g0 + CH],
                    pb(bk, 0, 128)),
                    wbufs=[BUF(f"ofm{h}_s{cg // 4}")], rbufs=[pbb[bank]])
        for ts in range(nts_all):
            for cg in range(4 * ts, 4 * ts + 4):
                corr_chunk(cg)
            if dbg_tap is not None and ts == nts_all - 1:
                for h in range(2):
                    dbg_tap(lambda h=h: d_dbg_of2.ap()[128 * h:128 * h + 128, :],
                            f"ofm{h}")
            c0 = 512 * ts
            for h in range(H):
                sqb = ("gtmp", "utmp")[h % 2]
                eng("dve", lambda e, h=h, c0=c0, sq=sqb: e.tensor_mul(
                    V(sq), V(f"ofm{h}")[:, c0:c0 + 512], V(f"ofm{h}")[:, c0:c0 + 512]),
                    wbufs=[BUF(sqb)], rbufs=[BUF(f"ofm{h}_s{ts}")])
                eng("te", lambda e, h=h, sq=sqb: e.matmul(
                    pb(1, 0, 512, 0, 1), V("ones_b")[:, 0:1], V(sq),
                    start=(h == 0), stop=(h == 15)),
                    wbufs=[pbb[1]], rbufs=[BUF(sqb), BUF("ones_b")])
            eng("act", lambda e: e.activation(
                V("row0"), pb(1, 0, 512, 0, 1), AF.Ln, scale=1.0 / D,
                bias=V("misc")[0:1, 2:3]),
                wbufs=[BUF("row0")], rbufs=[pbb[1], BUF("misc")])
            eng("act", lambda e: e.activation(V("row0"), V("row0"), AF.Exp, scale=-0.5),
                wbufs=[BUF("row0")])
            eng("te", lambda e: e.matmul(
                pb(2, 0, 512), V("ones_f")[0:1, 0:128], V("row0"),
                start=True, stop=True),
                wbufs=[pbb[2]], rbufs=[BUF("row0"), BUF("ones_f")])
            for h in range(H):
                eng("dve", lambda e, h=h, c0=c0: e.tensor_mul(
                    V(f"ofm{h}")[:, c0:c0 + 512], V(f"ofm{h}")[:, c0:c0 + 512],
                    pb(2, 0, 512)),
                    wbufs=[BUF(f"ofm{h}_s{ts}")], rbufs=[pbb[2]])
                eng("dve", lambda e, h=h, c0=c0: e.tensor_scalar(
                    V(f"ofm{h}")[:, c0:c0 + 512], V(f"ofm{h}")[:, c0:c0 + 512],
                    V("nwcol")[:, h:h + 1], None, ALU.mult),
                    wbufs=[BUF(f"ofm{h}_s{ts}")], rbufs=[BUF("nwcol")])
        for fp in range(16):
            pr = fp % 2
            b0, b1 = 2 * pr, 2 * pr + 1
            wb0, wb1 = BUF(f"wqk{b0}"), BUF(f"wqk{b1}")
            dma("pool", lambda p=pr: V(f"wqkp{p}"),
                lambda fp=fp: d_wo.ap()[fp, :, 0:2048], wbufs=[wb0, wb1])
            base = 0 if fp % 2 == 0 else 4
            for fv in range(16):
                p_ = b0 if fv < 8 else b1
                wb_ = wb0 if fv < 8 else wb1
                for ts in range(nts_all):
                    bank = base + ts
                    eng("te", lambda e, bk=bank, p=p_, fv=fv, ts=ts: e.matmul(
                        pb(bk, 0, 512),
                        V(f"wqk{p}")[:, 128 * (fv % 8):128 * (fv % 8) + 128],
                        V(f"ofm{fv}")[:, 512 * ts:512 * ts + 512],
                        start=(fv == 0), stop=(fv == 15)),
                        wbufs=[pbb[bank]], rbufs=[wb_, BUF(f"ofm{fv}_s{ts}")])
            x2par = fp % 2
            x2b = BUF(f"x2_{x2par}")
            dma("pool", lambda p=x2par: V(f"x2_{p}"),
                lambda fp=fp: d_x.ap()[128 * fp:128 * fp + 128, :], wbufs=[x2b])
            for ts in range(nts_all):
                bank = base + ts
                eng("dve", lambda e, bk=bank, fp=fp, ts=ts, p=x2par: e.tensor_add(
                    V(f"x1_{fp}")[:, 512 * ts:512 * ts + 512],
                    V(f"x2_{p}")[:, 512 * ts:512 * ts + 512], pb(bk, 0, 512)),
                    wbufs=[BUF(f"x1_{fp}")], rbufs=[x2b, pbb[bank]])
        for ts in range(nts_all):
            c0 = 512 * ts
            for fp in range(16):
                sqb = ("gtmp", "utmp")[fp % 2]
                eng("te", lambda e, fp=fp, c0=c0: e.matmul(
                    pb(1, 0, 512, 0, 1), V("ones_b")[:, 0:1],
                    V(f"x1_{fp}")[:, c0:c0 + 512], start=(fp == 0), stop=(fp == 15)),
                    wbufs=[pbb[1]], rbufs=[BUF(f"x1_{fp}"), BUF("ones_b")])
                eng("dve", lambda e, fp=fp, c0=c0, sq=sqb: e.tensor_mul(
                    V(sq), V(f"x1_{fp}")[:, c0:c0 + 512],
                    V(f"x1_{fp}")[:, c0:c0 + 512]),
                    wbufs=[BUF(sqb)], rbufs=[BUF(f"x1_{fp}")])
                eng("te", lambda e, fp=fp, sq=sqb: e.matmul(
                    pb(5, 0, 512, 0, 1), V("ones_b")[:, 0:1], V(sq),
                    start=(fp == 0), stop=(fp == 15)),
                    wbufs=[pbb[5]], rbufs=[BUF(sqb), BUF("ones_b")])
            eng("act", lambda e: e.activation(
                V("row1"), pb(1, 0, 512, 0, 1), AF.Copy, scale=1.0 / D),
                wbufs=[BUF("row1")], rbufs=[pbb[1]])
            eng("act", lambda e: e.activation(V("row2"), V("row1"), AF.Square),
                wbufs=[BUF("row2")], rbufs=[BUF("row1")])
            eng("dve", lambda e: e.scalar_tensor_tensor(
                V("row2"), pb(5, 0, 512, 0, 1), 1.0 / D, V("row2"),
                ALU.mult, ALU.subtract),
                wbufs=[BUF("row2")], rbufs=[pbb[5]])
            eng("act", lambda e: e.activation(
                V("row2"), V("row2"), AF.Ln, bias=V("misc")[0:1, 3:4]),
                wbufs=[BUF("row2")], rbufs=[BUF("misc")])
            eng("act", lambda e: e.activation(V("row2"), V("row2"), AF.Exp, scale=-0.5),
                wbufs=[BUF("row2")])
            eng("te", lambda e: e.matmul(
                pb(2, 0, 512), V("ones_f")[0:1, 0:128], V("row1"),
                start=True, stop=True),
                wbufs=[pbb[2]], rbufs=[BUF("row1"), BUF("ones_f")])
            eng("te", lambda e: e.matmul(
                pb(6, 0, 512), V("ones_f")[0:1, 0:128], V("row2"),
                start=True, stop=True),
                wbufs=[pbb[6]], rbufs=[BUF("row2"), BUF("ones_f")])
            for fp in range(16):
                eng("dve", lambda e, fp=fp, c0=c0: e.tensor_sub(
                    V(f"x1_{fp}")[:, c0:c0 + 512], V(f"x1_{fp}")[:, c0:c0 + 512],
                    pb(2, 0, 512)),
                    wbufs=[BUF(f"x1_{fp}")], rbufs=[pbb[2]])
                eng("dve", lambda e, fp=fp, c0=c0: e.tensor_mul(
                    V(f"x1_{fp}")[:, c0:c0 + 512], V(f"x1_{fp}")[:, c0:c0 + 512],
                    pb(6, 0, 512)),
                    wbufs=[BUF(f"x1_{fp}")], rbufs=[pbb[6]])

    def emit_ffn(wag_tok):
        P.wait("sp", wag_tok)
        P.wait("te", wag_tok)
        PLANE = 264 * 128
        for half in range(nffn):
            t0 = half * TH
            ob_seq = []
            for c in range(NIT):
                ob_seq += [c, NIT + c]
            for pos, ot in enumerate(ob_seq):
                par = pos % 2
                bank0 = 0 if par == 0 else 4
                shard, olocal = ot // GU_SH, ot % GU_SH
                gb = BUF(f"gwb{par}")
                eoff = shard * SH_N + olocal * 16 * 128
                dma("sp", lambda p=par: V(f"gwb{p}"),
                    lambda eo=eoff: bass.AP(d_wdq_all, eo, [[PLANE, 128], [1, 2048]]),
                    wbufs=[gb])
                for q in range(16):
                    for ts in range(nts_h):
                        eng("te", lambda e, bk=bank0 + ts, p=par, q=q, ts=ts,
                            t0=t0: e.matmul(
                                pb(bk, 0, 512), V(f"gwb{p}")[:, 128 * q:128 * q + 128],
                                V(f"x1_{q}")[:, t0 + 512 * ts:t0 + 512 * ts + 512],
                                start=(q == 0), stop=(q == 15)),
                            wbufs=[pbb[bank0 + ts]], rbufs=[gb, BUF(f"x1_{q}")])
                if ot < NIT:
                    for ts in range(nts_h):
                        eng("act", lambda e, bk=bank0 + ts, c=ot, ts=ts: e.activation(
                            V(f"hh{c}")[:, 512 * ts:512 * ts + 512], pb(bk, 0, 512),
                            AF.Silu),
                            wbufs=[BUF(f"hh{ot}")], rbufs=[pbb[bank0 + ts]])
                else:
                    for ts in range(nts_h):
                        eng("dve", lambda e, bk=bank0 + ts, c=ot - NIT,
                            ts=ts: e.tensor_mul(
                                V(f"hh{c}")[:, 512 * ts:512 * ts + 512],
                                V(f"hh{c}")[:, 512 * ts:512 * ts + 512],
                                pb(bk, 0, 512)),
                            wbufs=[BUF(f"hh{ot - NIT}")], rbufs=[pbb[bank0 + ts]])
            for ts in range(nts_h):
                hoff = 512 * ts
                for F in range(16):
                    par = F % 2
                    bank = (2 if par == 0 else 6) + (F // 2) % 2
                    Floc = F % 8
                    for gdn in range(4):
                        sh = 2 * gdn + F // 8
                        db = BUF(f"dnb{(F * 4 + gdn) % 2}")
                        eoff = sh * SH_N + (176 + Floc * 11) * 128
                        dma("sp", lambda gg=(F * 4 + gdn) % 2: V(f"dnb{gg}"),
                            lambda eo=eoff: bass.AP(d_wdq_all, eo,
                                                    [[PLANE, 128], [1, 11 * 128]]),
                            wbufs=[db])
                        for itl in range(DN_IT):
                            it = gdn * DN_IT + itl
                            eng("te", lambda e, bk=bank, gg=(F * 4 + gdn) % 2,
                                itl=itl, it=it, ho=hoff: e.matmul(
                                    pb(bk, 0, 512),
                                    V(f"dnb{gg}")[:, 128 * itl:128 * itl + 128],
                                    V(f"hh{it}")[:, ho:ho + 512],
                                    start=(it == 0), stop=(it == NIT - 1)),
                                wbufs=[pbb[bank]], rbufs=[db, BUF(f"hh{it}")])
                    eng("dve", lambda e, bk=bank, fp=F: e.tensor_copy(
                        V(f"ysb{fp}")[:, 0:512], pb(bk, 0, 512)),
                        wbufs=[BUF(f"ysb{F}")], rbufs=[pbb[bank]])
                c0 = 0
                bS, bB, bV, bW = (0, 4, 1, 5) if ts == 0 else (2, 6, 3, 7)
                for fp in range(16):
                    sqb = ("gtmp", "utmp")[fp % 2]
                    eng("dve", lambda e, fp=fp, c0=c0, sq=sqb: e.tensor_mul(
                        V(sq), V(f"ysb{fp}")[:, c0:c0 + 512],
                        V(f"ysb{fp}")[:, c0:c0 + 512]),
                        wbufs=[BUF(sqb)], rbufs=[BUF(f"ysb{fp}")])
                    eng("te", lambda e, fp=fp, sq=sqb, b=bS: e.matmul(
                        pb(b, 0, 512, 0, 1), V("ones_b")[:, 0:1], V(sq),
                        start=(fp == 0), stop=(fp == 15)),
                        wbufs=[pbb[bS]], rbufs=[BUF(sqb), BUF("ones_b")])
                eng("act", lambda e, b=bS: e.activation(
                    V("row0"), pb(b, 0, 512, 0, 1), AF.Ln, scale=1.0 / D,
                    bias=V("misc")[0:1, 2:3]),
                    wbufs=[BUF("row0")], rbufs=[pbb[bS], BUF("misc")])
                eng("act", lambda e: e.activation(V("row0"), V("row0"), AF.Exp,
                                                  scale=-0.5),
                    wbufs=[BUF("row0")])
                eng("te", lambda e, b=bB: e.matmul(
                    pb(b, 0, 512), V("ones_f")[0:1, 0:128], V("row0"),
                    start=True, stop=True),
                    wbufs=[pbb[bB]], rbufs=[BUF("row0"), BUF("ones_f")])
                for fp in range(16):
                    eng("dve", lambda e, fp=fp, c0=c0, b=bB: e.tensor_mul(
                        V(f"ysb{fp}")[:, c0:c0 + 512], V(f"ysb{fp}")[:, c0:c0 + 512],
                        pb(b, 0, 512)),
                        wbufs=[BUF(f"ysb{fp}")], rbufs=[pbb[bB]])
                    eng("dve", lambda e, fp=fp, c0=c0, t0=t0, ho=hoff: e.tensor_add(
                        V(f"ysb{fp}")[:, c0:c0 + 512], V(f"ysb{fp}")[:, c0:c0 + 512],
                        V(f"x1_{fp}")[:, t0 + ho + c0:t0 + ho + c0 + 512]),
                        wbufs=[BUF(f"ysb{fp}")], rbufs=[BUF(f"x1_{fp}")])
                for fp in range(16):
                    eng("te", lambda e, fp=fp, c0=c0, b=bS: e.matmul(
                        pb(b, 0, 512, 0, 1), V("ones_b")[:, 0:1],
                        V(f"ysb{fp}")[:, c0:c0 + 512], start=(fp == 0), stop=(fp == 15)),
                        wbufs=[pbb[bS]], rbufs=[BUF(f"ysb{fp}"), BUF("ones_b")])
                    eng("dve", lambda e, fp=fp, c0=c0: e.tensor_mul(
                        V("gtmp"), V(f"ysb{fp}")[:, c0:c0 + 512],
                        V(f"ysb{fp}")[:, c0:c0 + 512]),
                        wbufs=[BUF("gtmp")], rbufs=[BUF(f"ysb{fp}")])
                    eng("te", lambda e, fp=fp, b=bV: e.matmul(
                        pb(b, 0, 512, 0, 1), V("ones_b")[:, 0:1], V("gtmp"),
                        start=(fp == 0), stop=(fp == 15)),
                        wbufs=[pbb[bV]], rbufs=[BUF("gtmp"), BUF("ones_b")])
                eng("act", lambda e, b=bS: e.activation(
                    V("row1"), pb(b, 0, 512, 0, 1), AF.Copy, scale=1.0 / D),
                    wbufs=[BUF("row1")], rbufs=[pbb[bS]])
                eng("act", lambda e: e.activation(V("row2"), V("row1"), AF.Square),
                    wbufs=[BUF("row2")], rbufs=[BUF("row1")])
                eng("dve", lambda e, b=bV: e.scalar_tensor_tensor(
                    V("row2"), pb(b, 0, 512, 0, 1), 1.0 / D, V("row2"),
                    ALU.mult, ALU.subtract),
                    wbufs=[BUF("row2")], rbufs=[pbb[bV]])
                eng("act", lambda e: e.activation(
                    V("row2"), V("row2"), AF.Ln, bias=V("misc")[0:1, 3:4]),
                    wbufs=[BUF("row2")], rbufs=[BUF("misc")])
                eng("act", lambda e: e.activation(V("row2"), V("row2"), AF.Exp,
                                                  scale=-0.5),
                    wbufs=[BUF("row2")])
                eng("te", lambda e, b=bB: e.matmul(
                    pb(b, 0, 512), V("ones_f")[0:1, 0:128], V("row1"),
                    start=True, stop=True),
                    wbufs=[pbb[bB]], rbufs=[BUF("row1"), BUF("ones_f")])
                eng("te", lambda e, b=bW: e.matmul(
                    pb(b, 0, 512), V("ones_f")[0:1, 0:128], V("row2"),
                    start=True, stop=True),
                    wbufs=[pbb[bW]], rbufs=[BUF("row2"), BUF("ones_f")])
                for fp in range(16):
                    spar = fp % 2
                    ob = BUF(f"ostg{spar}")
                    eng("dve", lambda e, fp=fp, c0=c0, p=spar, b=bB: e.tensor_sub(
                        V(f"ostg{p}"), V(f"ysb{fp}")[:, c0:c0 + 512], pb(b, 0, 512)),
                        wbufs=[ob], rbufs=[BUF(f"ysb{fp}"), pbb[bB]])
                    eng("dve", lambda e, p=spar, b=bW: e.tensor_mul(
                        V(f"ostg{p}"), V(f"ostg{p}"), pb(b, 0, 512)),
                        wbufs=[ob], rbufs=[pbb[bW]])
                    dma("pool", lambda fp=fp, t0=t0, c0=c0, ho=hoff: d_out.ap()[
                            128 * fp:128 * fp + 128,
                            t0 + ho + c0:t0 + ho + c0 + 512],
                        lambda p=spar: V(f"ostg{p}"), rbufs=[ob])

    emit_setup()
    prep_gen = prep_steps()
    state = {"done": False}

    def prep_pump():
        if state["done"]:
            return
        try:
            for _ in range(1):
                next(prep_gen)
        except StopIteration:
            state["done"] = True

    def tap(dst_fn, src_name, cols=None, rows=128):
        if not dbg:
            return
        b = BUF(src_name)
        dma("sp", dst_fn,
            lambda n=src_name, c=cols, r=rows: V(n)[0:r, :] if c is None else V(n)[0:r, 0:c],
            rbufs=[b])

    for b in range(nblk):
        pass1_block(b, prep_pump)
        if dbg and b == nblk - 1:
            tap(lambda: d_dbg_cum.ap()[:, 0:TBv], "cum16", rows=16)
    while not state["done"]:
        prep_pump()
    if dbg:
        for h in range(2):
            tap(lambda h=h: d_dbg_s.ap()[128 * h:128 * h + 128, :], f"S{h}")
        tap(lambda: d_dbg_pt.ap()[0:128, :], "PT1")
        tap(lambda: d_dbg_pt.ap()[128:256, :], "expD1")
        dma("sp", lambda: d_dbg_wdq.ap(),
            lambda: bass.AP(d_wdq_in, 0, [[128, SH_N // 128], [1, 128]]),
            extra=[(f"dmas{l}", P.cnt[f"dmas{l}"]) for l in range(NLANE)
                   if P.cnt[f"dmas{l}"] > 0])
    seed = [(s, P.cnt[s]) for s in (["te", "dve", "act"]
            + [f"dmas{l}" for l in range(NLANE)]
            + [f"dmap{l}" for l in range(NLANE)]) if P.cnt[s] > 0]
    for nm in ([f"Sin{h}" for h in range(H)] +
               ["acc16", "wcol", "zst0", "tst0", "tst1"]):
        BUF(nm).r = list(seed)
    wag_tok = emit_wdq_ag(
        [(f"dmas{l}", P.cnt[f"dmas{l}"]) for l in range(NLANE)
         if P.cnt[f"dmas{l}"] > 0]
        + [(f"dmap{l}", P.cnt[f"dmap{l}"]) for l in range(NLANE)
           if P.cnt[f"dmap{l}"] > 0])
    zag_tok = emit_z_ag()
    emit_combine(zag_tok)
    if dbg:
        for h in range(2):
            tap(lambda h=h: d_dbg_sin.ap()[128 * h:128 * h + 128, :], f"Sin{h}")
            tap(lambda h=h: d_dbg_qd.ap()[128 * h:128 * h + 128, :], f"qd{h}")
            for s_ in range(tcore // 512):
                tap(lambda h=h: d_dbg_of1.ap()[128 * h:128 * h + 128, :],
                    f"ofm{h}_s{s_}")
    emit_pass2(dbg_tap=tap if dbg else None)
    if dbg:
        for h in range(2):
            tap(lambda h=h: d_dbg_ofm.ap()[128 * h:128 * h + 128, :], f"ofm{h}")
        for f in range(16):
            tap(lambda f=f: d_dbg_x1.ap()[128 * f:128 * f + 128, :], f"x1_{f}")
    P.barrier()
    emit_ffn(wag_tok)
    for l in range(NLANE):
        if P.cnt[f"dmas{l}"] > 0:
            P.wait("sp", (f"dmas{l}", P.cnt[f"dmas{l}"]))
        if P.cnt[f"dmap{l}"] > 0:
            P.wait("sp", (f"dmap{l}", P.cnt[f"dmap{l}"]))

    with (
        nc.sbuf_tensor("arena", [128, _off[0]], U8) as arena_t,
        nc.psum_tensor("pb0", [128, 512], F32) as pb0,
        nc.psum_tensor("pb1", [128, 512], F32) as pb1,
        nc.psum_tensor("pb2", [128, 512], F32) as pb2,
        nc.psum_tensor("pb3", [128, 512], F32) as pb3,
        nc.psum_tensor("pb4", [128, 512], F32) as pb4,
        nc.psum_tensor("pb5", [128, 512], F32) as pb5,
        nc.psum_tensor("pb6", [128, 512], F32) as pb6,
        nc.psum_tensor("pb7", [128, 512], F32) as pb7,
        nc.semaphore("s_te") as s_te,
        nc.semaphore("s_dve") as s_dve,
        nc.semaphore("s_act") as s_act,
        nc.semaphore("s_cc") as s_cc,
        nc.Block() as block,
    ):
        view_ctx["arena"] = arena_t
        view_ctx["pb"] = [pb0, pb1, pb2, pb3, pb4, pb5, pb6, pb7]
        sems = {"te": s_te, "dve": s_dve, "act": s_act, "cc": s_cc}
        import contextlib
        _semstack = contextlib.ExitStack()
        for _q in ("dmap", "dmas"):
            for _l in range(NLANE):
                sems[f"{_q}{_l}"] = _semstack.enter_context(
                    nc.semaphore(f"s_{_q}{_l}"))

        @block.tensor
        def _(e):
            for cl in P.ops["te"]:
                cl(e, sems)

        @block.vector
        def _(e):
            for cl in P.ops["dve"]:
                cl(e, sems)

        @block.scalar
        def _(e):
            for cl in P.ops["act"]:
                cl(e, sems)

        @block.gpsimd
        def _(e):
            for cl in P.ops["pool"]:
                cl(e, sems)

        @block.sync
        def _(e):
            for cl in P.ops["sp"]:
                cl(e, sems)

    return nc


def kernel(L0_attn_x, L0_attn_Wq, L0_attn_Wk, L0_attn_Wv, L0_attn_Wf,
           L0_attn_Wo, L0_attn_norm_w, L0_ffn_gate_up_packed,
           L0_ffn_gate_up_scales, L0_ffn_down_packed, L0_ffn_down_scales):
    inputs = dict(L0_attn_x=L0_attn_x, L0_attn_Wq=L0_attn_Wq,
                  L0_attn_Wk=L0_attn_Wk, L0_attn_Wv=L0_attn_Wv,
                  L0_attn_Wf=L0_attn_Wf, L0_attn_Wo=L0_attn_Wo,
                  L0_attn_norm_w=L0_attn_norm_w,
                  L0_ffn_gate_up_packed=L0_ffn_gate_up_packed,
                  L0_ffn_gate_up_scales=L0_ffn_gate_up_scales,
                  L0_ffn_down_packed=L0_ffn_down_packed,
                  L0_ffn_down_scales=L0_ffn_down_scales)
    x = np.asarray(L0_attn_x)
    bsz, seq, d = x.shape
    tcore = bsz * seq // NCORES
    in_maps = host_prepare(inputs, tcore)
    if _CACHE.get("tcore") != tcore:
        _CACHE["nc"] = build_program(tcore)
        _CACHE["tcore"] = tcore
    res = run_bass_kernel_spmd(_CACHE["nc"], in_maps, core_ids=list(range(NCORES)))
    out = np.empty((bsz * seq, d), np.float32)
    for r in range(NCORES):
        out[r * tcore:(r + 1) * tcore][:, PERM] = res.results[r]["xout"].T
    return out.reshape(bsz, seq, d)


# version 17 (final)
# speedup vs baseline: 1.1133x; 1.0864x over previous
import numpy as np
import ml_dtypes

import concourse.bass as bass
import concourse.mybir as mybir
from concourse.bass_utils import run_bass_kernel_spmd

F32 = mybir.dt.float32
BF16 = mybir.dt.bfloat16
F16 = mybir.dt.float16
U8 = mybir.dt.uint8
AF = mybir.ActivationFunctionType
ALU = mybir.AluOpType
BF16NP = ml_dtypes.bfloat16

D = 2048
H = 16
E = 128
CH = 128
NCORES = 8
I_FFN = 5632
NGU = 2 * I_FFN // 128
NIT = I_FFN // 128
GU_SH = NGU // NCORES
DN_IT = 11
DN_FT = 8
DN_FW = DN_FT * 128
SH_GU = 16 * 128 * 1408
SH_DN = DN_IT * 128 * DN_FW
SH_N = SH_GU + SH_DN
ZROWS = H * 128 + 128

perm128 = np.concatenate([np.arange(0, 128, 2), np.arange(1, 128, 2)])
PERM = (np.arange(0, D, 128)[:, None] + perm128[None, :]).reshape(-1)
PERM_GU = (np.arange(0, 2 * I_FFN, 128)[:, None] + perm128[None, :]).reshape(-1)

_CACHE = {}


class Buf:
    __slots__ = ("w", "r")

    def __init__(self):
        self.w = None
        self.r = []

    def for_write(self):
        t = list(self.r)
        if self.w is not None:
            t.append(self.w)
        return t

    def for_read(self):
        return [self.w] if self.w is not None else []

    def wrote(self, tok):
        self.w = tok
        self.r = []

    def read(self, tok):
        if tok is not None:
            self.r.append(tok)


NLANE = 8


class Prog:
    STREAMS = ("te", "dve", "act", "pool", "sp")
    SEMS = {"te": 1, "dve": 1, "act": 1, "cc": 1}
    for _q in ("dmap", "dmas"):
        for _l in range(NLANE):
            SEMS[f"{_q}{_l}"] = 16
    OWN = {"te": "te", "dve": "dve", "act": "act"}

    def __init__(self):
        self.ops = {s: [] for s in self.STREAMS}
        self.cnt = {s: 0 for s in self.SEMS}
        self.nq = {"dmap": 0, "dmas": 0}
        self.waited = {s: {} for s in self.STREAMS}

    def op(self, stream, make, waits=(), sig=None):
        for t in waits:
            self.wait(stream, t)
        tok = None
        if sig is not None:
            self.cnt[sig] += self.SEMS[sig]
            tok = (sig, self.cnt[sig])

            def cl(eng, sems, _m=make, _s=sig, _a=self.SEMS[sig]):
                _m(eng).then_inc(sems[_s], _a)
        else:
            def cl(eng, sems, _m=make):
                _m(eng)
        self.ops[stream].append(cl)
        return tok

    def wait(self, stream, tok):
        if tok is None:
            return
        sem, v = tok
        if self.OWN.get(stream) == sem:
            return
        assert v <= self.cnt[sem], f"wait on future token {sem}:{v}"
        if self.waited[stream].get(sem, 0) >= v:
            return
        self.waited[stream][sem] = v

        def cl(eng, sems, _sem=sem, _v=v):
            eng.wait_ge(sems[_sem], _v)
        self.ops[stream].append(cl)

    def barrier(self):
        names = ["te", "dve", "act"] + [f"dmap{l}" for l in range(NLANE)] \
            + [f"dmas{l}" for l in range(NLANE)]
        toks = [(s, self.cnt[s]) for s in names if self.cnt[s] > 0]
        for s in self.STREAMS:
            for t in toks:
                self.wait(s, t)


DTSZ = {F32: 4, BF16: 2, U8: 1, F16: 2}


def _tile_oxf(w):
    a = w.reshape(16, 128, 16, 128)
    return np.ascontiguousarray(a.transpose(2, 1, 0, 3)).reshape(16, 128, 2048)


def _u8t(packed):
    o, p8 = packed.shape
    a = packed.view(np.uint8).reshape(o, p8, 4)
    return np.ascontiguousarray(a.transpose(1, 2, 0).reshape(p8 * 4, o))


def _dup_rows(u8t, tiles):
    n = len(tiles)
    out = np.empty((n, 128, u8t.shape[1]), np.uint8)
    for k, t in enumerate(tiles):
        blk = u8t[64 * t:64 * t + 64]
        out[k, 0:64] = blk
        out[k, 64:128] = blk
    return out


def host_prepare(inputs, tcore):
    x = np.asarray(inputs["L0_attn_x"], np.float32).reshape(-1, D)
    Wq = np.asarray(inputs["L0_attn_Wq"], np.float32)[PERM, :]
    Wk = np.asarray(inputs["L0_attn_Wk"], np.float32)[PERM, :]
    Wv = np.asarray(inputs["L0_attn_Wv"], np.float32)[PERM, :]
    Wf = np.asarray(inputs["L0_attn_Wf"], np.float32)[PERM, :]
    Wo = np.asarray(inputs["L0_attn_Wo"], np.float32)[:, PERM]
    nw = np.asarray(inputs["L0_attn_norm_w"], np.float32)
    gup = np.asarray(inputs["L0_ffn_gate_up_packed"], np.int32)[PERM_GU, :]
    gus = np.asarray(inputs["L0_ffn_gate_up_scales"], np.float16)[:, PERM_GU]
    dnp = np.asarray(inputs["L0_ffn_down_packed"], np.int32)[PERM, :]
    dns = np.asarray(inputs["L0_ffn_down_scales"], np.float16)[:, PERM]

    wq_t = _tile_oxf(Wq)
    wk_t = _tile_oxf(Wk)
    wo_t = _tile_oxf(Wo)
    wv_t = np.ascontiguousarray(
        Wv.reshape(16, 128, 4, 512).transpose(2, 0, 1, 3)).reshape(64, 128, 512)
    wf_t = np.ascontiguousarray(Wf.reshape(16, 128, 16))

    u8_gu = _u8t(gup)
    u8_dn = _u8t(dnp)

    ones_f = np.ones((128, 128), np.float32)
    ones_b = np.ones((128, 128), BF16NP)
    id_f = np.eye(128, dtype=np.float32)
    id_b = np.eye(128, dtype=np.float32).astype(BF16NP)
    mt = np.triu(np.full((128, 128), -1e9, np.float32), 1)
    sel = np.zeros((16, 2048), np.float32)
    for h in range(16):
        sel[h, 128 * h:128 * h + 128] = 1.0
    nwcol = np.ascontiguousarray(nw.reshape(16, 128).T)
    misc = np.zeros((128, 8), np.float32)
    misc[:, 0] = -4.0
    misc[:, 1] = -6.0
    misc[:, 2] = 1e-6
    misc[:, 3] = 1e-5

    in_maps = []
    for r in range(NCORES):
        toks = slice(r * tcore, (r + 1) * tcore)
        xc = np.ascontiguousarray(x[toks][:, PERM].T)
        gu_cols = slice(1408 * r, 1408 * r + 1408)
        gu_u8 = _dup_rows(u8_gu[:, gu_cols], range(16))
        gu_sc = np.ascontiguousarray(
            np.broadcast_to(gus[:, gu_cols][:, None, :], (16, 128, 1408)))
        it0 = DN_IT * (r // 2)
        ft0 = DN_FT * (r % 2)
        dn_cols = slice(128 * ft0, 128 * ft0 + DN_FW)
        dn_u8 = _dup_rows(u8_dn[:, dn_cols], range(it0, it0 + DN_IT))
        dn_sc = np.ascontiguousarray(
            np.broadcast_to(dns[it0:it0 + DN_IT, dn_cols][:, None, :],
                            (DN_IT, 128, DN_FW)))
        g0 = (r // 4) * 4
        msel = np.zeros((8, 8), np.float32)
        maskb = np.full((8,), -1e9, np.float32)
        for p in range(g0, r):
            maskb[p] = 0.0
            for q in range(p + 1, r):
                msel[p, q] = 1.0
        in_maps.append({
            "xcore": xc,
            "wq_t": wq_t, "wk_t": wk_t, "wv_t": wv_t, "wf_t": wf_t, "wo_t": wo_t,
            "gu_u8": gu_u8, "gu_sc": gu_sc, "dn_u8": dn_u8, "dn_sc": dn_sc,
            "ones_f": ones_f, "ones_b": ones_b, "id_f": id_f, "id_b": id_b,
            "mt_f": mt, "sel_f": sel, "nwcol": nwcol, "misc": misc,
            "mselr": np.ascontiguousarray(
                np.broadcast_to(msel.reshape(1, 64), (128, 64))).astype(np.float32),
            "maskb": np.ascontiguousarray(
                np.broadcast_to(maskb.reshape(1, 8), (128, 8))).astype(np.float32),
        })
    return in_maps


def build_program(tcore, dbg=False, variant=0):
    TBv = 256
    nblk = tcore // TBv
    cpb = TBv // CH
    nchunk = tcore // CH
    TH = min(1024, tcore)
    nffn = tcore // TH
    nts_all = tcore // 512
    nts_h = TH // 512

    nc = bass.Bass("TRN2", target_bir_lowering=False, detect_race_conditions=False)

    d_x = nc.dram_tensor("xcore", [D, tcore], F32, kind="ExternalInput")
    d_wq = nc.dram_tensor("wq_t", [16, 128, 2048], F32, kind="ExternalInput")
    d_wk = nc.dram_tensor("wk_t", [16, 128, 2048], F32, kind="ExternalInput")
    d_wv = nc.dram_tensor("wv_t", [64, 128, 512], F32, kind="ExternalInput")
    d_wf = nc.dram_tensor("wf_t", [16, 128, 16], F32, kind="ExternalInput")
    d_wo = nc.dram_tensor("wo_t", [16, 128, 2048], F32, kind="ExternalInput")
    d_guu = nc.dram_tensor("gu_u8", [16, 128, 1408], U8, kind="ExternalInput")
    d_gus = nc.dram_tensor("gu_sc", [16, 128, 1408], F16, kind="ExternalInput")
    d_dnu = nc.dram_tensor("dn_u8", [DN_IT, 128, DN_FW], U8, kind="ExternalInput")
    d_dns = nc.dram_tensor("dn_sc", [DN_IT, 128, DN_FW], F16, kind="ExternalInput")
    d_onef = nc.dram_tensor("ones_f", [128, 128], F32, kind="ExternalInput")
    d_oneb = nc.dram_tensor("ones_b", [128, 128], BF16, kind="ExternalInput")
    d_idf = nc.dram_tensor("id_f", [128, 128], F32, kind="ExternalInput")
    d_idb = nc.dram_tensor("id_b", [128, 128], BF16, kind="ExternalInput")
    d_mt = nc.dram_tensor("mt_f", [128, 128], F32, kind="ExternalInput")
    d_sel = nc.dram_tensor("sel_f", [16, 2048], F32, kind="ExternalInput")
    d_nw = nc.dram_tensor("nwcol", [128, 16], F32, kind="ExternalInput")
    d_misc = nc.dram_tensor("misc", [128, 8], F32, kind="ExternalInput")
    d_msel = nc.dram_tensor("mselr", [128, 64], F32, kind="ExternalInput")
    d_mask = nc.dram_tensor("maskb", [128, 8], F32, kind="ExternalInput")
    d_out = nc.dram_tensor("xout", [D, tcore], F32, kind="ExternalOutput")
    if dbg:
        d_dbg_wdq = nc.dram_tensor("dbg_wdq", [SH_N // 128, 128], BF16,
                                   kind="ExternalOutput")
        d_dbg_cum = nc.dram_tensor("dbg_cum", [16, 256], F32, kind="ExternalOutput")
        d_dbg_qd = nc.dram_tensor("dbg_qd", [2 * 128, tcore], BF16, kind="ExternalOutput")
        d_dbg_ofm = nc.dram_tensor("dbg_ofm", [2 * 128, tcore], BF16, kind="ExternalOutput")
        d_dbg_x1 = nc.dram_tensor("dbg_x1", [D, tcore], BF16, kind="ExternalOutput")
        d_dbg_sin = nc.dram_tensor("dbg_sin", [2 * 128, 128], BF16, kind="ExternalOutput")
        d_dbg_of1 = nc.dram_tensor("dbg_of1", [2 * 128, tcore], BF16, kind="ExternalOutput")
        d_dbg_of2 = nc.dram_tensor("dbg_of2", [2 * 128, tcore], BF16, kind="ExternalOutput")
        d_dbg_qp = nc.dram_tensor("dbg_qp", [128, 256], BF16, kind="ExternalOutput")
        d_dbg_kp = nc.dram_tensor("dbg_kp", [128, 256], BF16, kind="ExternalOutput")
        d_dbg_s = nc.dram_tensor("dbg_s", [2 * 128, 128], F32, kind="ExternalOutput")
        d_dbg_pt = nc.dram_tensor("dbg_pt", [2 * 128, 128], BF16, kind="ExternalOutput")
    d_wdq_in = nc.dram_tensor("wdq_in", [SH_N // 128, 128], BF16)
    d_wdq_all = nc.dram_tensor("wdq_all", [NCORES * (SH_N // 128), 128], BF16, addr_space="Shared")
    d_z_in = nc.dram_tensor("z_in", [ZROWS, 128], F32)
    d_z_all = nc.dram_tensor("z_all", [NCORES * ZROWS, 128], F32, addr_space="Shared")

    P = Prog()
    view_ctx = {}
    sb_handles = {}
    _off = [0]
    SBUF_BYTES = 212600

    def sb(name, parts, cols, dt):
        a = max(4, DTSZ[dt])
        o = (_off[0] + a - 1) // a * a
        assert o + cols * DTSZ[dt] <= SBUF_BYTES, f"arena overflow at {name}"
        _off[0] = o + cols * DTSZ[dt]
        sb_handles[name] = (o, parts, cols, dt)

    def alias(name, byte_off, parts, cols, dt):
        assert byte_off % DTSZ[dt] == 0
        assert byte_off + cols * DTSZ[dt] <= SBUF_BYTES
        sb_handles[name] = (byte_off, parts, cols, dt)

    def V(name):
        off, parts, cols, dt = sb_handles[name]
        t = view_ctx["arena"]
        o = off // DTSZ[dt]
        return t.ap().bitcast(dt)[0:parts, o:o + cols]

    PW = 384
    for name, parts, cols, dt in [
        ("ones_f", 128, 128, F32), ("ones_b", 128, 128, BF16),
        ("id_f", 16, 16, F32),
        ("nwcol", 128, 16, F32), ("misc", 128, 8, F32),
        ("row0", 1, 512, F32), ("row1", 1, 512, F32), ("row2", 1, 512, F32),
    ]:
        sb(name, parts, cols, dt)
    for h in range(H):
        sb(f"qd{h}", 128, tcore, BF16)
    ofm_first = _off[0]
    for h in range(H):
        sb(f"ofm{h}", 128, tcore, BF16)
    for h in range(H):
        sb(f"kT{h}", 128, TBv, BF16)
    for tt in range(cpb):
        sb(f"vtm{tt}", 128, D, BF16)
    for f in range(16):
        sb(f"xbf{f}", 128, TBv, BF16)
    for par in range(3):
        sb(f"wqk{par}", 128, 1024, BF16)
    for par in range(2):
        sb(f"wv{par}", 128, 512, BF16)
    alias("wqk3", sb_handles["wv0"][0], 128, 1024, BF16)
    alias("wqkp0", sb_handles["wqk0"][0], 128, 2048, BF16)
    alias("wqkp1", sb_handles["wqk2"][0], 128, 2048, BF16)
    for name, parts, cols, dt in [
        ("id_b", 128, 128, BF16), ("mt_b", 128, 128, BF16),
        ("sel_b", 16, 2048, BF16), ("mselr", 128, 64, F32),
        ("maskb", 128, 8, F32),
        ("cumhi", 16, TBv, BF16), ("culo", 16, TBv, BF16),
        ("expcum_b", 16, TBv, BF16),
    ]:
        sb(name, parts, cols, dt)
    for f in range(16):
        sb(f"wf{f}", 128, 16, BF16)
    for h in range(H):
        sb(f"S{h}", 128, 128, F32)
        sb(f"Sbf{h}", 128, 128, BF16)
    for name, parts, cols, dt in [
        ("pcol", 128, (nchunk + 1) * 16, F32), ("ltot", 128, 16, F32),
        ("cum16", 16, TBv, F32), ("e1", 16, TBv, F32),
    ]:
        sb(name, parts, cols, dt)
    for par in range(2):
        sb(f"negcumT{par}", 128, 16, F32)
        sb(f"emcT{par}", 128, 16, F32)
        sb(f"exptot{par}", 128, 16, F32)
        sb(f"totrow{par}", 1, 16, F32)
        for ln in range(2):
            sb(f"PT{par}{ln}", 128, 128, BF16)
            sb(f"expD{par}{ln}", 128, 128, BF16)
    attn_end = _off[0]
    prep0 = _off[0]
    for par in range(2):
        sb(f"pu8_{par}", 128, PW, U8)
        sb(f"psc_{par}", 128, PW, F32)
        sb(f"pwo_{par}", 128, PW, BF16)
    sb("pnib", 128, PW, U8)
    sb("pnf", 128, PW, F32)
    sb("pge", 128, PW, F32)
    sb("pm", 128, PW, F32)
    sb("pr4", 128, PW, F32)
    prep_end = _off[0]
    sb("wqkp2", 128, 2048, BF16)
    alias("wqk4", sb_handles["wqkp2"][0], 128, 1024, BF16)
    alias("wqk5", sb_handles["wqkp2"][0] + 2048, 128, 1024, BF16)

    qd0 = sb_handles["qd0"][0]
    xbf0 = sb_handles["xbf0"][0]
    for par in range(2):
        for ln in range(2):
            alias(f"qd3{par}{ln}", sb_handles[f"PT{par}{ln}"][0], 128, 128, BF16)
        alias(f"qd3{par}", sb_handles[f"PT{par}0"][0], 128, 128, BF16)
    po = prep0

    def pov(name, parts, cols, dt):
        nonlocal po
        a = max(4, DTSZ[dt])
        po = (po + a - 1) // a * a
        alias(name, po, parts, cols, dt)
        po += cols * DTSZ[dt]
    for h in range(H):
        pov(f"Sin{h}", 128, 128, BF16)
    pov("acc16", 128, 16, F32)
    pov("wcol", 128, 16, F32)
    pov("zst0", 128, 128, BF16)
    pov("tstall", 128, 128, F32)
    pov("tst0", 128, 16, F32)
    pov("tst1", 128, 16, F32)
    _off[0] = max(_off[0], po)
    for f in range(16):
        alias(f"x1_{f}", qd0 + f * tcore * 2, 128, tcore, BF16)
    fo = ofm_first

    def fal(name, parts, cols, dt):
        nonlocal fo
        a = max(4, DTSZ[dt])
        fo = (fo + a - 1) // a * a
        alias(name, fo, parts, cols, dt)
        fo += cols * DTSZ[dt]
    for c in range(NIT):
        fal(f"hh{c}", 128, TH, BF16)
    for f in range(16):
        fal(f"ysb{f}", 128, 512, BF16)
    for par in range(2):
        fal(f"ostg{par}", 128, 512, F32)
    for k in range(2):
        fal(f"gwb{k}", 128, 2048, BF16)
    for k in range(2):
        fal(f"dnb{k}", 128, 11 * 128, BF16)
    fal("gtmp", 128, 512, BF16)
    fal("utmp", 128, 512, BF16)
    assert fo <= prep_end, (fo, prep_end)
    alias("zrank", xbf0, 128, 2048, F32)
    alias("xall", xbf0, 128, 16 * TBv, BF16)
    assert 2 * tcore * 2 <= 16 * TBv * 2
    for par in range(2):
        alias(f"x2_{par}", xbf0 + par * tcore * 2, 128, tcore, BF16)
    assert _off[0] <= SBUF_BYTES, f"total {_off[0]} > {SBUF_BYTES}"
    build_program.last_total = _off[0]

    bufs = {}

    def BUF(name):
        if name not in bufs:
            bufs[name] = Buf()
        return bufs[name]

    pbb = [Buf() for _ in range(8)]

    def pb(i, c0, c1, p0=0, p1=128):
        return view_ctx["pb"][i].ap()[p0:p1, c0:c1]

    def pb16(i, c0, ncols, p0=0, p1=128):
        return view_ctx["pb"][i].ap().bitcast(BF16)[p0:p1, 2 * c0:2 * c0 + ncols]

    def dma(queue, dst_fn, src_fn, wbufs=(), rbufs=(), extra=()):
        qn = "dmap" if queue == "pool" else "dmas"
        lane = P.nq[qn] % NLANE
        P.nq[qn] += 1
        semn = f"{qn}{lane}"
        waits = list(extra)
        for b in wbufs:
            waits += b.for_write()
        for b in rbufs:
            waits += b.for_read()
        if variant == 10 and P.cnt[semn] > 0:
            waits.append((semn, P.cnt[semn]))
        tok = P.op(queue, lambda e: e.dma_start(out=dst_fn(), in_=src_fn()),
                   waits=waits, sig=semn)
        for b in wbufs:
            b.wrote(tok)
        for b in rbufs:
            b.read(tok)
        return tok

    def eng(stream, make, wbufs=(), rbufs=(), extra=()):
        waits = list(extra)
        for b in wbufs:
            waits += b.for_write()
        for b in rbufs:
            waits += b.for_read()
        if variant == 9:
            for s2 in ("te", "dve", "act"):
                if P.cnt[s2] > 0:
                    waits.append((s2, P.cnt[s2]))
        tok = P.op(stream, make, waits=waits,
                   sig={"te": "te", "dve": "dve", "act": "act"}[stream])
        for b in wbufs:
            b.wrote(tok)
        for b in rbufs:
            b.read(tok)
        return tok

    def emit_setup():
        for name, dt_ in [("ones_f", d_onef), ("ones_b", d_oneb),
                          ("id_b", d_idb), ("mt_b", d_mt), ("sel_b", d_sel),
                          ("nwcol", d_nw), ("misc", d_misc), ("mselr", d_msel),
                          ("maskb", d_mask)]:
            dma("pool", lambda n=name: V(n), lambda t=dt_: t.ap(), wbufs=[BUF(name)])
        dma("pool", lambda: V("id_f"), lambda: d_idf.ap()[0:16, 0:16],
            wbufs=[BUF("id_f")])
        for f in range(16):
            dma("pool", lambda f=f: V(f"wf{f}"), lambda f=f: d_wf.ap()[f],
                wbufs=[BUF(f"wf{f}")])
        for h in range(H):
            eng("dve", lambda e, h=h: e.memset(V(f"S{h}"), 0.0), wbufs=[BUF(f"S{h}")])
            eng("dve", lambda e, h=h: e.memset(V(f"Sbf{h}"), 0.0), wbufs=[BUF(f"Sbf{h}")])
        eng("dve", lambda e: e.memset(V("pcol")[:, 0:16], 1.0), wbufs=[BUF("pcol")])
        eng("dve", lambda e: e.memset(V("ltot"), 0.0), wbufs=[BUF("ltot")])

    def prep_steps():
        jobs = []
        for q in range(16):
            for (c0, W) in ((0, 384), (384, 384), (768, 384), (1152, 256)):
                jobs.append(("gu", q, c0, W))
        for it in range(DN_IT):
            for (c0, W) in ((0, 384), (384, 384), (768, 256)):
                jobs.append(("dn", it, c0, W))
        par = 0
        for (kind, idx, c0, W) in jobs:
            u8b, scb, wob = BUF(f"pu8_{par}"), BUF(f"psc_{par}"), BUF(f"pwo_{par}")
            src_u = d_guu if kind == "gu" else d_dnu
            src_s = d_gus if kind == "gu" else d_dns
            dma("sp", lambda p=par, W=W: V(f"pu8_{p}")[:, 0:W],
                lambda s=src_u, i=idx, c0=c0, W=W: s.ap()[i, :, c0:c0 + W],
                wbufs=[u8b])
            dma("pool", lambda p=par, W=W: V(f"psc_{p}")[:, 0:W],
                lambda s=src_s, i=idx, c0=c0, W=W: s.ap()[i, :, c0:c0 + W],
                wbufs=[scb])
            yield
            nib, nf, ge, m, r4 = (BUF("pnib"), BUF("pnf"), BUF("pge"),
                                  BUF("pm"), BUF("pr4"))
            eng("dve", lambda e, W=W, p=par: e.tensor_single_scalar(
                V("pnib")[0:64, 0:W], V(f"pu8_{p}")[0:64, 0:W], 15,
                ALU.bitwise_and), wbufs=[nib], rbufs=[u8b])
            eng("dve", lambda e, W=W, p=par: e.tensor_single_scalar(
                V("pnib")[64:128, 0:W], V(f"pu8_{p}")[64:128, 0:W], 4,
                ALU.logical_shift_right), wbufs=[nib], rbufs=[u8b])
            eng("dve", lambda e, W=W: e.tensor_copy(
                V("pnf")[:, 0:W], V("pnib")[:, 0:W]),
                wbufs=[nf], rbufs=[nib])
            eng("dve", lambda e, W=W: e.tensor_single_scalar(
                V("pge")[:, 0:W], V("pnf")[:, 0:W], 8.0, ALU.is_ge),
                wbufs=[ge], rbufs=[nf])
            eng("dve", lambda e, W=W: e.scalar_tensor_tensor(
                V("pm")[:, 0:W], V("pge")[:, 0:W], -8.0, V("pnf")[:, 0:W],
                ALU.mult, ALU.add), wbufs=[m], rbufs=[ge, nf])
            eng("dve", lambda e, W=W: e.tensor_scalar(
                V("pge")[:, 0:W], V("pge")[:, 0:W], -2.0, 1.0, ALU.mult, ALU.add),
                wbufs=[ge])
            yield
            eng("act", lambda e, W=W: e.activation(
                V("pr4")[:, 0:W], V("pm")[:, 0:W], AF.Relu, bias=V("misc")[:, 0:1]),
                wbufs=[r4], rbufs=[m, BUF("misc")])
            eng("act", lambda e, W=W: e.activation(
                V("pnf")[:, 0:W], V("pm")[:, 0:W], AF.Relu, bias=V("misc")[:, 1:2]),
                wbufs=[nf], rbufs=[m, BUF("misc")])
            eng("dve", lambda e, W=W: e.tensor_add(
                V("pr4")[:, 0:W], V("pm")[:, 0:W], V("pr4")[:, 0:W]),
                wbufs=[r4], rbufs=[m])
            eng("dve", lambda e, W=W: e.scalar_tensor_tensor(
                V("pr4")[:, 0:W], V("pr4")[:, 0:W], 0.5, V("pnf")[:, 0:W],
                ALU.mult, ALU.add), wbufs=[r4], rbufs=[nf])
            eng("dve", lambda e, W=W, p=par: e.tensor_mul(
                V("pge")[:, 0:W], V("pge")[:, 0:W], V(f"psc_{p}")[:, 0:W]),
                wbufs=[ge], rbufs=[scb])
            eng("dve", lambda e, W=W, p=par: e.tensor_mul(
                V(f"pwo_{p}")[:, 0:W], V("pr4")[:, 0:W], V("pge")[:, 0:W]),
                wbufs=[wob], rbufs=[r4, ge])
            PLANE = 264 * 128
            nt = W // 128
            if kind == "gu":
                eoff = ((c0 // 128) * 16 + idx) * 128
                tstride = 16 * 128
            else:
                eoff = (176 + (c0 // 128) * 11 + idx) * 128
                tstride = 11 * 128
            dma("sp", lambda eo=eoff, tstr=tstride, nt=nt: bass.AP(
                    d_wdq_in, eo, [[PLANE, 128], [tstr, nt], [1, 128]]),
                lambda p=par, W=W: V(f"pwo_{p}")[:, 0:W],
                rbufs=[wob])
            par ^= 1
            yield

    def chunk_prep(cg):
        par = cg % 2
        lc = cg % cpb
        cb = lc * CH
        last = cb + CH - 1
        cumb = BUF("cum16")
        bank = 2 + 4 * par
        eng("te", lambda e: e.transpose(
            pb(bank, 256, 272), V("cum16")[:, cb:cb + CH], V("id_f")),
            wbufs=[pbb[bank]], rbufs=[cumb, BUF("id_f")])
        eng("act", lambda e: e.activation(
            V(f"negcumT{par}"), pb(bank, 256, 272), AF.Copy, scale=-1.0),
            wbufs=[BUF(f"negcumT{par}")], rbufs=[pbb[bank]])
        eng("act", lambda e: e.activation(
            V("e1")[:, 0:CH], V("cum16")[:, cb:cb + CH], AF.Exp, scale=-1.0,
            bias=V("cum16")[:, last:last + 1]),
            wbufs=[BUF("e1")], rbufs=[cumb])
        eng("te", lambda e: e.transpose(
            pb(bank, 272, 288), V("e1")[:, 0:CH], V("id_f")),
            wbufs=[pbb[bank]], rbufs=[BUF("e1"), BUF("id_f")])
        eng("act", lambda e: e.activation(
            V(f"emcT{par}"), pb(bank, 272, 288), AF.Copy),
            wbufs=[BUF(f"emcT{par}")], rbufs=[pbb[bank]])
        eng("te", lambda e: e.transpose(
            pb(bank, 304, 320, 0, 1), V("cum16")[:, last:last + 1], V("id_f")),
            wbufs=[pbb[bank]], rbufs=[cumb, BUF("id_f")])
        eng("act", lambda e: e.activation(
            V(f"totrow{par}"), pb(bank, 304, 320, 0, 1), AF.Copy),
            wbufs=[BUF(f"totrow{par}")], rbufs=[pbb[bank]])
        eng("te", lambda e: e.matmul(
            pb(bank, 288, 304), V("ones_f")[0:1, 0:128], V(f"totrow{par}"),
            start=True, stop=True),
            wbufs=[pbb[bank]], rbufs=[BUF(f"totrow{par}"), BUF("ones_f")])
        eng("dve", lambda e: e.tensor_add(V("ltot"), V("ltot"), pb(bank, 288, 304)),
            wbufs=[BUF("ltot")], rbufs=[pbb[bank]])
        eng("act", lambda e: e.activation(
            V(f"exptot{par}"), pb(bank, 288, 304), AF.Exp),
            wbufs=[BUF(f"exptot{par}")], rbufs=[pbb[bank]])
        eng("dve", lambda e, c=cg: e.tensor_mul(
            V("pcol")[:, (c + 1) * 16:(c + 2) * 16],
            V("pcol")[:, c * 16:(c + 1) * 16], V(f"exptot{par}")),
            wbufs=[BUF("pcol")], rbufs=[BUF(f"exptot{par}")])

    def inner2(cg, g):
        lc = cg % cpb
        cb = lc * CH
        par = g % 2
        cpar = cg % 2
        X = 4 * par
        hs = (2 * g, 2 * g + 1)
        g0 = cg * CH
        vb = BUF(f"vtm{lc}")
        def sc_q(i):
            return pb(X, 128 * i, 128 * i + 128)

        def kt_q(i):
            return pb16(X, 256 + 64 * i, 128)

        def d_q(i):
            return pb(X + 1, 128 * i, 128 * i + 128)

        def s_q(i):
            return pb(X + 1, 256 + 128 * i, 384 + 128 * i)

        def b_q(i):
            return pb(X + 2, 128 * i, 128 * i + 128)

        def o_q(i):
            return pb(X + 3, 128 * i, 128 * i + 128)
        for i, h in enumerate(hs):
            eng("te", lambda e, i=i, h=h: e.matmul(
                sc_q(i), V(f"kT{h}")[:, cb:cb + CH], V(f"qd{h}")[:, g0:g0 + CH],
                start=True, stop=True),
                wbufs=[pbb[X]], rbufs=[BUF(f"kT{h}"), BUF(f"qd{h}")])
        for i, h in enumerate(hs):
            eng("te", lambda e, i=i, h=h: e.matmul(
                d_q(i), V("sel_b")[:, 128 * h:128 * h + 128],
                V("cumhi")[:, cb:cb + CH], start=True, stop=False),
                wbufs=[pbb[X + 1]], rbufs=[BUF("sel_b"), BUF("cumhi")])
            eng("te", lambda e, i=i, h=h: e.matmul(
                d_q(i), V("sel_b")[:, 128 * h:128 * h + 128],
                V("culo")[:, cb:cb + CH], start=False, stop=False),
                wbufs=[pbb[X + 1]], rbufs=[BUF("sel_b"), BUF("culo")])
            eng("te", lambda e, i=i: e.matmul(
                d_q(i), V("mt_b"), V("id_b"), start=False, stop=True),
                wbufs=[pbb[X + 1]], rbufs=[BUF("mt_b"), BUF("id_b")])
        for i, h in enumerate(hs):
            eng("te", lambda e, i=i, h=h: e.matmul(
                b_q(i), V("sel_b")[:, 128 * h:128 * h + 128],
                V("expcum_b")[:, cb:cb + CH], start=True, stop=True),
                wbufs=[pbb[X + 2]], rbufs=[BUF("sel_b"), BUF("expcum_b")])
        for i, h in enumerate(hs):
            eng("act", lambda e, i=i, h=h: e.activation(
                V(f"expD{par}{i}"), d_q(i), AF.Exp,
                bias=V(f"negcumT{cpar}")[:, h:h + 1]),
                wbufs=[BUF(f"expD{par}{i}")],
                rbufs=[pbb[X + 1], BUF(f"negcumT{cpar}")])
        for i, h in enumerate(hs):
            eng("dve", lambda e, i=i: e.tensor_mul(
                V(f"PT{par}{i}"), V(f"expD{par}{i}"), sc_q(i)),
                wbufs=[BUF(f"PT{par}{i}")],
                rbufs=[BUF(f"expD{par}{i}"), pbb[X]])
            eng("dve", lambda e, i=i, h=h: e.tensor_mul(
                V(f"qd{h}")[:, g0:g0 + CH], V(f"qd{h}")[:, g0:g0 + CH], b_q(i)),
                wbufs=[BUF(f"qd{h}")], rbufs=[pbb[X + 2]])
        for i, h in enumerate(hs):
            eng("te", lambda e, i=i, h=h: e.matmul(
                o_q(i), V(f"vtm{lc}")[:, 128 * h:128 * h + 128], V(f"PT{par}{i}"),
                start=True, stop=False),
                wbufs=[pbb[X + 3]], rbufs=[vb, BUF(f"PT{par}{i}")])
            eng("te", lambda e, i=i, h=h: e.matmul(
                o_q(i), V(f"Sbf{h}"), V(f"qd{h}")[:, g0:g0 + CH],
                start=False, stop=True),
                wbufs=[pbb[X + 3]], rbufs=[BUF(f"Sbf{h}"), BUF(f"qd{h}")])
        for i, h in enumerate(hs):
            eng("te", lambda e, i=i, h=h: e.transpose(
                kt_q(i), V(f"kT{h}")[:, cb:cb + CH], V("id_b")),
                wbufs=[pbb[X]], rbufs=[BUF(f"kT{h}"), BUF("id_b")])
        for i, h in enumerate(hs):
            eng("act", lambda e, i=i, h=h: e.activation(
                V(f"qd3{par}{i}"), kt_q(i), AF.Copy,
                scale=V(f"emcT{cpar}")[:, h:h + 1]),
                wbufs=[BUF(f"qd3{par}{i}")],
                rbufs=[pbb[X], BUF(f"emcT{cpar}")])
        for i, h in enumerate(hs):
            eng("te", lambda e, i=i, h=h: e.matmul(
                s_q(i), V(f"qd3{par}{i}"), V(f"vtm{lc}")[:, 128 * h:128 * h + 128],
                start=True, stop=True),
                wbufs=[pbb[X + 1]], rbufs=[BUF(f"qd3{par}{i}"), vb])
        for i, h in enumerate(hs):
            eng("act", lambda e, i=i, h=h: e.activation(
                V(f"ofm{h}")[:, g0:g0 + CH], o_q(i), AF.Copy),
                wbufs=[BUF(f"ofm{h}_s{cg // 4}")], rbufs=[pbb[X + 3]])
        for i, h in enumerate(hs):
            eng("dve", lambda e, i=i, h=h: e.scalar_tensor_tensor(
                V(f"S{h}"), V(f"S{h}"), V(f"exptot{cpar}")[:, h:h + 1], s_q(i),
                ALU.mult, ALU.add),
                wbufs=[BUF(f"S{h}")], rbufs=[pbb[X + 1], BUF(f"exptot{cpar}")])
            eng("dve", lambda e, h=h: e.tensor_copy(V(f"Sbf{h}"), V(f"S{h}")),
                wbufs=[BUF(f"Sbf{h}")], rbufs=[BUF(f"S{h}")])

    def pass1_block(b, prep_pump):
        t0 = b * TBv
        xb = [BUF(f"xbf{f}") for f in range(16)]
        dma("pool", lambda: V("xall"),
            lambda t0=t0: bass.AP(d_x, t0, [[tcore, 128], [128 * tcore, 16],
                                            [1, TBv]]),
            wbufs=xb)
        for wi, (wd, kind) in enumerate(((d_wq, "q"), (d_wk, "k"))):
            for o in range(16):
                gidx = wi * 16 + o
                pr = gidx % 3
                b0, b1 = 2 * pr, 2 * pr + 1
                wb0, wb1 = BUF(f"wqk{b0}"), BUF(f"wqk{b1}")
                dma("pool", lambda p=pr: V(f"wqkp{p}"),
                    lambda w=wd, o=o: w.ap()[o, :, 0:2048],
                    wbufs=[wb0, wb1])
                bank = 0 if o % 2 == 0 else 4
                for f in range(16):
                    p_ = b0 if f < 8 else b1
                    wb_ = wb0 if f < 8 else wb1
                    eng("te", lambda e, bk=bank, p=p_, f=f: e.matmul(
                        pb(bk, 0, TBv), V(f"wqk{p}")[:, 128 * (f % 8):128 * (f % 8) + 128],
                        V(f"xbf{f}"), start=(f == 0), stop=(f == 15)),
                        wbufs=[pbb[bank]], rbufs=[wb_, xb[f]])
                af_pr = AF.Copy if variant == 6 else AF.Silu
                if kind == "q":
                    eng("act", lambda e, bk=bank, o=o, t0=t0, af=af_pr: e.activation(
                        V(f"qd{o}")[:, t0:t0 + TBv], pb(bk, 0, TBv), af),
                        wbufs=[BUF(f"qd{o}")], rbufs=[pbb[bank]])
                else:
                    eng("act", lambda e, bk=bank, o=o, af=af_pr: e.activation(
                        V(f"kT{o}"), pb(bk, 0, TBv), af),
                        wbufs=[BUF(f"kT{o}")], rbufs=[pbb[bank]])
                prep_pump()
        for oq in range(4):
            base = 0 if oq % 2 == 0 else 4
            for j in range(4):
                pr = (oq * 4 + j) % 3
                wvb0, wvb1 = BUF(f"wqk{2 * pr}"), BUF(f"wqk{2 * pr + 1}")
                eidx = oq * 16 + 4 * j
                dma("pool", lambda p=pr: V(f"wqkp{p}"),
                    lambda k=eidx: bass.AP(d_wv, k * 128 * 512,
                                           [[512, 128], [128 * 512, 4], [1, 512]]),
                    wbufs=[wvb0, wvb1])
                for fi in range(4):
                    f = 4 * j + fi
                    ph = 2 * pr + fi // 2
                    wbh = wvb0 if fi < 2 else wvb1
                    for tt in range(cpb):
                        bank = base + tt
                        eng("te", lambda e, bk=bank, p=ph, f=f, fi=fi, tt=tt: e.matmul(
                            pb(bk, 0, 512), V(f"xbf{f}")[:, 128 * tt:128 * tt + 128],
                            V(f"wqk{p}")[:, 512 * (fi % 2):512 * (fi % 2) + 512],
                            start=(f == 0), stop=(f == 15)),
                            wbufs=[pbb[bank]], rbufs=[wbh, xb[f]])
            for tt in range(cpb):
                bank = base + tt
                eng("dve", lambda e, bk=bank, tt=tt, oq=oq: e.tensor_copy(
                    V(f"vtm{tt}")[:, 512 * oq:512 * oq + 512], pb(bk, 0, 512)),
                    wbufs=[BUF(f"vtm{tt}")], rbufs=[pbb[bank]])
            prep_pump()
        for f in range(16):
            eng("te", lambda e, f=f: e.matmul(
                pb(1, 0, TBv, 0, 16), V(f"wf{f}"), V(f"xbf{f}"),
                start=(f == 0), stop=(f == 15)),
                wbufs=[pbb[1]], rbufs=[BUF(f"wf{f}"), xb[f]])
        eng("act", lambda e: e.activation(
            V("e1"), pb(1, 0, TBv, 0, 16), AF.Exp, scale=-1.0),
            wbufs=[BUF("e1")], rbufs=[pbb[1]])
        eng("dve", lambda e: e.tensor_single_scalar(V("e1"), V("e1"), 1.0, ALU.add),
            wbufs=[BUF("e1")])
        eng("act", lambda e: e.activation(V("e1"), V("e1"), AF.Ln),
            wbufs=[BUF("e1")])
        for lc in range(cpb):
            eng("dve", lambda e, lc=lc: e.tensor_tensor_scan(
                V("cum16")[:, lc * CH:lc * CH + CH], V("ones_f")[0:16, 0:CH],
                V("e1")[:, lc * CH:lc * CH + CH], 0.0, ALU.mult, ALU.subtract),
                wbufs=[BUF("cum16")], rbufs=[BUF("e1"), BUF("ones_f")])
        eng("dve", lambda e: e.tensor_copy(V("cumhi"), V("cum16")),
            wbufs=[BUF("cumhi")], rbufs=[BUF("cum16")])
        eng("dve", lambda e: e.tensor_sub(V("culo"), V("cum16"), V("cumhi")),
            wbufs=[BUF("culo")], rbufs=[BUF("cum16"), BUF("cumhi")])
        eng("act", lambda e: e.activation(V("expcum_b"), V("cum16"), AF.Exp),
            wbufs=[BUF("expcum_b")], rbufs=[BUF("cum16")])
        prep_pump()
        if dbg and b == nblk - 1:
            dma("sp", lambda: d_dbg_qp.ap(),
                lambda t0=t0: V("qd0")[:, t0:t0 + TBv], rbufs=[BUF("qd0")])
            dma("sp", lambda: d_dbg_kp.ap(), lambda: V("kT0"), rbufs=[BUF("kT0")])
        for lc in range(cpb):
            cg = b * cpb + lc
            chunk_prep(cg)
            for g in range(H // 2):
                inner2(cg, g)
            prep_pump()

    def emit_wdq_ag(prep_toks):
        return P.op("pool", lambda e: e.collective_compute(
            "AllGather", ALU.bypass, replica_groups=[list(range(NCORES))],
            ins=[d_wdq_in.ap().opt()], outs=[d_wdq_all.ap().opt()]),
            waits=prep_toks, sig="cc")

    def emit_z_ag():
        toks = []
        for h in range(H):
            toks.append(dma("sp", lambda h=h: d_z_in.ap()[128 * h:128 * h + 128, :],
                            lambda h=h: V(f"S{h}"), rbufs=[BUF(f"S{h}")]))
        toks.append(dma("sp", lambda: d_z_in.ap()[H * 128:H * 128 + 128, 0:16],
                        lambda: V("ltot"), rbufs=[BUF("ltot")]))
        toks.append(dma("sp", lambda: d_z_in.ap()[H * 128:H * 128 + 128, 16:128],
                        lambda: V("ones_f")[:, 0:112], rbufs=[BUF("ones_f")]))
        return P.op("pool", lambda e: e.collective_compute(
            "AllGather", ALU.bypass, replica_groups=[list(range(NCORES))],
            ins=[d_z_in.ap().opt()], outs=[d_z_all.ap().opt()]),
            waits=toks, sig="cc")

    def emit_combine(zag_tok):
        P.wait("sp", zag_tok)
        P.wait("pool", zag_tok)
        zrb = BUF("zrank")
        for f in range(16):
            zrb.r += BUF(f"xbf{f}").for_write()
        tb = BUF("tstall")
        for r in range(NCORES):
            dma("sp", lambda r=r: V("tstall")[:, 16 * r:16 * r + 16],
                lambda r=r: d_z_all.ap()[
                    r * ZROWS + H * 128:r * ZROWS + H * 128 + 128, 0:16],
                wbufs=[tb])
        for p in range(NCORES):
            accb = BUF("acc16")
            eng("dve", lambda e, p=p: e.tensor_scalar(
                V("acc16"), V("ones_f")[:, 0:16], V("maskb")[:, p:p + 1], None,
                ALU.mult), wbufs=[accb], rbufs=[BUF("maskb"), BUF("ones_f")])
            for r in range(NCORES):
                eng("dve", lambda e, p=p, r=r: e.scalar_tensor_tensor(
                    V("acc16"), V("tstall")[:, 16 * r:16 * r + 16],
                    V("mselr")[:, 8 * p + r:8 * p + r + 1],
                    V("acc16"), ALU.mult, ALU.add),
                    wbufs=[accb], rbufs=[tb, BUF("mselr")])
            eng("act", lambda e, p=p: e.activation(
                V("wcol"), V("acc16"), AF.Exp),
                wbufs=[BUF("wcol")], rbufs=[accb])
            dma("sp", lambda: V("zrank"),
                lambda p=p: bass.AP(d_z_all, (p * ZROWS) * 128,
                                    [[128, 128], [128 * 128, 16], [1, 128]]),
                wbufs=[zrb])
            for h in range(H):
                if p == 0:
                    eng("dve", lambda e, h=h: e.tensor_scalar(
                        V(f"S{h}"), V("zrank")[:, 128 * h:128 * h + 128],
                        V("wcol")[:, h:h + 1], None, ALU.mult),
                        wbufs=[BUF(f"S{h}")], rbufs=[zrb, BUF("wcol")])
                else:
                    eng("dve", lambda e, h=h: e.scalar_tensor_tensor(
                        V(f"S{h}"), V("zrank")[:, 128 * h:128 * h + 128],
                        V("wcol")[:, h:h + 1], V(f"S{h}"),
                        ALU.mult, ALU.add),
                        wbufs=[BUF(f"S{h}")], rbufs=[zrb, BUF("wcol")])
        for h in range(H):
            eng("dve", lambda e, h=h: e.tensor_copy(V(f"Sin{h}"), V(f"S{h}")),
                wbufs=[BUF(f"Sin{h}")], rbufs=[BUF(f"S{h}")])

    def emit_pass2(dbg_tap=None):
        for f in range(16):
            bufs[f"x1_{f}"] = BUF(f"qd{f}")
        for par in range(2):
            bb = BUF(f"x2_{par}")
            bb.r += BUF("zrank").for_write()
        for nm, src in (("gtmp", "wv0"), ("utmp", "wv1")):
            BUF(nm).r += BUF(src).for_write()
        def corr_chunk(cg):
            for h in range(H):
                par = (cg * H + h) % 2
                bank = 3 + 4 * par
                g0 = cg * CH
                eng("dve", lambda e, p=par, h=h, cg=cg, g0=g0: e.tensor_scalar(
                    V(f"qd3{p}"), V(f"qd{h}")[:, g0:g0 + CH],
                    V("pcol")[:, cg * 16 + h:cg * 16 + h + 1], None, ALU.mult),
                    wbufs=[BUF(f"qd3{par}")], rbufs=[BUF(f"qd{h}"), BUF("pcol")])
                eng("te", lambda e, p=par, h=h, bk=bank: e.matmul(
                    pb(bk, 0, 128), V(f"Sin{h}"), V(f"qd3{p}"), start=True, stop=True),
                    wbufs=[pbb[bank]], rbufs=[BUF(f"Sin{h}"), BUF(f"qd3{par}")])
                eng("dve", lambda e, h=h, g0=g0, bk=bank: e.tensor_add(
                    V(f"ofm{h}")[:, g0:g0 + CH], V(f"ofm{h}")[:, g0:g0 + CH],
                    pb(bk, 0, 128)),
                    wbufs=[BUF(f"ofm{h}_s{cg // 4}")], rbufs=[pbb[bank]])
        for ts in range(nts_all):
            for cg in range(4 * ts, 4 * ts + 4):
                corr_chunk(cg)
            if dbg_tap is not None and ts == nts_all - 1:
                for h in range(2):
                    dbg_tap(lambda h=h: d_dbg_of2.ap()[128 * h:128 * h + 128, :],
                            f"ofm{h}")
            c0 = 512 * ts
            for h in range(H):
                sqb = ("gtmp", "utmp")[h % 2]
                eng("dve", lambda e, h=h, c0=c0, sq=sqb: e.tensor_mul(
                    V(sq), V(f"ofm{h}")[:, c0:c0 + 512], V(f"ofm{h}")[:, c0:c0 + 512]),
                    wbufs=[BUF(sqb)], rbufs=[BUF(f"ofm{h}_s{ts}")])
                eng("te", lambda e, h=h, sq=sqb: e.matmul(
                    pb(1, 0, 512, 0, 1), V("ones_b")[:, 0:1], V(sq),
                    start=(h == 0), stop=(h == 15)),
                    wbufs=[pbb[1]], rbufs=[BUF(sqb), BUF("ones_b")])
            eng("act", lambda e: e.activation(
                V("row0"), pb(1, 0, 512, 0, 1), AF.Ln, scale=1.0 / D,
                bias=V("misc")[0:1, 2:3]),
                wbufs=[BUF("row0")], rbufs=[pbb[1], BUF("misc")])
            eng("act", lambda e: e.activation(V("row0"), V("row0"), AF.Exp, scale=-0.5),
                wbufs=[BUF("row0")])
            eng("te", lambda e: e.matmul(
                pb(2, 0, 512), V("ones_f")[0:1, 0:128], V("row0"),
                start=True, stop=True),
                wbufs=[pbb[2]], rbufs=[BUF("row0"), BUF("ones_f")])
            for h in range(H):
                eng("dve", lambda e, h=h, c0=c0: e.tensor_mul(
                    V(f"ofm{h}")[:, c0:c0 + 512], V(f"ofm{h}")[:, c0:c0 + 512],
                    pb(2, 0, 512)),
                    wbufs=[BUF(f"ofm{h}_s{ts}")], rbufs=[pbb[2]])
                eng("dve", lambda e, h=h, c0=c0: e.tensor_scalar(
                    V(f"ofm{h}")[:, c0:c0 + 512], V(f"ofm{h}")[:, c0:c0 + 512],
                    V("nwcol")[:, h:h + 1], None, ALU.mult),
                    wbufs=[BUF(f"ofm{h}_s{ts}")], rbufs=[BUF("nwcol")])
        for fp in range(16):
            pr = fp % 3
            b0, b1 = 2 * pr, 2 * pr + 1
            wb0, wb1 = BUF(f"wqk{b0}"), BUF(f"wqk{b1}")
            dma("pool", lambda p=pr: V(f"wqkp{p}"),
                lambda fp=fp: d_wo.ap()[fp, :, 0:2048], wbufs=[wb0, wb1])
            base = 0 if fp % 2 == 0 else 4
            for fv in range(16):
                p_ = b0 if fv < 8 else b1
                wb_ = wb0 if fv < 8 else wb1
                for ts in range(nts_all):
                    bank = base + ts
                    eng("te", lambda e, bk=bank, p=p_, fv=fv, ts=ts: e.matmul(
                        pb(bk, 0, 512),
                        V(f"wqk{p}")[:, 128 * (fv % 8):128 * (fv % 8) + 128],
                        V(f"ofm{fv}")[:, 512 * ts:512 * ts + 512],
                        start=(fv == 0), stop=(fv == 15)),
                        wbufs=[pbb[bank]], rbufs=[wb_, BUF(f"ofm{fv}_s{ts}")])
            x2par = fp % 2
            x2b = BUF(f"x2_{x2par}")
            dma("pool", lambda p=x2par: V(f"x2_{p}"),
                lambda fp=fp: d_x.ap()[128 * fp:128 * fp + 128, :], wbufs=[x2b])
            for ts in range(nts_all):
                bank = base + ts
                eng("dve", lambda e, bk=bank, fp=fp, ts=ts, p=x2par: e.tensor_add(
                    V(f"x1_{fp}")[:, 512 * ts:512 * ts + 512],
                    V(f"x2_{p}")[:, 512 * ts:512 * ts + 512], pb(bk, 0, 512)),
                    wbufs=[BUF(f"x1_{fp}")], rbufs=[x2b, pbb[bank]])
        for ts in range(nts_all):
            c0 = 512 * ts
            for fp in range(16):
                sqb = ("gtmp", "utmp")[fp % 2]
                eng("te", lambda e, fp=fp, c0=c0: e.matmul(
                    pb(1, 0, 512, 0, 1), V("ones_b")[:, 0:1],
                    V(f"x1_{fp}")[:, c0:c0 + 512], start=(fp == 0), stop=(fp == 15)),
                    wbufs=[pbb[1]], rbufs=[BUF(f"x1_{fp}"), BUF("ones_b")])
                eng("dve", lambda e, fp=fp, c0=c0, sq=sqb: e.tensor_mul(
                    V(sq), V(f"x1_{fp}")[:, c0:c0 + 512],
                    V(f"x1_{fp}")[:, c0:c0 + 512]),
                    wbufs=[BUF(sqb)], rbufs=[BUF(f"x1_{fp}")])
                eng("te", lambda e, fp=fp, sq=sqb: e.matmul(
                    pb(5, 0, 512, 0, 1), V("ones_b")[:, 0:1], V(sq),
                    start=(fp == 0), stop=(fp == 15)),
                    wbufs=[pbb[5]], rbufs=[BUF(sqb), BUF("ones_b")])
            eng("act", lambda e: e.activation(
                V("row1"), pb(1, 0, 512, 0, 1), AF.Copy, scale=1.0 / D),
                wbufs=[BUF("row1")], rbufs=[pbb[1]])
            eng("act", lambda e: e.activation(V("row2"), V("row1"), AF.Square),
                wbufs=[BUF("row2")], rbufs=[BUF("row1")])
            eng("dve", lambda e: e.scalar_tensor_tensor(
                V("row2"), pb(5, 0, 512, 0, 1), 1.0 / D, V("row2"),
                ALU.mult, ALU.subtract),
                wbufs=[BUF("row2")], rbufs=[pbb[5]])
            eng("act", lambda e: e.activation(
                V("row2"), V("row2"), AF.Ln, bias=V("misc")[0:1, 3:4]),
                wbufs=[BUF("row2")], rbufs=[BUF("misc")])
            eng("act", lambda e: e.activation(V("row2"), V("row2"), AF.Exp, scale=-0.5),
                wbufs=[BUF("row2")])
            eng("te", lambda e: e.matmul(
                pb(2, 0, 512), V("ones_f")[0:1, 0:128], V("row1"),
                start=True, stop=True),
                wbufs=[pbb[2]], rbufs=[BUF("row1"), BUF("ones_f")])
            eng("te", lambda e: e.matmul(
                pb(6, 0, 512), V("ones_f")[0:1, 0:128], V("row2"),
                start=True, stop=True),
                wbufs=[pbb[6]], rbufs=[BUF("row2"), BUF("ones_f")])
            for fp in range(16):
                eng("dve", lambda e, fp=fp, c0=c0: e.tensor_sub(
                    V(f"x1_{fp}")[:, c0:c0 + 512], V(f"x1_{fp}")[:, c0:c0 + 512],
                    pb(2, 0, 512)),
                    wbufs=[BUF(f"x1_{fp}")], rbufs=[pbb[2]])
                eng("dve", lambda e, fp=fp, c0=c0: e.tensor_mul(
                    V(f"x1_{fp}")[:, c0:c0 + 512], V(f"x1_{fp}")[:, c0:c0 + 512],
                    pb(6, 0, 512)),
                    wbufs=[BUF(f"x1_{fp}")], rbufs=[pbb[6]])

    def emit_ffn(wag_tok):
        P.wait("sp", wag_tok)
        P.wait("te", wag_tok)
        PLANE = 264 * 128
        for half in range(nffn):
            t0 = half * TH
            ob_seq = []
            for c in range(NIT):
                ob_seq += [c, NIT + c]
            for pos, ot in enumerate(ob_seq):
                par = pos % 2
                bank0 = 0 if par == 0 else 4
                shard, olocal = ot // GU_SH, ot % GU_SH
                gb = BUF(f"gwb{par}")
                eoff = shard * SH_N + olocal * 16 * 128
                dma("sp", lambda p=par: V(f"gwb{p}"),
                    lambda eo=eoff: bass.AP(d_wdq_all, eo, [[PLANE, 128], [1, 2048]]),
                    wbufs=[gb])
                for q in range(16):
                    for ts in range(nts_h):
                        eng("te", lambda e, bk=bank0 + ts, p=par, q=q, ts=ts,
                            t0=t0: e.matmul(
                                pb(bk, 0, 512), V(f"gwb{p}")[:, 128 * q:128 * q + 128],
                                V(f"x1_{q}")[:, t0 + 512 * ts:t0 + 512 * ts + 512],
                                start=(q == 0), stop=(q == 15)),
                            wbufs=[pbb[bank0 + ts]], rbufs=[gb, BUF(f"x1_{q}")])
                if ot < NIT:
                    for ts in range(nts_h):
                        eng("act", lambda e, bk=bank0 + ts, c=ot, ts=ts: e.activation(
                            V(f"hh{c}")[:, 512 * ts:512 * ts + 512], pb(bk, 0, 512),
                            AF.Silu),
                            wbufs=[BUF(f"hh{ot}")], rbufs=[pbb[bank0 + ts]])
                else:
                    for ts in range(nts_h):
                        eng("dve", lambda e, bk=bank0 + ts, c=ot - NIT,
                            ts=ts: e.tensor_mul(
                                V(f"hh{c}")[:, 512 * ts:512 * ts + 512],
                                V(f"hh{c}")[:, 512 * ts:512 * ts + 512],
                                pb(bk, 0, 512)),
                            wbufs=[BUF(f"hh{ot - NIT}")], rbufs=[pbb[bank0 + ts]])
            for ts in range(nts_h):
                hoff = 512 * ts
                for F in range(16):
                    par = F % 2
                    bank = (2 if par == 0 else 6) + (F // 2) % 2
                    Floc = F % 8
                    for gdn in range(4):
                        sh = 2 * gdn + F // 8
                        db = BUF(f"dnb{(F * 4 + gdn) % 2}")
                        eoff = sh * SH_N + (176 + Floc * 11) * 128
                        dma("sp", lambda gg=(F * 4 + gdn) % 2: V(f"dnb{gg}"),
                            lambda eo=eoff: bass.AP(d_wdq_all, eo,
                                                    [[PLANE, 128], [1, 11 * 128]]),
                            wbufs=[db])
                        for itl in range(DN_IT):
                            it = gdn * DN_IT + itl
                            eng("te", lambda e, bk=bank, gg=(F * 4 + gdn) % 2,
                                itl=itl, it=it, ho=hoff: e.matmul(
                                    pb(bk, 0, 512),
                                    V(f"dnb{gg}")[:, 128 * itl:128 * itl + 128],
                                    V(f"hh{it}")[:, ho:ho + 512],
                                    start=(it == 0), stop=(it == NIT - 1)),
                                wbufs=[pbb[bank]], rbufs=[db, BUF(f"hh{it}")])
                    eng("dve", lambda e, bk=bank, fp=F: e.tensor_copy(
                        V(f"ysb{fp}")[:, 0:512], pb(bk, 0, 512)),
                        wbufs=[BUF(f"ysb{F}")], rbufs=[pbb[bank]])
                c0 = 0
                bS, bB, bV, bW = (0, 4, 1, 5) if ts == 0 else (2, 6, 3, 7)
                for fp in range(16):
                    sqb = ("gtmp", "utmp")[fp % 2]
                    eng("dve", lambda e, fp=fp, c0=c0, sq=sqb: e.tensor_mul(
                        V(sq), V(f"ysb{fp}")[:, c0:c0 + 512],
                        V(f"ysb{fp}")[:, c0:c0 + 512]),
                        wbufs=[BUF(sqb)], rbufs=[BUF(f"ysb{fp}")])
                    eng("te", lambda e, fp=fp, sq=sqb, b=bS: e.matmul(
                        pb(b, 0, 512, 0, 1), V("ones_b")[:, 0:1], V(sq),
                        start=(fp == 0), stop=(fp == 15)),
                        wbufs=[pbb[bS]], rbufs=[BUF(sqb), BUF("ones_b")])
                eng("act", lambda e, b=bS: e.activation(
                    V("row0"), pb(b, 0, 512, 0, 1), AF.Ln, scale=1.0 / D,
                    bias=V("misc")[0:1, 2:3]),
                    wbufs=[BUF("row0")], rbufs=[pbb[bS], BUF("misc")])
                eng("act", lambda e: e.activation(V("row0"), V("row0"), AF.Exp,
                                                  scale=-0.5),
                    wbufs=[BUF("row0")])
                eng("te", lambda e, b=bB: e.matmul(
                    pb(b, 0, 512), V("ones_f")[0:1, 0:128], V("row0"),
                    start=True, stop=True),
                    wbufs=[pbb[bB]], rbufs=[BUF("row0"), BUF("ones_f")])
                for fp in range(16):
                    eng("dve", lambda e, fp=fp, c0=c0, b=bB: e.tensor_mul(
                        V(f"ysb{fp}")[:, c0:c0 + 512], V(f"ysb{fp}")[:, c0:c0 + 512],
                        pb(b, 0, 512)),
                        wbufs=[BUF(f"ysb{fp}")], rbufs=[pbb[bB]])
                    eng("dve", lambda e, fp=fp, c0=c0, t0=t0, ho=hoff: e.tensor_add(
                        V(f"ysb{fp}")[:, c0:c0 + 512], V(f"ysb{fp}")[:, c0:c0 + 512],
                        V(f"x1_{fp}")[:, t0 + ho + c0:t0 + ho + c0 + 512]),
                        wbufs=[BUF(f"ysb{fp}")], rbufs=[BUF(f"x1_{fp}")])
                for fp in range(16):
                    eng("te", lambda e, fp=fp, c0=c0, b=bS: e.matmul(
                        pb(b, 0, 512, 0, 1), V("ones_b")[:, 0:1],
                        V(f"ysb{fp}")[:, c0:c0 + 512], start=(fp == 0), stop=(fp == 15)),
                        wbufs=[pbb[bS]], rbufs=[BUF(f"ysb{fp}"), BUF("ones_b")])
                    eng("dve", lambda e, fp=fp, c0=c0: e.tensor_mul(
                        V("gtmp"), V(f"ysb{fp}")[:, c0:c0 + 512],
                        V(f"ysb{fp}")[:, c0:c0 + 512]),
                        wbufs=[BUF("gtmp")], rbufs=[BUF(f"ysb{fp}")])
                    eng("te", lambda e, fp=fp, b=bV: e.matmul(
                        pb(b, 0, 512, 0, 1), V("ones_b")[:, 0:1], V("gtmp"),
                        start=(fp == 0), stop=(fp == 15)),
                        wbufs=[pbb[bV]], rbufs=[BUF("gtmp"), BUF("ones_b")])
                eng("act", lambda e, b=bS: e.activation(
                    V("row1"), pb(b, 0, 512, 0, 1), AF.Copy, scale=1.0 / D),
                    wbufs=[BUF("row1")], rbufs=[pbb[bS]])
                eng("act", lambda e: e.activation(V("row2"), V("row1"), AF.Square),
                    wbufs=[BUF("row2")], rbufs=[BUF("row1")])
                eng("dve", lambda e, b=bV: e.scalar_tensor_tensor(
                    V("row2"), pb(b, 0, 512, 0, 1), 1.0 / D, V("row2"),
                    ALU.mult, ALU.subtract),
                    wbufs=[BUF("row2")], rbufs=[pbb[bV]])
                eng("act", lambda e: e.activation(
                    V("row2"), V("row2"), AF.Ln, bias=V("misc")[0:1, 3:4]),
                    wbufs=[BUF("row2")], rbufs=[BUF("misc")])
                eng("act", lambda e: e.activation(V("row2"), V("row2"), AF.Exp,
                                                  scale=-0.5),
                    wbufs=[BUF("row2")])
                eng("te", lambda e, b=bB: e.matmul(
                    pb(b, 0, 512), V("ones_f")[0:1, 0:128], V("row1"),
                    start=True, stop=True),
                    wbufs=[pbb[bB]], rbufs=[BUF("row1"), BUF("ones_f")])
                eng("te", lambda e, b=bW: e.matmul(
                    pb(b, 0, 512), V("ones_f")[0:1, 0:128], V("row2"),
                    start=True, stop=True),
                    wbufs=[pbb[bW]], rbufs=[BUF("row2"), BUF("ones_f")])
                for fp in range(16):
                    spar = fp % 2
                    ob = BUF(f"ostg{spar}")
                    eng("dve", lambda e, fp=fp, c0=c0, p=spar, b=bB: e.tensor_sub(
                        V(f"ostg{p}"), V(f"ysb{fp}")[:, c0:c0 + 512], pb(b, 0, 512)),
                        wbufs=[ob], rbufs=[BUF(f"ysb{fp}"), pbb[bB]])
                    eng("dve", lambda e, p=spar, b=bW: e.tensor_mul(
                        V(f"ostg{p}"), V(f"ostg{p}"), pb(b, 0, 512)),
                        wbufs=[ob], rbufs=[pbb[bW]])
                    dma("pool", lambda fp=fp, t0=t0, c0=c0, ho=hoff: d_out.ap()[
                            128 * fp:128 * fp + 128,
                            t0 + ho + c0:t0 + ho + c0 + 512],
                        lambda p=spar: V(f"ostg{p}"), rbufs=[ob])

    emit_setup()
    prep_gen = prep_steps()
    state = {"done": False}

    def prep_pump():
        if state["done"]:
            return
        try:
            for _ in range(1):
                next(prep_gen)
        except StopIteration:
            state["done"] = True

    def tap(dst_fn, src_name, cols=None, rows=128):
        if not dbg:
            return
        b = BUF(src_name)
        dma("sp", dst_fn,
            lambda n=src_name, c=cols, r=rows: V(n)[0:r, :] if c is None else V(n)[0:r, 0:c],
            rbufs=[b])

    for b in range(nblk):
        pass1_block(b, prep_pump)
        if dbg and b == nblk - 1:
            tap(lambda: d_dbg_cum.ap()[:, 0:TBv], "cum16", rows=16)
    while not state["done"]:
        prep_pump()
    if dbg:
        for h in range(2):
            tap(lambda h=h: d_dbg_s.ap()[128 * h:128 * h + 128, :], f"S{h}")
        tap(lambda: d_dbg_pt.ap()[0:128, :], "PT1")
        tap(lambda: d_dbg_pt.ap()[128:256, :], "expD1")
        dma("sp", lambda: d_dbg_wdq.ap(),
            lambda: bass.AP(d_wdq_in, 0, [[128, SH_N // 128], [1, 128]]),
            extra=[(f"dmas{l}", P.cnt[f"dmas{l}"]) for l in range(NLANE)
                   if P.cnt[f"dmas{l}"] > 0])
    seed = [(s, P.cnt[s]) for s in (["te", "dve", "act"]
            + [f"dmas{l}" for l in range(NLANE)]
            + [f"dmap{l}" for l in range(NLANE)]) if P.cnt[s] > 0]
    for nm in ([f"Sin{h}" for h in range(H)] +
               ["acc16", "wcol", "zst0", "tst0", "tst1"]):
        BUF(nm).r = list(seed)
    wag_tok = emit_wdq_ag(
        [(f"dmas{l}", P.cnt[f"dmas{l}"]) for l in range(NLANE)
         if P.cnt[f"dmas{l}"] > 0]
        + [(f"dmap{l}", P.cnt[f"dmap{l}"]) for l in range(NLANE)
           if P.cnt[f"dmap{l}"] > 0])
    zag_tok = emit_z_ag()
    emit_combine(zag_tok)
    if dbg:
        for h in range(2):
            tap(lambda h=h: d_dbg_sin.ap()[128 * h:128 * h + 128, :], f"Sin{h}")
            tap(lambda h=h: d_dbg_qd.ap()[128 * h:128 * h + 128, :], f"qd{h}")
            for s_ in range(tcore // 512):
                tap(lambda h=h: d_dbg_of1.ap()[128 * h:128 * h + 128, :],
                    f"ofm{h}_s{s_}")
    emit_pass2(dbg_tap=tap if dbg else None)
    if dbg:
        for h in range(2):
            tap(lambda h=h: d_dbg_ofm.ap()[128 * h:128 * h + 128, :], f"ofm{h}")
        for f in range(16):
            tap(lambda f=f: d_dbg_x1.ap()[128 * f:128 * f + 128, :], f"x1_{f}")
    P.barrier()
    emit_ffn(wag_tok)
    for l in range(NLANE):
        if P.cnt[f"dmas{l}"] > 0:
            P.wait("sp", (f"dmas{l}", P.cnt[f"dmas{l}"]))
        if P.cnt[f"dmap{l}"] > 0:
            P.wait("sp", (f"dmap{l}", P.cnt[f"dmap{l}"]))

    with (
        nc.sbuf_tensor("arena", [128, _off[0]], U8) as arena_t,
        nc.psum_tensor("pb0", [128, 512], F32) as pb0,
        nc.psum_tensor("pb1", [128, 512], F32) as pb1,
        nc.psum_tensor("pb2", [128, 512], F32) as pb2,
        nc.psum_tensor("pb3", [128, 512], F32) as pb3,
        nc.psum_tensor("pb4", [128, 512], F32) as pb4,
        nc.psum_tensor("pb5", [128, 512], F32) as pb5,
        nc.psum_tensor("pb6", [128, 512], F32) as pb6,
        nc.psum_tensor("pb7", [128, 512], F32) as pb7,
        nc.semaphore("s_te") as s_te,
        nc.semaphore("s_dve") as s_dve,
        nc.semaphore("s_act") as s_act,
        nc.semaphore("s_cc") as s_cc,
        nc.Block() as block,
    ):
        view_ctx["arena"] = arena_t
        view_ctx["pb"] = [pb0, pb1, pb2, pb3, pb4, pb5, pb6, pb7]
        sems = {"te": s_te, "dve": s_dve, "act": s_act, "cc": s_cc}
        import contextlib
        _semstack = contextlib.ExitStack()
        for _q in ("dmap", "dmas"):
            for _l in range(NLANE):
                sems[f"{_q}{_l}"] = _semstack.enter_context(
                    nc.semaphore(f"s_{_q}{_l}"))

        @block.tensor
        def _(e):
            for cl in P.ops["te"]:
                cl(e, sems)

        @block.vector
        def _(e):
            for cl in P.ops["dve"]:
                cl(e, sems)

        @block.scalar
        def _(e):
            for cl in P.ops["act"]:
                cl(e, sems)

        @block.gpsimd
        def _(e):
            for cl in P.ops["pool"]:
                cl(e, sems)

        @block.sync
        def _(e):
            for cl in P.ops["sp"]:
                cl(e, sems)

    return nc


def kernel(L0_attn_x, L0_attn_Wq, L0_attn_Wk, L0_attn_Wv, L0_attn_Wf,
           L0_attn_Wo, L0_attn_norm_w, L0_ffn_gate_up_packed,
           L0_ffn_gate_up_scales, L0_ffn_down_packed, L0_ffn_down_scales):
    inputs = dict(L0_attn_x=L0_attn_x, L0_attn_Wq=L0_attn_Wq,
                  L0_attn_Wk=L0_attn_Wk, L0_attn_Wv=L0_attn_Wv,
                  L0_attn_Wf=L0_attn_Wf, L0_attn_Wo=L0_attn_Wo,
                  L0_attn_norm_w=L0_attn_norm_w,
                  L0_ffn_gate_up_packed=L0_ffn_gate_up_packed,
                  L0_ffn_gate_up_scales=L0_ffn_gate_up_scales,
                  L0_ffn_down_packed=L0_ffn_down_packed,
                  L0_ffn_down_scales=L0_ffn_down_scales)
    x = np.asarray(L0_attn_x)
    bsz, seq, d = x.shape
    tcore = bsz * seq // NCORES
    in_maps = host_prepare(inputs, tcore)
    if _CACHE.get("tcore") != tcore:
        _CACHE["nc"] = build_program(tcore)
        _CACHE["tcore"] = tcore
    res = run_bass_kernel_spmd(_CACHE["nc"], in_maps, core_ids=list(range(NCORES)))
    out = np.empty((bsz * seq, d), np.float32)
    for r in range(NCORES):
        out[r * tcore:(r + 1) * tcore][:, PERM] = res.results[r]["xout"].T
    return out.reshape(bsz, seq, d)
